# version 20
# baseline (speedup 1.0000x reference)
import numpy as np
import ml_dtypes
import concourse.bass as bass
import concourse.mybir as mybir
from concourse.bass_utils import run_bass_kernel_spmd
from contextlib import ExitStack

F32 = mybir.dt.float32
BF16 = mybir.dt.bfloat16
FP8 = mybir.dt.float8e5
ALU = mybir.AluOpType
ACT = mybir.ActivationFunctionType

SL = 2048
D = 1024
HEADS = 16
TT = 16
DC = 8
EC = 24
KT = 16
NC1 = 32
TSN = 4
EXP_SHIFT = -12.0


def _threefry2x32(k0, k1, x0, x1):
    rot = [13, 15, 26, 6, 17, 29, 16, 24]
    ks0, ks1 = np.uint32(k0), np.uint32(k1)
    ks2 = np.uint32(np.uint32(ks0) ^ np.uint32(ks1) ^ np.uint32(0x1BD11BDA))
    x0 = (x0 + ks0).astype(np.uint32)
    x1 = (x1 + ks1).astype(np.uint32)

    def rotl(v, d):
        return ((v << np.uint32(d)) | (v >> np.uint32(32 - d))).astype(np.uint32)

    ks = [ks0, ks1, ks2]
    for i in range(5):
        r = rot[:4] if i % 2 == 0 else rot[4:]
        for j in range(4):
            x0 = (x0 + x1).astype(np.uint32)
            x1 = rotl(x1, r[j])
            x1 = (x1 ^ x0).astype(np.uint32)
        x0 = (x0 + ks[(i + 1) % 3]).astype(np.uint32)
        x1 = (x1 + ks[(i + 2) % 3] + np.uint32(i + 1)).astype(np.uint32)
    return x0, x1


def _dropout_keep_mask(shape, p_keep=0.8, key=(0, 42)):
    n = int(np.prod(shape))
    idx = np.arange(n, dtype=np.uint64)
    hi = (idx >> np.uint64(32)).astype(np.uint32)
    lo = (idx & np.uint64(0xFFFFFFFF)).astype(np.uint32)
    b0, b1 = _threefry2x32(key[0], key[1], hi, lo)
    bits = (b0 ^ b1).astype(np.uint32)
    fl = ((bits >> np.uint32(9)) | np.uint32(0x3F800000)).view(np.float32) \
        - np.float32(1.0)
    return (fl < np.float32(p_keep)).reshape(shape)


def build_nc(debug=False):
    nc = bass.Bass("TRN2", target_bir_lowering=False,
                   dynamic_dma_scratch_size=512)

    xin = nc.dram_tensor("xin", [SL, D], F32, kind="ExternalInput").ap()
    wqkv = nc.dram_tensor("wqkv", [24 * 128, 1024], F32, kind="ExternalInput").ap()
    wout = nc.dram_tensor("wout", [D, D], F32, kind="ExternalInput").ap()
    w1t = nc.dram_tensor("w1t", [32 * 128, 1024], F32, kind="ExternalInput").ap()
    w2t = nc.dram_tensor("w2t", [4 * D, D], F32, kind="ExternalInput").ap()
    brow = nc.dram_tensor("brow", [2, 3 * D], F32, kind="ExternalInput").ap()
    b1c = nc.dram_tensor("b1c", [128, 32], F32, kind="ExternalInput").ap()
    b2c = nc.dram_tensor("b2c", [128, 8], F32, kind="ExternalInput").ap()
    idbf = nc.dram_tensor("idbf", [128, 128], BF16, kind="ExternalInput").ap()
    idf32 = nc.dram_tensor("idf32", [128, 128], F32, kind="ExternalInput").ap()
    maskfm = nc.dram_tensor("maskfm", [D, SL], BF16, kind="ExternalInput").ap()
    yout = nc.dram_tensor("yout", [SL, D], F32, kind="ExternalOutput").ap()
    hsum = nc.dram_tensor("hsum", [16, SL], BF16, kind="Internal").ap()
    hinv = nc.dram_tensor("hinv", [16, SL], BF16, kind="Internal").ap()
    dbg = {}
    if debug:
        for nm, shp, dt in [("d_xfm", [128, DC * SL], BF16),
                            ("d_xfm_e", [128, DC * SL], BF16),
                            ("d_xfm_m", [128, DC * SL], BF16),
                            ("d_xfm_v", [128, DC * SL], BF16),
                            ("d_xfm_a", [128, DC * SL], BF16),
                            ("d_xfm_n", [128, DC * SL], BF16),
                            ("d_xfm_p", [128, DC * SL], BF16),
                            ("d_x2fm", [128, DC * SL], BF16),
                            ("d_hq", [128, NC1 * 512], mybir.dt.uint8),
                            ("d_xh2", [128, DC * SL], mybir.dt.uint8),
                            ("d_w1q", [128, DC * 4096], mybir.dt.uint8),
                            ("d_mbf", [1, SL], BF16),
                            ("d_b1q", [128, 32], F32),
                            ("d_b1c", [128, 32], F32),
                            ("d_rstd", [1, SL], BF16)]:
            dbg[nm] = nc.dram_tensor(nm, shp, dt, kind="ExternalOutput").ap()

    prog = {k: [] for k in ("sync", "scalar", "vector", "tensor", "gpsimd")}

    def em(eng, fn):
        prog[eng].append(fn)

    sems = {}

    class Sem:
        __slots__ = ("q", "n")

        def __init__(self, q):
            self.q = q
            self.n = 0

    perm = ExitStack()

    def mksem(name):
        sems[name] = Sem(perm.enter_context(nc.semaphore(name)))

    for name in ["cBIN", "cCONST", "cXIN", "cXCAST", "cXTR", "cXTRE",
                 "cSQV", "cSQMM", "cSTATS", "cNAR", "cBC", "cBCE",
                 "cWIN", "cWCAST", "cCST", "cCSV",
                 "cQKMM", "cQKEV", "cVTR", "cVTRE",
                 "cQKA", "cEXP", "cPV", "cOEV", "cINV", "cBCH", "cIBC",
                 "cNRM", "cPMM", "cX2",
                 "cSQV2", "cSQMM2", "cSTATS2", "cNAR2", "cBC2", "cBC2E",
                 "cXH2", "cFMM", "cFEV", "cHQ",
                 "cW2V", "cGMM", "cGEV", "cOTR", "cOASM",
                 "cMIN", "cMUSE", "cOUT", "cOCP", "cIRW", "cSUM", "cIDF", "cDBG", "cHIV", "cBQ",
                 "cSUMB", "cXINB", "cWINB", "cWINC", "cWIND", "cIRWB",
                 "cOUT1", "cOUT2", "cOUT3", "cMIN1", "cMIN2"]:
        mksem(name)

    def W(eng, sem, val):
        if val <= 0:
            return
        em(eng, lambda e, s=sems[sem].q, v=val: e.wait_ge(s, v))

    def I(eng, fn, sem=None, k=1):
        if sem is None:
            em(eng, fn)
            return None
        s = sems[sem]
        s.n += k
        em(eng, lambda e, f=fn, q=s.q, kk=k: f(e).then_inc(q, kk))
        return s.n

    def dma(eng, dst, src, sem):
        return I(eng, lambda e, d=dst, s=src: e.dma_start(out=d, in_=s), sem, 16)

    A = perm.enter_context

    ident = A(nc.sbuf_tensor("ident", [128, 128], BF16))
    wring = A(nc.sbuf_tensor("wring", [128, 3 * 1024], F32))
    borow_bf = A(nc.sbuf_tensor("borow_bf", [1, D], BF16))
    onesrow = A(nc.sbuf_tensor("onesrow", [1, 512], BF16))
    rstd_bf = A(nc.sbuf_tensor("rstd_bf", [1, SL], BF16))
    m_bf = A(nc.sbuf_tensor("m_bf", [1, SL], BF16))
    invr_bf = A(nc.sbuf_tensor("invr_bf", [1, SL], BF16))
    consts = A(nc.sbuf_tensor("consts", [128, 2], F32))
    onescol = A(nc.sbuf_tensor("onescol", [128, 1], BF16))
    b1csb = A(nc.sbuf_tensor("b1csb", [128, 32], F32))
    b2csb = A(nc.sbuf_tensor("b2csb", [128, 8], F32))
    b1q8 = A(nc.sbuf_tensor("b1q8", [128, 32], FP8))
    b2q8 = A(nc.sbuf_tensor("b2q8", [128, 8], FP8))
    b1q = A(nc.sbuf_tensor("b1q", [128, 32], F32))
    b2q = A(nc.sbuf_tensor("b2q", [128, 8], F32))
    scX = ExitStack()
    x_fm = scX.enter_context(nc.sbuf_tensor("x_fm", [128, DC * SL], BF16))

    dma("sync", ident[:], idbf, "cBIN")
    dma("sync", b1csb[:], b1c, "cBIN")
    dma("sync", b2csb[:], b2c, "cBIN")
    nbin = sems["cBIN"].n

    def g_memsets(e):
        e.memset(onescol[:], 1.0)
        e.memset(consts[:, 0:1], EXP_SHIFT)
        return e.memset(consts[:, 1:2], 1e-5)
    I("gpsimd", g_memsets, "cCONST")

    W("vector", "cBIN", nbin)
    I("vector", lambda e: e.tensor_copy(b1q8[:], b1csb[:]))
    I("vector", lambda e: e.tensor_copy(b2q8[:], b2csb[:]))
    I("vector", lambda e: e.memset(onesrow[:], 1.0), "cCONST")
    W("scalar", "cCONST", 2)
    I("scalar", lambda e: e.activation(b1q[:], b1q8[:], ACT.Identity,
                                       bias=0.0, scale=1.0))
    I("scalar", lambda e: e.activation(b2q[:], b2q8[:], ACT.Identity,
                                       bias=0.0, scale=1.0), "cBQ")
    nconst = sems["cCONST"].n

    scB = ExitStack()
    wqkv_bf = scB.enter_context(nc.sbuf_tensor("wqkv_bf", [128, DC * 3 * D], BF16, side="right"))
    r1neg = scB.enter_context(nc.sbuf_tensor("r1neg", [1, 3 * D], BF16, side="right"))
    r1b = scB.enter_context(nc.sbuf_tensor("r1b", [1, 3 * D], BF16, side="right"))

    W("sync", "cCONST", nconst)
    ring_i = 0

    WSLOT = ("cWIN", "cWINB", "cWINC")

    def stream_w(src_ap, width, cast_fn, rows_n=128, eng="vector"):
        nonlocal ring_i
        i = ring_i
        ring_i += 1
        sl = (i % 3) * 1024
        sw = WSLOT[i % 3]
        if i >= 3:
            W("sync", "cWCAST", i - 2)
        dma("sync", wring[0:rows_n, sl:sl + width], src_ap, sw)
        W(eng, sw, sems[sw].n)
        I(eng, lambda e, sl=sl: cast_fn(e, sl), "cWCAST")

    def emit_wqkv_piece(j):
        kc, p3 = divmod(j, 3)
        stream_w(wqkv[j * 128:(j + 1) * 128, :],
                 1024,
                 lambda e, sl, kc=kc, p3=p3: e.activation(
                     wqkv_bf[:, kc * 3 * D + p3 * 1024:
                             kc * 3 * D + (p3 + 1) * 1024],
                     wring[:, sl:sl + 1024], ACT.Identity,
                     bias=0.0, scale=1.0), eng="scalar")

    ph1 = ExitStack()
    xstage = ph1.enter_context(nc.sbuf_tensor("xstage", [128, 2 * D], F32, side="right"))
    xtmr = ph1.enter_context(nc.sbuf_tensor("xtmr", [128, 2 * D], BF16, side="right"))
    psTrX = [ph1.enter_context(nc.psum_tensor(f"psTrX{j}", [128, 128], BF16))
             for j in range(2)]

    W("vector", "cCONST", nconst)
    for tt in range(TT):
        sl = (tt % 2) * D
        if tt >= 2:
            W("sync", "cXCAST", tt - 1)
        sx = "cXIN" if tt % 2 == 0 else "cXINB"
        dma("sync", xstage[:, sl:sl + D], xin[tt * 128:(tt + 1) * 128, :], sx)
        W("vector", sx, sems[sx].n)
        if tt >= 2:
            W("vector", "cXTR", (tt - 1) * 8)
        I("vector", lambda e, sl=sl: e.tensor_copy(xtmr[:, sl:sl + D],
                                                   xstage[:, sl:sl + D]),
          "cXCAST")
        for c in range(DC):
            m = tt * 8 + c
            W("tensor", "cXCAST", tt + 1)
            if m >= 2:
                W("tensor", "cXTRE", m - 1)
            I("tensor", lambda e, m=m, sl=sl, c=c:
              e.transpose(psTrX[m % 2][:],
                          xtmr[:, sl + c * 128:sl + (c + 1) * 128], ident[:]),
              "cXTR")
            W("vector", "cXTR", m + 1)
            I("vector", lambda e, m=m, c=c, tt=tt:
              e.tensor_copy(
                  x_fm[:, c * SL + tt * 128:c * SL + (tt + 1) * 128],
                  psTrX[m % 2][:]), "cXTRE")
    ph1.close()
    nxtre = sems["cXTRE"].n
    if debug:
        W("sync", "cXTRE", nxtre)
        dma("sync", dbg["d_xfm_e"], x_fm[:], "cDBG")

    ph1b = ExitStack()
    sqs = ph1b.enter_context(nc.sbuf_tensor("sqs1", [128, 2 * SL], BF16))
    psSt = [ph1b.enter_context(nc.psum_tensor(f"psSt{j}", [1, SL], F32))
            for j in range(2)]
    W("vector", "cXTRE", nxtre)
    W("tensor", "cXTRE", nxtre)
    for qc in range(4):
        for c in range(DC):
            I("tensor", lambda e, qc=qc, c=c:
              e.matmul(psSt[0][:, qc * 512:(qc + 1) * 512], onescol[:],
                       x_fm[:, c * SL + qc * 512:c * SL + (qc + 1) * 512],
                       start=(c == 0), stop=(c == DC - 1)))
    for c in range(DC):
        if c >= 2:
            W("vector", "cSQMM", c - 1)
        I("vector", lambda e, c=c, sl=(c % 2) * SL:
          e.tensor_mul(sqs[:, sl:sl + SL], x_fm[:, c * SL:(c + 1) * SL],
                       x_fm[:, c * SL:(c + 1) * SL]), "cSQV")
        W("tensor", "cSQV", c + 1)
        for qc in range(4):
            I("tensor", lambda e, qc=qc, c=c, sl=(c % 2) * SL:
              e.matmul(psSt[1][:, qc * 512:(qc + 1) * 512], onescol[:],
                       sqs[:, sl + qc * 512:sl + (qc + 1) * 512],
                       start=(c == 0), stop=(c == DC - 1)),
              "cSQMM" if qc == 3 else None)
    I("tensor", lambda e: e.nop(), "cSTATS")
    W("vector", "cSTATS", 1)
    I("vector", lambda e: e.tensor_scalar(m_bf[:], psSt[0][:], 1.0 / D,
                                          None, ALU.mult))
    I("vector", lambda e: e.tensor_mul(sqs[0:1, 0:SL], m_bf[:], m_bf[:]))
    I("vector", lambda e: e.scalar_tensor_tensor(psSt[0][:], psSt[1][:],
                                                 1.0 / D, sqs[0:1, 0:SL], ALU.mult,
                                                 ALU.subtract), "cNAR")
    W("scalar", "cCONST", nconst)
    W("scalar", "cNAR", 1)
    I("scalar", lambda e: e.activation(invr_bf[:], psSt[0][:], ACT.Sqrt,
                                       bias=consts[0:1, 1:2], scale=1.0), "cNAR")
    W("vector", "cNAR", 2)
    def _recst_cNAR(e):
        with nc.allow_low_precision("LN rstd in bf16; 0.4% relative scale "
                                    "noise is far inside the native gate"):
            return e.reciprocal(rstd_bf[:], invr_bf[:])
    I("vector", _recst_cNAR, "cNAR")
    ph1b.close()

    ph1c = ExitStack()
    psB = ph1c.enter_context(nc.psum_tensor("psB", [128, SL], F32))
    rbc1 = ExitStack()
    rbc1_t = rbc1.enter_context(nc.sbuf_tensor("rbc1", [128, SL], BF16, side="right"))
    W("tensor", "cNAR", 3)
    for qc in range(4):
        I("tensor", lambda e, qc=qc:
          e.matmul(psB[:, qc * 512:(qc + 1) * 512], onesrow[0:1, 0:128],
                   rstd_bf[0:1, qc * 512:(qc + 1) * 512], start=True, stop=True),
          "cBC")
    W("vector", "cBC", 4)
    I("vector", lambda e: e.tensor_copy(rbc1_t[:], psB[:]), "cBCE")
    ph1c.close()

    scC = ExitStack()
    qk_fm = scC.enter_context(nc.sbuf_tensor("qk_fm", [128, 16 * SL], BF16))
    scV = ExitStack()
    v_fm = scV.enter_context(nc.sbuf_tensor("v_fm", [128, 8 * SL], BF16))
    for j in range(24):
        emit_wqkv_piece(j)
    nwq = sems["cWCAST"].n

    for p3 in range(3):
        stream_w(brow[0:1, p3 * 1024:(p3 + 1) * 1024], 1024,
                 lambda e, sl, p3=p3: e.tensor_copy(
                     r1b[0:1, p3 * 1024:(p3 + 1) * 1024],
                     wring[0:1, sl:sl + 1024]), rows_n=1)
    stream_w(brow[1:2, 0:1024], 1024,
             lambda e, sl: e.tensor_copy(borow_bf[0:1, 0:1024],
                                         wring[0:1, sl:sl + 1024]), rows_n=1)

    ph2cs = ExitStack()
    psCS = ph2cs.enter_context(nc.psum_tensor("psCS", [1, 512], F32))
    W("tensor", "cWCAST", nwq)
    for piece in range(6):
        if piece >= 1:
            W("tensor", "cCSV", piece)
        for kc in range(DC):
            I("tensor", lambda e, piece=piece, kc=kc:
              e.matmul(psCS[:], onescol[:],
                       wqkv_bf[:, kc * 3 * D + piece * 512:
                               kc * 3 * D + (piece + 1) * 512],
                       start=(kc == 0), stop=(kc == DC - 1)),
              "cCST" if kc == DC - 1 else None)
        W("vector", "cCST", piece + 1)
        I("vector", lambda e, piece=piece:
          e.tensor_scalar(r1neg[0:1, piece * 512:(piece + 1) * 512], psCS[:],
                          -1.0, None, ALU.mult), "cCSV")
    I("vector", lambda e: e.tensor_copy(r1neg[0:1, 0:1], r1neg[0:1, 0:1]), "cCSV")
    ncsv = sems["cCSV"].n
    ph2cs.close()

    ph2mm = ExitStack()
    psQK = [ph2mm.enter_context(nc.psum_tensor(f"psQK{j}", [128, 512], F32))
            for j in range(4)]
    W("tensor", "cCSV", ncsv)
    W("tensor", "cBCE", 1)
    g = 0
    for ec in range(EC):
        for tq in range(4):
            bank = g % 4
            if g >= 4:
                W("tensor", "cQKEV", g - 3)
            for kc in range(DC):
                I("tensor", lambda e, bank=bank, ec=ec, tq=tq, kc=kc:
                  e.matmul(psQK[bank][:],
                           wqkv_bf[:, kc * 3 * D + ec * 128:
                                   kc * 3 * D + (ec + 1) * 128],
                           x_fm[:, kc * SL + tq * 512:kc * SL + (tq + 1) * 512],
                           start=(kc == 0), stop=False))
            I("tensor", lambda e, bank=bank, ec=ec, tq=tq:
              e.matmul(psQK[bank][:], r1neg[0:1, ec * 128:(ec + 1) * 128],
                       m_bf[0:1, tq * 512:(tq + 1) * 512],
                       start=False, stop=False))
            I("tensor", lambda e, bank=bank, ec=ec, tq=tq:
              e.matmul(psQK[bank][:], r1b[0:1, ec * 128:(ec + 1) * 128],
                       invr_bf[0:1, tq * 512:(tq + 1) * 512],
                       start=False, stop=True), "cQKMM")
            W("vector", "cQKMM", g + 1)
            if ec < 16:
                dstap = lambda ec=ec, tq=tq: qk_fm[:, ec * SL + tq * 512:
                                                   ec * SL + (tq + 1) * 512]
            else:
                dstap = lambda ec=ec, tq=tq: v_fm[:, (ec - 16) * SL + tq * 512:
                                                  (ec - 16) * SL + (tq + 1) * 512]
            I("vector", lambda e, bank=bank, d=dstap, tq=tq:
              e.tensor_mul(d(), psQK[bank][:],
                           rbc1_t[:, tq * 512:(tq + 1) * 512]), "cQKEV")
            g += 1
    nqkev = sems["cQKEV"].n
    if debug:
        W("sync", "cQKEV", nqkev)
        dma("sync", dbg["d_xfm_m"], x_fm[:], "cDBG")
    ph2mm.close()
    rbc1.close()
    scB.close()

    scD = ExitStack()
    v_aug = scD.enter_context(nc.sbuf_tensor("v_aug", [128, 16 * 1040], BF16, side="right"))
    pt_c = scD.enter_context(nc.sbuf_tensor("pt_c", [128, 3 * 1024], BF16,
                                            side="right"))
    ph2tr = ExitStack()
    psTrV = [ph2tr.enter_context(nc.psum_tensor(f"psTrV{j}", [128, 128], BF16))
             for j in range(2)]
    W("vector", "cQKEV", 96)
    for tt in range(TT):
        I("vector", lambda e, tt=tt: e.memset(
            v_aug[:, tt * 1040:(tt + 1) * 1040]
            .rearrange("p (h c) -> p h c", h=16, c=65)[:, :, 64:65], 1.0))
    for c in range(8):
        ecv = 16 + c
        W("tensor", "cQKEV", (ecv + 1) * 4)
        for tt in range(TT):
            m = c * TT + tt
            if m >= 2:
                W("tensor", "cVTRE", m - 1)
            I("tensor", lambda e, m=m, c=c, tt=tt:
              e.transpose(psTrV[m % 2][:],
                          v_fm[:, c * SL + tt * 128:c * SL + (tt + 1) * 128],
                          ident[:]), "cVTR")
            W("vector", "cVTR", m + 1)
            I("vector", lambda e, m=m, c=c, tt=tt:
              e.tensor_copy(
                  v_aug[:, tt * 1040 + (2 * c) * 65:
                        tt * 1040 + (2 * c) * 65 + 130]
                  .rearrange("p (h c) -> p h c", h=2, c=65)[:, :, 0:64],
                  psTrV[m % 2][:].rearrange("p (h c) -> p h c", h=2, c=64)),
              "cVTRE")
    nvtre = sems["cVTRE"].n
    if debug:
        W("sync", "cVTRE", nvtre)
        dma("sync", dbg["d_xfm_v"], x_fm[:], "cDBG")
    ph2tr.close()
    scV.close()

    ph3 = ExitStack()
    psS = [ph3.enter_context(nc.psum_tensor(f"psS{j}", [128, 1024], F32))
           for j in range(2)]
    psO = [ph3.enter_context(nc.psum_tensor(f"psO{j}", [128, 1024], F32))
           for j in range(2)]
    scE = ExitStack()
    attn_fm = scE.enter_context(nc.sbuf_tensor("attn_fm", [128, 8 * SL], BF16))
    scE2 = ExitStack()
    sums_bf = scE2.enter_context(nc.sbuf_tensor("sums_bf", [16, SL], BF16))
    inv_bf = scE2.enter_context(nc.sbuf_tensor("inv_bf", [16, SL], BF16))
    invb = scE2.enter_context(nc.sbuf_tensor("invb", [128, SL], BF16))
    invrow = [scE2.enter_context(nc.sbuf_tensor("invrow0", [1, SL], BF16))]
    srow2 = [scE2.enter_context(nc.sbuf_tensor(f"srow{s}", [1, SL], BF16))
             for s in range(2)]

    W("tensor", "cVTRE", nvtre)
    W("scalar", "cVTRE", nvtre)
    CH = 8 * 2 * KT * 2

    def _cqk(s):
        c, r = divmod(s, 2 * KT)
        qh, kt = divmod(r, KT)
        return c, qh, kt

    def emit_qk(k):
        s, qq = k // 2, k % 2
        c, qh, kt = _cqk(s)
        if k >= 2:
            W("tensor", "cEXP", k - 1)
        qcol = c * SL + qh * 1024 + qq * 512
        kcol = (8 + c) * SL + kt * 128
        I("tensor", lambda e, k=k, qcol=qcol, kcol=kcol:
          e.matmul(psS[k % 2][:, 0:512],
                   qk_fm[0:64, kcol:kcol + 128],
                   qk_fm[0:64, qcol:qcol + 512],
                   start=True, stop=True, tile_position=(0, 0)))
        I("tensor", lambda e, k=k, qcol=qcol, kcol=kcol:
          e.matmul(psS[k % 2][:, 512:1024],
                   qk_fm[64:128, kcol:kcol + 128],
                   qk_fm[64:128, qcol:qcol + 512],
                   start=True, stop=True, tile_position=(64, 0)), "cQKA")

    def emit_exp(k):
        W("scalar", "cQKA", k + 1)
        if k >= 3:
            W("scalar", "cPV", 2 * (k - 2))
        I("scalar", lambda e, k=k:
          e.activation(pt_c[:, (k % 3) * 1024:(k % 3) * 1024 + 1024],
                       psS[k % 2][:], ACT.Exp, bias=consts[:, 0:1],
                       scale=0.125), "cEXP")

    def emit_pv(k):
        s, qq = k // 2, k % 2
        c, qh, kt = _cqk(s)
        m = 2 * c + qh
        W("tensor", "cEXP", k + 1)
        for sub in range(2):
            if kt == 0 and qq == 0 and m >= 1:
                W("tensor", "cOEV", 4 * (m - 1) + 2 * (sub + 1))
            I("tensor", lambda e, sub=sub, qq=qq, kt=kt, c=c, k=k:
              e.matmul(psO[sub][0:65, qq * 512:(qq + 1) * 512],
                       v_aug[:, kt * 1040 + (2 * c + sub) * 65:
                             kt * 1040 + (2 * c + sub) * 65 + 65],
                       pt_c[:, (k % 3) * 1024 + sub * 512:
                             (k % 3) * 1024 + (sub + 1) * 512],
                       start=(kt == 0), stop=(kt == KT - 1)), "cPV")

    def emit_evac(k):
        s = k // 2
        c, qh, kt = _cqk(s)
        m = 2 * c + qh
        W("vector", "cPV", 2 * (k + 1))
        if qh == 0 and c >= 1:
            W("vector", "cSUM", 32 * c)
        for sub in range(2):
            I("vector", lambda e, sub=sub, c=c, qh=qh:
              e.tensor_copy(attn_fm[sub * 64:sub * 64 + 64,
                                    c * SL + qh * 1024:c * SL + qh * 1024 + 1024],
                            psO[sub][0:64, :]), "cOEV")
            I("vector", lambda e, sub=sub, qh=qh:
              e.tensor_copy(srow2[sub][0:1, qh * 1024:qh * 1024 + 1024],
                            psO[sub][64:65, :]), "cOEV")
        if qh == 1:
            W("sync", "cOEV", 4 * (m + 1))
            for sub in range(2):
                dma("sync", hsum[2 * c + sub:2 * c + sub + 1, :],
                    srow2[sub][:], "cSUM")

    emit_qk(0)
    for k in range(CH):
        emit_exp(k)
        if k + 1 < CH:
            emit_qk(k + 1)
        if k >= 2:
            emit_pv(k - 2)
            if (k - 2) % 2 == 1 and _cqk((k - 2) // 2)[2] == KT - 1:
                emit_evac(k - 2)
    for k in (CH - 2, CH - 1):
        emit_pv(k)
        if k % 2 == 1 and _cqk(k // 2)[2] == KT - 1:
            emit_evac(k)
    W("sync", "cSUM", sems["cSUM"].n)
    W("sync", "cSUMB", sems["cSUMB"].n)
    dma("sync", sums_bf[:], hsum, "cSUM")
    W("vector", "cSUM", sems["cSUM"].n)
    def _recip_lp(e):
        with nc.allow_low_precision("softmax denominator reciprocal in bf16; "
                                    "0.4% relative is far inside native tol"):
            return e.reciprocal(inv_bf[:], sums_bf[:])
    I("vector", _recip_lp, "cINV")
    W("sync", "cINV", 1)
    dma("sync", hinv, inv_bf[:], "cHIV")
    for hp in range(HEADS // 2):
        for sub in range(2):
            h = 2 * hp + sub
            ib = h
            if ib >= 1:
                W("sync", "cBCH", ib)
            W("sync", "cHIV", 16)
            dma("sync", invrow[0][:], hinv[h:h + 1, :], "cIRW")
            W("tensor", "cIRW", sems["cIRW"].n)
            W("tensor", "cPV", sems["cPV"].n)
            if hp >= 1:
                W("tensor", "cIBC", hp)
            for qq in range(4):
                I("tensor", lambda e, sub=sub, qq=qq, ib=ib:
                  e.matmul(psS[qq // 2][sub * 64:sub * 64 + 64,
                                        (qq % 2) * 512:(qq % 2) * 512 + 512],
                           onesrow[0:1, 0:64],
                           invrow[0][0:1, qq * 512:(qq + 1) * 512],
                           start=True, stop=True), "cBCH" if qq == 3 else None)
        W("vector", "cBCH", 2 * hp + 2)
        I("vector", lambda e: e.tensor_copy(invb[:, 0:1024], psS[0][:]))
        I("vector", lambda e: e.tensor_copy(invb[:, 1024:2048], psS[1][:]), "cIBC")
        I("vector", lambda e, hp=hp:
          e.tensor_mul(attn_fm[:, hp * SL:(hp + 1) * SL],
                       attn_fm[:, hp * SL:(hp + 1) * SL],
                       invb[:]), "cNRM")
    nnrm = sems["cNRM"].n
    if debug:
        W("sync", "cNRM", nnrm)
        dma("sync", dbg["d_xfm_n"], x_fm[:], "cDBG")
    ph3.close()
    scD.close()
    scE2.close()

    scG = ExitStack()
    x2_fm = scG.enter_context(nc.sbuf_tensor("x2_fm", [128, DC * SL], BF16, side="right"))
    scF = ExitStack()
    wout_bf = scF.enter_context(nc.sbuf_tensor("wout_bf", [128, DC * D], BF16, side="right"))

    for kc in range(DC):
        stream_w(wout[kc * 128:(kc + 1) * 128, :], 1024,
                 lambda e, sl, kc=kc: e.activation(
                     wout_bf[:, kc * D:(kc + 1) * D], wring[:, sl:sl + 1024],
                     ACT.Identity, bias=0.0, scale=1.0), eng="scalar")
    nwo = sems["cWCAST"].n

    ph4 = ExitStack()
    psP = [ph4.enter_context(nc.psum_tensor(f"psP{j}", [128, 512], F32))
           for j in range(4)]
    W("tensor", "cWCAST", nwo)
    W("tensor", "cNRM", nnrm)
    g = 0
    for oc in range(DC):
        for tq in range(4):
            bank = g % 4
            if g >= 4:
                W("tensor", "cX2", g - 3)
            for kc in range(DC):
                I("tensor", lambda e, bank=bank, oc=oc, tq=tq, kc=kc:
                  e.matmul(psP[bank][:],
                           wout_bf[:, kc * D + oc * 128:kc * D + (oc + 1) * 128],
                           attn_fm[:, kc * SL + tq * 512:kc * SL + (tq + 1) * 512],
                           start=(kc == 0), stop=False))
            I("tensor", lambda e, bank=bank, oc=oc, tq=tq:
              e.matmul(psP[bank][:], borow_bf[0:1, oc * 128:(oc + 1) * 128],
                       onesrow[0:1, 0:512],
                       start=False, stop=True), "cPMM")
            W("vector", "cPMM", g + 1)
            I("vector", lambda e, bank=bank, oc=oc, tq=tq:
              e.tensor_tensor(
                  x2_fm[:, oc * SL + tq * 512:oc * SL + (tq + 1) * 512],
                  psP[bank][:],
                  x_fm[:, oc * SL + tq * 512:oc * SL + (tq + 1) * 512],
                  ALU.add), "cX2")
            g += 1
    nx2 = sems["cX2"].n
    ph4.close()
    scE.close()
    scC.close()
    scX.close()
    scF.close()

    if debug:
        W("sync", "cX2", nx2)
    scI = ExitStack()
    identf = scI.enter_context(nc.sbuf_tensor("identf", [128, 128], F32))
    w1q8 = scI.enter_context(nc.sbuf_tensor("w1q8", [128, DC * 4096], FP8))
    hq8 = scI.enter_context(nc.sbuf_tensor("hq8", [128, NC1 * 512], FP8))
    hbf = scI.enter_context(nc.sbuf_tensor("hbf", [128, 2 * 512], BF16))
    y2bf = scI.enter_context(nc.sbuf_tensor("y2bf", [128, 3 * 512], BF16))
    w2q8 = scI.enter_context(nc.sbuf_tensor("w2q8", [128, NC1 * 1024], FP8))
    t1s = scI.enter_context(nc.sbuf_tensor("t1s", [128, 512], F32))
    ofm = scI.enter_context(nc.sbuf_tensor("ofm", [128, 3 * 512], F32))
    otm = scI.enter_context(nc.sbuf_tensor("otm", [128, 4 * D], F32))
    mring = scI.enter_context(nc.sbuf_tensor("mring", [128, 3 * 512], BF16))



    W("sync", "cX2", nx2)
    dma("sync", identf[:], idf32, "cIDF")
    scH = ExitStack()
    sqs = scH.enter_context(nc.sbuf_tensor("sqs2", [128, 2 * SL], BF16))
    m2bc = scH.enter_context(nc.sbuf_tensor("m2bc", [128, SL], BF16))
    r2bc = scH.enter_context(nc.sbuf_tensor("r2bc", [128, SL], BF16))
    xh2q8 = scH.enter_context(nc.sbuf_tensor("xh2q8", [128, DC * SL], FP8))
    ph4b = ExitStack()
    psSt2 = [ph4b.enter_context(nc.psum_tensor(f"psSt2{j}", [1, SL], F32))
             for j in range(2)]
    W("tensor", "cX2", nx2)
    W("vector", "cX2", nx2)
    for qc in range(4):
        for c in range(DC):
            I("tensor", lambda e, qc=qc, c=c:
              e.matmul(psSt2[0][:, qc * 512:(qc + 1) * 512], onescol[:],
                       x2_fm[:, c * SL + qc * 512:c * SL + (qc + 1) * 512],
                       start=(c == 0), stop=(c == DC - 1)))
    for c in range(DC):
        if c >= 2:
            W("vector", "cSQMM2", c - 1)
        I("vector", lambda e, c=c, sl=(c % 2) * SL:
          e.tensor_mul(sqs[:, sl:sl + SL], x2_fm[:, c * SL:(c + 1) * SL],
                       x2_fm[:, c * SL:(c + 1) * SL]), "cSQV2")
        W("tensor", "cSQV2", c + 1)
        for qc in range(4):
            I("tensor", lambda e, qc=qc, c=c, sl=(c % 2) * SL:
              e.matmul(psSt2[1][:, qc * 512:(qc + 1) * 512], onescol[:],
                       sqs[:, sl + qc * 512:sl + (qc + 1) * 512],
                       start=(c == 0), stop=(c == DC - 1)),
              "cSQMM2" if qc == 3 else None)
    I("tensor", lambda e: e.nop(), "cSTATS2")
    W("vector", "cSTATS2", 1)
    I("vector", lambda e: e.tensor_scalar(m_bf[:], psSt2[0][:], 1.0 / D,
                                          None, ALU.mult))
    I("vector", lambda e: e.tensor_mul(sqs[0:1, 0:SL], m_bf[:], m_bf[:]))
    I("vector", lambda e: e.scalar_tensor_tensor(psSt2[0][:], psSt2[1][:],
                                                 1.0 / D, sqs[0:1, 0:SL], ALU.mult,
                                                 ALU.subtract), "cNAR2")
    W("scalar", "cCONST", nconst)
    W("scalar", "cNAR2", 1)
    I("scalar", lambda e: e.activation(invr_bf[:], psSt2[0][:], ACT.Sqrt,
                                       bias=consts[0:1, 1:2], scale=1.0), "cNAR2")
    W("vector", "cNAR2", 2)
    def _recst_cNAR2(e):
        with nc.allow_low_precision("LN rstd in bf16; 0.4% relative scale "
                                    "noise is far inside the native gate"):
            return e.reciprocal(rstd_bf[:], invr_bf[:])
    I("vector", _recst_cNAR2, "cNAR2")
    ph4b.close()

    W("scalar", "cX2", nx2)
    w1base = sems["cWCAST"].n
    for q4 in range(4):
        for kc in range(DC):
            stream_w(w1t[(q4 * 8 + kc) * 128:(q4 * 8 + kc + 1) * 128, :],
                     1024,
                     lambda e, sl, kc=kc, q4=q4: e.activation(
                         w1q8[:, kc * 4096 + q4 * 1024:
                              kc * 4096 + (q4 + 1) * 1024],
                         wring[:, sl:sl + 1024], ACT.Identity,
                         bias=0.0, scale=1.0), eng="scalar")
    nw1 = sems["cWCAST"].n
    w2base = ring_i
    w2win = []
    for k in range(NC1):
        i = ring_i
        ring_i += 1
        sl = (i % 3) * 1024
        sw = WSLOT[i % 3]
        j = i - 3
        if j >= 0:
            if j < w2base:
                W("sync", "cWCAST", j + 1)
            else:
                W("sync", "cW2V", j + 1 - w2base)
        dma("sync", wring[:, sl:sl + 1024],
            w2t[k * 128:(k + 1) * 128, 0:1024], sw)
        w2win.append((sw, sems[sw].n))

    ph4c = ExitStack()
    psB2 = ph4c.enter_context(nc.psum_tensor("psB2", [128, SL], F32))
    W("tensor", "cNAR2", 3)
    for qc in range(4):
        I("tensor", lambda e, qc=qc:
          e.matmul(psB2[:, qc * 512:(qc + 1) * 512], onesrow[0:1, 0:128],
                   m_bf[0:1, qc * 512:(qc + 1) * 512], start=True, stop=True),
          "cBC2" if qc == 3 else None)
    W("vector", "cBC2", 1)
    I("vector", lambda e: e.tensor_copy(m2bc[:], psB2[:]), "cBC2E")
    W("tensor", "cBC2E", 1)
    for qc in range(4):
        I("tensor", lambda e, qc=qc:
          e.matmul(psB2[:, qc * 512:(qc + 1) * 512], onesrow[0:1, 0:128],
                   rstd_bf[0:1, qc * 512:(qc + 1) * 512], start=True, stop=True),
          "cBC2" if qc == 3 else None)
    W("vector", "cBC2", 2)
    I("vector", lambda e: e.tensor_copy(r2bc[:], psB2[:]), "cBC2E")
    ph4c.close()

    for c in range(DC):
        sl = (c % 2) * SL
        I("vector", lambda e, c=c, sl=sl:
          e.tensor_sub(sqs[:, sl:sl + SL], x2_fm[:, c * SL:(c + 1) * SL],
                       m2bc[:]))
        I("vector", lambda e, c=c, sl=sl:
          e.tensor_mul(xh2q8[:, c * SL:(c + 1) * SL], sqs[:, sl:sl + SL],
                       r2bc[:]), "cXH2")
    nxh2 = sems["cXH2"].n

    ph5 = ExitStack()
    psF = [ph5.enter_context(nc.psum_tensor(f"psF{j}", [128, 512], F32))
           for j in range(3)]
    psG = [ph5.enter_context(nc.psum_tensor(f"psG{j}", [128, 512], F32))
           for j in range(3)]
    psTrO = [ph5.enter_context(nc.psum_tensor(f"psTrO{j}", [128, 128], F32))
             for j in range(2)]

    W("tensor", "cXH2", nxh2)
    gf = 0
    gg = 0
    tro = 0
    pend = []

    def emit_oasm(p):
        nonlocal tro
        ts_p, oc_p = divmod(p, DC)
        W("tensor", "cIDF", 16)
        W("tensor", "cOASM", p + 1)
        for j in range(4):
            tt = ts_p * 4 + j
            if tro >= 2:
                W("tensor", "cOCP", tro - 1)
            I("tensor", lambda e, tro=tro, j=j, p=p:
              e.transpose(psTrO[tro % 2][:],
                          ofm[:, (p % 3) * 512 + j * 128:
                              (p % 3) * 512 + (j + 1) * 128],
                          identf[:]), "cOTR")
            W("vector", "cOTR", tro + 1)
            if oc_p == 0 and tt >= 4:
                so = ("cOUT", "cOUT1", "cOUT2", "cOUT3")[tt % 4]
                W("vector", so, 16 * ts_p)
            I("vector", lambda e, tro=tro, tt=tt, oc_p=oc_p:
              e.tensor_copy(otm[:, (tt % 4) * D + oc_p * 128:
                            (tt % 4) * D + (oc_p + 1) * 128],
                            psTrO[tro % 2][:]), "cOCP")
            tro += 1
        if oc_p == DC - 1:
            for j in range(4):
                tt = ts_p * 4 + j
                W("sync", "cOCP", (ts_p * DC + 7) * 4 + j + 1)
                dma("sync", yout[tt * 128:(tt + 1) * 128, :],
                    otm[:, (tt % 4) * D:(tt % 4) * D + D],
                    ("cOUT", "cOUT1", "cOUT2", "cOUT3")[tt % 4])

    for ts in range(TSN):
        for ncc in range(NC1):
            bank = gf % 3
            if ts == 0:
                W("tensor", "cWCAST", w1base + 8 * (ncc // 8 + 1))
            if gf >= 3:
                W("tensor", "cFEV", gf - 2)
            for kc in range(DC):
                I("tensor", lambda e, bank=bank, ncc=ncc, ts=ts, kc=kc:
                  e.matmul(psF[bank][:],
                           w1q8[:, kc * 4096 + ncc * 128:kc * 4096 + (ncc + 1) * 128],
                           xh2q8[:, kc * SL + ts * 512:kc * SL + (ts + 1) * 512],
                           start=(kc == 0), stop=(kc == DC - 1)), "cFMM" if kc == DC - 1 else None)
            W("scalar", "cBQ", 1)
            W("scalar", "cFMM", gf + 1)
            if gf >= 2:
                W("scalar", "cHQ", gf - 1)
            hs = gf % 2
            I("scalar", lambda e, bank=bank, hs=hs, ncc=ncc:
              e.activation(hbf[:, hs * 512:(hs + 1) * 512], psF[bank][:],
                           ACT.Relu, bias=b1q[:, ncc:ncc + 1], scale=1.0),
              "cFEV")
            W("vector", "cFEV", gf + 1)
            if ts >= 1:
                W("vector", "cGMM", ts * 8)
            I("vector", lambda e, hs=hs, ncc=ncc:
              e.tensor_copy(hq8[:, ncc * 512:(ncc + 1) * 512],
                            hbf[:, hs * 512:(hs + 1) * 512]), "cHQ")
            if ts == 0:
                sl2 = ((w2base + ncc) % 3) * 1024
                W("vector", w2win[ncc][0], w2win[ncc][1])
                I("vector", lambda e, ncc=ncc, sl2=sl2:
                  e.tensor_copy(w2q8[:, ncc * 1024:(ncc + 1) * 1024],
                                wring[:, sl2:sl2 + 1024]), "cW2V")
            gf += 1
        for oc in range(DC):
            gb = gg % 3
            if gg >= 3:
                W("sync", "cMUSE", gg - 2)
            sm = ("cMIN", "cMIN1", "cMIN2")[gg % 3]
            dma("sync", mring[:, (gg % 3) * 512:(gg % 3 + 1) * 512],
                maskfm[oc * 128:(oc + 1) * 128, ts * 512:(ts + 1) * 512], sm)
            nmin = (sm, sems[sm].n)
            W("tensor", "cHQ", (ts + 1) * NC1)
            if gg == 0:
                W("tensor", "cW2V", NC1)
            if gg >= 3:
                W("tensor", "cGEV", gg - 2)
            for ncc in range(NC1):
                I("tensor", lambda e, gb=gb, ncc=ncc, oc=oc:
                  e.matmul(psG[gb][:],
                           w2q8[:, ncc * 1024 + oc * 128:
                                ncc * 1024 + (oc + 1) * 128],
                           hq8[:, ncc * 512:(ncc + 1) * 512],
                           start=(ncc == 0), stop=(ncc == NC1 - 1)),
                  "cGMM" if ncc == NC1 - 1 else None)
            W("scalar", "cGMM", gg + 1)
            if gg >= 3:
                W("scalar", "cMUSE", gg - 2)
            I("scalar", lambda e, gb=gb, gg=gg, oc=oc:
              e.activation(y2bf[:, (gg % 3) * 512:(gg % 3 + 1) * 512],
                           psG[gb][:], ACT.Identity,
                           bias=b2q[:, oc:oc + 1], scale=1.0), "cGEV")
            W("vector", "cGEV", gg + 1)
            W("vector", nmin[0], nmin[1])
            I("vector", lambda e, gg=gg:
              e.tensor_mul(t1s[:], y2bf[:, (gg % 3) * 512:(gg % 3 + 1) * 512],
                           mring[:, (gg % 3) * 512:(gg % 3 + 1) * 512]), "cMUSE")
            if gg >= 3:
                W("vector", "cOTR", 4 * (gg - 2))
            I("vector", lambda e, gg=gg, oc=oc, ts=ts:
              e.tensor_tensor(ofm[:, (gg % 3) * 512:(gg % 3 + 1) * 512],
                              t1s[:],
                              x2_fm[:, oc * SL + ts * 512:oc * SL + (ts + 1) * 512],
                              ALU.add), "cOASM")
            pend.append(gg)
            if len(pend) > 2:
                emit_oasm(pend.pop(0))
            gg += 1
    while pend:
        emit_oasm(pend.pop(0))
    ph5.close()
    scH.close()
    scI.close()
    scG.close()

    if debug:
        W("sync", "cOUT", sems["cOUT"].n)
        W("sync", "cDBG", sems["cDBG"].n)
        dma("sync", dbg["d_xfm"], x_fm[:], "cOUT")
        dma("sync", dbg["d_x2fm"], x2_fm[:], "cOUT")
        dma("sync", dbg["d_hq"], hq8[:].bitcast(mybir.dt.uint8), "cOUT")
        dma("sync", dbg["d_xh2"], xh2q8[:].bitcast(mybir.dt.uint8), "cOUT")
        dma("sync", dbg["d_w1q"], w1q8[:].bitcast(mybir.dt.uint8), "cOUT")
        dma("sync", dbg["d_mbf"], m_bf[:], "cOUT")
        dma("sync", dbg["d_b1q"], b1q[:], "cOUT")
        dma("sync", dbg["d_b1c"], b1csb[:], "cOUT")
        dma("sync", dbg["d_rstd"], rstd_bf[:], "cOUT")
    for so in ("cOUT", "cOUT1", "cOUT2", "cOUT3"):
        W("gpsimd", so, sems[so].n)

    with nc.Block() as blk:
        @blk.sync
        def _(e):
            for fn in prog["sync"]:
                fn(e)

        @blk.scalar
        def _(e):
            for fn in prog["scalar"]:
                fn(e)

        @blk.vector
        def _(e):
            for fn in prog["vector"]:
                fn(e)

        @blk.tensor
        def _(e):
            for fn in prog["tensor"]:
                fn(e)

        @blk.gpsimd
        def _(e):
            for fn in prog["gpsimd"]:
                fn(e)

    perm.close()
    return nc


_NC_CACHE = None


def kernel(L0_attn_x, L0_attn_W_qkv, L0_attn_b_qkv, L0_attn_W_out,
           L0_attn_b_out, L0_ffn_W1, L0_ffn_b1, L0_ffn_W2, L0_ffn_b2):
    global _NC_CACHE
    x = np.asarray(L0_attn_x, dtype=np.float32)
    B = x.shape[0]
    keep = _dropout_keep_mask((B, SL, D))
    maskf = np.where(keep, np.float32(1.25), np.float32(0.0))

    wqkv = np.ascontiguousarray(
        np.asarray(L0_attn_W_qkv, dtype=np.float32)
        .reshape(8, 128, 3, 1024).transpose(0, 2, 1, 3).reshape(24 * 128, 1024))
    wout = np.ascontiguousarray(np.asarray(L0_attn_W_out, dtype=np.float32))
    w1t = np.ascontiguousarray(
        np.asarray(L0_ffn_W1, dtype=np.float32).T
        .reshape(8, 128, 4, 1024).transpose(2, 0, 1, 3).reshape(32 * 128, 1024))
    w2t = np.ascontiguousarray(np.asarray(L0_ffn_W2, dtype=np.float32).T)
    bq = np.asarray(L0_attn_b_qkv, dtype=np.float32)
    bo = np.asarray(L0_attn_b_out, dtype=np.float32)
    b1 = np.asarray(L0_ffn_b1, dtype=np.float32)
    b2 = np.asarray(L0_ffn_b2, dtype=np.float32)

    brow = np.zeros((2, 3 * D), dtype=np.float32)
    brow[0, :] = bq
    brow[1, :D] = bo
    b1c = np.ascontiguousarray(b1.reshape(32, 128).T)
    b2c = np.ascontiguousarray(b2.reshape(8, 128).T)
    idbf = np.eye(128, dtype=ml_dtypes.bfloat16)
    idf32 = np.eye(128, dtype=np.float32)

    if _NC_CACHE is None:
        _NC_CACHE = build_nc()
    nc = _NC_CACHE

    in_maps = []
    for b in range(B):
        in_maps.append({
            "xin": np.ascontiguousarray(x[b]),
            "wqkv": wqkv, "wout": wout, "w1t": w1t, "w2t": w2t,
            "brow": brow, "b1c": b1c, "b2c": b2c,
            "idbf": idbf, "idf32": idf32,
            "maskfm": np.ascontiguousarray(
                maskf[b].T.astype(ml_dtypes.bfloat16)),
        })
    res = run_bass_kernel_spmd(nc, in_maps, core_ids=list(range(B)))
    out = np.stack([np.asarray(r["yout"]) for r in res.results], axis=0)
    return out.astype(np.float32)


# version 21
# speedup vs baseline: 1.0253x; 1.0253x over previous
import numpy as np
import ml_dtypes
import concourse.bass as bass
import concourse.mybir as mybir
from concourse.bass_utils import run_bass_kernel_spmd
from contextlib import ExitStack

F32 = mybir.dt.float32
BF16 = mybir.dt.bfloat16
FP8 = mybir.dt.float8e5
ALU = mybir.AluOpType
ACT = mybir.ActivationFunctionType

SL = 2048
D = 1024
HEADS = 16
TT = 16
DC = 8
EC = 24
KT = 16
NC1 = 32
TSN = 4
EXP_SHIFT = -12.0


def _threefry2x32(k0, k1, x0, x1):
    rot = [13, 15, 26, 6, 17, 29, 16, 24]
    ks0, ks1 = np.uint32(k0), np.uint32(k1)
    ks2 = np.uint32(np.uint32(ks0) ^ np.uint32(ks1) ^ np.uint32(0x1BD11BDA))
    x0 = (x0 + ks0).astype(np.uint32)
    x1 = (x1 + ks1).astype(np.uint32)

    def rotl(v, d):
        return ((v << np.uint32(d)) | (v >> np.uint32(32 - d))).astype(np.uint32)

    ks = [ks0, ks1, ks2]
    for i in range(5):
        r = rot[:4] if i % 2 == 0 else rot[4:]
        for j in range(4):
            x0 = (x0 + x1).astype(np.uint32)
            x1 = rotl(x1, r[j])
            x1 = (x1 ^ x0).astype(np.uint32)
        x0 = (x0 + ks[(i + 1) % 3]).astype(np.uint32)
        x1 = (x1 + ks[(i + 2) % 3] + np.uint32(i + 1)).astype(np.uint32)
    return x0, x1


def _dropout_keep_mask(shape, p_keep=0.8, key=(0, 42)):
    n = int(np.prod(shape))
    idx = np.arange(n, dtype=np.uint64)
    hi = (idx >> np.uint64(32)).astype(np.uint32)
    lo = (idx & np.uint64(0xFFFFFFFF)).astype(np.uint32)
    b0, b1 = _threefry2x32(key[0], key[1], hi, lo)
    bits = (b0 ^ b1).astype(np.uint32)
    fl = ((bits >> np.uint32(9)) | np.uint32(0x3F800000)).view(np.float32) \
        - np.float32(1.0)
    return (fl < np.float32(p_keep)).reshape(shape)


def build_nc(debug=False):
    nc = bass.Bass("TRN2", target_bir_lowering=False,
                   dynamic_dma_scratch_size=512)

    xin = nc.dram_tensor("xin", [SL, D], F32, kind="ExternalInput").ap()
    wqkv = nc.dram_tensor("wqkv", [24 * 128, 1024], F32, kind="ExternalInput").ap()
    wout = nc.dram_tensor("wout", [D, D], F32, kind="ExternalInput").ap()
    w1t = nc.dram_tensor("w1t", [32 * 128, 1024], F32, kind="ExternalInput").ap()
    w2t = nc.dram_tensor("w2t", [4 * D, D], F32, kind="ExternalInput").ap()
    brow = nc.dram_tensor("brow", [2, 3 * D], F32, kind="ExternalInput").ap()
    b1c = nc.dram_tensor("b1c", [128, 32], F32, kind="ExternalInput").ap()
    b2c = nc.dram_tensor("b2c", [128, 8], F32, kind="ExternalInput").ap()
    idbf = nc.dram_tensor("idbf", [128, 128], BF16, kind="ExternalInput").ap()
    idf32 = nc.dram_tensor("idf32", [128, 128], F32, kind="ExternalInput").ap()
    maskfm = nc.dram_tensor("maskfm", [D, SL], BF16, kind="ExternalInput").ap()
    yout = nc.dram_tensor("yout", [SL, D], F32, kind="ExternalOutput").ap()
    hsum = nc.dram_tensor("hsum", [16, SL], BF16, kind="Internal").ap()
    hinv = nc.dram_tensor("hinv", [16, SL], BF16, kind="Internal").ap()
    dbg = {}
    if debug:
        for nm, shp, dt in [("d_xfm", [128, DC * SL], BF16),
                            ("d_xfm_e", [128, DC * SL], BF16),
                            ("d_xfm_m", [128, DC * SL], BF16),
                            ("d_xfm_v", [128, DC * SL], BF16),
                            ("d_xfm_a", [128, DC * SL], BF16),
                            ("d_xfm_n", [128, DC * SL], BF16),
                            ("d_xfm_p", [128, DC * SL], BF16),
                            ("d_x2fm", [128, DC * SL], BF16),
                            ("d_hq", [128, NC1 * 512], mybir.dt.uint8),
                            ("d_xh2", [128, DC * SL], mybir.dt.uint8),
                            ("d_w1q", [128, DC * 4096], mybir.dt.uint8),
                            ("d_mbf", [1, SL], BF16),
                            ("d_b1q", [128, 32], F32),
                            ("d_b1c", [128, 32], F32),
                            ("d_rstd", [1, SL], BF16)]:
            dbg[nm] = nc.dram_tensor(nm, shp, dt, kind="ExternalOutput").ap()

    prog = {k: [] for k in ("sync", "scalar", "vector", "tensor", "gpsimd")}

    def em(eng, fn):
        prog[eng].append(fn)

    sems = {}

    class Sem:
        __slots__ = ("q", "n")

        def __init__(self, q):
            self.q = q
            self.n = 0

    perm = ExitStack()

    def mksem(name):
        sems[name] = Sem(perm.enter_context(nc.semaphore(name)))

    for name in ["cBIN", "cCONST", "cXIN", "cXCAST", "cXTR", "cXTRE",
                 "cSQV", "cSQMM", "cSTATS", "cNAR", "cBC", "cBCE",
                 "cWIN", "cWCAST", "cCST", "cCSV",
                 "cQKMM", "cQKEV", "cVTR", "cVTRE",
                 "cQKA", "cEXP", "cPV", "cOEV", "cINV", "cBCH", "cIBC",
                 "cNRM", "cPMM", "cX2",
                 "cSQV2", "cSQMM2", "cSTATS2", "cNAR2", "cBC2", "cBC2E",
                 "cXH2", "cFMM", "cFEV", "cHQ",
                 "cW2V", "cGMM", "cGEV", "cOTR", "cOASM",
                 "cMIN", "cMUSE", "cOUT", "cOCP", "cIRW", "cSUM", "cIDF", "cDBG", "cHIV", "cBQ",
                 "cSUMB", "cXINB", "cWINB", "cWINC", "cWIND", "cIRWB",
                 "cOUT1", "cOUT2", "cOUT3", "cMIN1", "cMIN2"]:
        mksem(name)

    def W(eng, sem, val):
        if val <= 0:
            return
        em(eng, lambda e, s=sems[sem].q, v=val: e.wait_ge(s, v))

    def I(eng, fn, sem=None, k=1):
        if sem is None:
            em(eng, fn)
            return None
        s = sems[sem]
        s.n += k
        em(eng, lambda e, f=fn, q=s.q, kk=k: f(e).then_inc(q, kk))
        return s.n

    def dma(eng, dst, src, sem):
        return I(eng, lambda e, d=dst, s=src: e.dma_start(out=d, in_=s), sem, 16)

    A = perm.enter_context

    ident = A(nc.sbuf_tensor("ident", [128, 128], BF16))
    wring = A(nc.sbuf_tensor("wring", [128, 3 * 1024], F32))
    borow_bf = A(nc.sbuf_tensor("borow_bf", [1, D], BF16))
    onesrow = A(nc.sbuf_tensor("onesrow", [1, 512], BF16))
    rstd_bf = A(nc.sbuf_tensor("rstd_bf", [1, SL], BF16))
    m_bf = A(nc.sbuf_tensor("m_bf", [1, SL], BF16))
    invr_bf = A(nc.sbuf_tensor("invr_bf", [1, SL], BF16))
    consts = A(nc.sbuf_tensor("consts", [128, 2], F32))
    onescol = A(nc.sbuf_tensor("onescol", [128, 1], BF16))
    b1csb = A(nc.sbuf_tensor("b1csb", [128, 32], F32))
    b2csb = A(nc.sbuf_tensor("b2csb", [128, 8], F32))
    b1q8 = A(nc.sbuf_tensor("b1q8", [128, 32], FP8))
    b2q8 = A(nc.sbuf_tensor("b2q8", [128, 8], FP8))
    b1q = A(nc.sbuf_tensor("b1q", [128, 32], F32))
    b2q = A(nc.sbuf_tensor("b2q", [128, 8], F32))
    scX = ExitStack()
    x_fm = scX.enter_context(nc.sbuf_tensor("x_fm", [128, DC * SL], BF16))

    dma("sync", ident[:], idbf, "cBIN")
    dma("sync", b1csb[:], b1c, "cBIN")
    dma("sync", b2csb[:], b2c, "cBIN")
    nbin = sems["cBIN"].n

    def g_memsets(e):
        e.memset(onescol[:], 1.0)
        e.memset(consts[:, 0:1], EXP_SHIFT)
        return e.memset(consts[:, 1:2], 1e-5)
    I("gpsimd", g_memsets, "cCONST")

    W("vector", "cBIN", nbin)
    I("vector", lambda e: e.tensor_copy(b1q8[:], b1csb[:]))
    I("vector", lambda e: e.tensor_copy(b2q8[:], b2csb[:]))
    I("vector", lambda e: e.memset(onesrow[:], 1.0), "cCONST")
    W("scalar", "cCONST", 2)
    I("scalar", lambda e: e.activation(b1q[:], b1q8[:], ACT.Identity,
                                       bias=0.0, scale=1.0))
    I("scalar", lambda e: e.activation(b2q[:], b2q8[:], ACT.Identity,
                                       bias=0.0, scale=1.0), "cBQ")
    nconst = sems["cCONST"].n

    scB = ExitStack()
    wqkv_bf = scB.enter_context(nc.sbuf_tensor("wqkv_bf", [128, DC * 3 * D], BF16, side="right"))
    r1neg = scB.enter_context(nc.sbuf_tensor("r1neg", [1, 3 * D], BF16, side="right"))
    r1b = scB.enter_context(nc.sbuf_tensor("r1b", [1, 3 * D], BF16, side="right"))

    W("sync", "cCONST", nconst)
    ring_i = 0

    WSLOT = ("cWIN", "cWINB", "cWINC")

    def stream_w(src_ap, width, cast_fn, rows_n=128, eng="vector"):
        nonlocal ring_i
        i = ring_i
        ring_i += 1
        sl = (i % 3) * 1024
        sw = WSLOT[i % 3]
        if i >= 3:
            W("sync", "cWCAST", i - 2)
        dma("sync", wring[0:rows_n, sl:sl + width], src_ap, sw)
        W(eng, sw, sems[sw].n)
        I(eng, lambda e, sl=sl: cast_fn(e, sl), "cWCAST")

    def emit_wqkv_piece(j):
        kc, p3 = divmod(j, 3)
        stream_w(wqkv[j * 128:(j + 1) * 128, :],
                 1024,
                 lambda e, sl, kc=kc, p3=p3: e.activation(
                     wqkv_bf[:, kc * 3 * D + p3 * 1024:
                             kc * 3 * D + (p3 + 1) * 1024],
                     wring[:, sl:sl + 1024], ACT.Identity,
                     bias=0.0, scale=1.0), eng="scalar")

    ph1 = ExitStack()
    xstage = ph1.enter_context(nc.sbuf_tensor("xstage", [128, 2 * D], F32, side="right"))
    xtmr = ph1.enter_context(nc.sbuf_tensor("xtmr", [128, 2 * D], BF16, side="right"))
    psTrX = [ph1.enter_context(nc.psum_tensor(f"psTrX{j}", [128, 128], BF16))
             for j in range(2)]

    W("vector", "cCONST", nconst)
    for tt in range(TT):
        sl = (tt % 2) * D
        if tt >= 2:
            W("sync", "cXCAST", tt - 1)
        sx = "cXIN" if tt % 2 == 0 else "cXINB"
        dma("sync", xstage[:, sl:sl + D], xin[tt * 128:(tt + 1) * 128, :], sx)
        W("vector", sx, sems[sx].n)
        if tt >= 2:
            W("vector", "cXTR", (tt - 1) * 8)
        I("vector", lambda e, sl=sl: e.tensor_copy(xtmr[:, sl:sl + D],
                                                   xstage[:, sl:sl + D]),
          "cXCAST")
        for c in range(DC):
            m = tt * 8 + c
            W("tensor", "cXCAST", tt + 1)
            if m >= 2:
                W("tensor", "cXTRE", m - 1)
            I("tensor", lambda e, m=m, sl=sl, c=c:
              e.transpose(psTrX[m % 2][:],
                          xtmr[:, sl + c * 128:sl + (c + 1) * 128], ident[:]),
              "cXTR")
            W("vector", "cXTR", m + 1)
            I("vector", lambda e, m=m, c=c, tt=tt:
              e.tensor_copy(
                  x_fm[:, c * SL + tt * 128:c * SL + (tt + 1) * 128],
                  psTrX[m % 2][:]), "cXTRE")
    ph1.close()
    nxtre = sems["cXTRE"].n
    if debug:
        W("sync", "cXTRE", nxtre)
        dma("sync", dbg["d_xfm_e"], x_fm[:], "cDBG")

    ph1b = ExitStack()
    sqs = ph1b.enter_context(nc.sbuf_tensor("sqs1", [128, 2 * SL], BF16))
    psSt = [ph1b.enter_context(nc.psum_tensor(f"psSt{j}", [1, SL], F32))
            for j in range(2)]
    W("vector", "cXTRE", nxtre)
    W("tensor", "cXTRE", nxtre)
    for qc in range(4):
        for c in range(DC):
            I("tensor", lambda e, qc=qc, c=c:
              e.matmul(psSt[0][:, qc * 512:(qc + 1) * 512], onescol[:],
                       x_fm[:, c * SL + qc * 512:c * SL + (qc + 1) * 512],
                       start=(c == 0), stop=(c == DC - 1)))
    for c in range(DC):
        if c >= 2:
            W("vector", "cSQMM", c - 1)
        I("vector", lambda e, c=c, sl=(c % 2) * SL:
          e.tensor_mul(sqs[:, sl:sl + SL], x_fm[:, c * SL:(c + 1) * SL],
                       x_fm[:, c * SL:(c + 1) * SL]), "cSQV")
        W("tensor", "cSQV", c + 1)
        for qc in range(4):
            I("tensor", lambda e, qc=qc, c=c, sl=(c % 2) * SL:
              e.matmul(psSt[1][:, qc * 512:(qc + 1) * 512], onescol[:],
                       sqs[:, sl + qc * 512:sl + (qc + 1) * 512],
                       start=(c == 0), stop=(c == DC - 1)),
              "cSQMM" if qc == 3 else None)
    I("tensor", lambda e: e.nop(), "cSTATS")
    W("vector", "cSTATS", 1)
    I("vector", lambda e: e.tensor_scalar(m_bf[:], psSt[0][:], 1.0 / D,
                                          None, ALU.mult))
    I("vector", lambda e: e.tensor_mul(sqs[0:1, 0:SL], m_bf[:], m_bf[:]))
    I("vector", lambda e: e.scalar_tensor_tensor(psSt[0][:], psSt[1][:],
                                                 1.0 / D, sqs[0:1, 0:SL], ALU.mult,
                                                 ALU.subtract), "cNAR")
    W("scalar", "cCONST", nconst)
    W("scalar", "cNAR", 1)
    I("scalar", lambda e: e.activation(invr_bf[:], psSt[0][:], ACT.Sqrt,
                                       bias=consts[0:1, 1:2], scale=1.0), "cNAR")
    W("vector", "cNAR", 2)
    def _recst_cNAR(e):
        with nc.allow_low_precision("LN rstd in bf16; 0.4% relative scale "
                                    "noise is far inside the native gate"):
            return e.reciprocal(rstd_bf[:], invr_bf[:])
    I("vector", _recst_cNAR, "cNAR")
    ph1b.close()

    ph1c = ExitStack()
    psB = ph1c.enter_context(nc.psum_tensor("psB", [128, SL], F32))
    rbc1 = ExitStack()
    rbc1_t = rbc1.enter_context(nc.sbuf_tensor("rbc1", [128, SL], BF16, side="right"))
    W("tensor", "cNAR", 3)
    for qc in range(4):
        I("tensor", lambda e, qc=qc:
          e.matmul(psB[:, qc * 512:(qc + 1) * 512], onesrow[0:1, 0:128],
                   rstd_bf[0:1, qc * 512:(qc + 1) * 512], start=True, stop=True),
          "cBC")
    W("vector", "cBC", 4)
    I("vector", lambda e: e.tensor_copy(rbc1_t[:], psB[:]), "cBCE")
    ph1c.close()

    scC = ExitStack()
    qk_fm = scC.enter_context(nc.sbuf_tensor("qk_fm", [128, 16 * SL], BF16))
    scV = ExitStack()
    v_fm = scV.enter_context(nc.sbuf_tensor("v_fm", [128, 8 * SL], BF16))
    for j in range(24):
        emit_wqkv_piece(j)
    nwq = sems["cWCAST"].n

    for p3 in range(3):
        stream_w(brow[0:1, p3 * 1024:(p3 + 1) * 1024], 1024,
                 lambda e, sl, p3=p3: e.tensor_copy(
                     r1b[0:1, p3 * 1024:(p3 + 1) * 1024],
                     wring[0:1, sl:sl + 1024]), rows_n=1)
    stream_w(brow[1:2, 0:1024], 1024,
             lambda e, sl: e.tensor_copy(borow_bf[0:1, 0:1024],
                                         wring[0:1, sl:sl + 1024]), rows_n=1)

    ph2cs = ExitStack()
    psCS = ph2cs.enter_context(nc.psum_tensor("psCS", [1, 512], F32))
    W("tensor", "cWCAST", nwq)
    for piece in range(6):
        if piece >= 1:
            W("tensor", "cCSV", piece)
        for kc in range(DC):
            I("tensor", lambda e, piece=piece, kc=kc:
              e.matmul(psCS[:], onescol[:],
                       wqkv_bf[:, kc * 3 * D + piece * 512:
                               kc * 3 * D + (piece + 1) * 512],
                       start=(kc == 0), stop=(kc == DC - 1)),
              "cCST" if kc == DC - 1 else None)
        W("vector", "cCST", piece + 1)
        I("vector", lambda e, piece=piece:
          e.tensor_scalar(r1neg[0:1, piece * 512:(piece + 1) * 512], psCS[:],
                          -1.0, None, ALU.mult), "cCSV")
    I("vector", lambda e: e.tensor_copy(r1neg[0:1, 0:1], r1neg[0:1, 0:1]), "cCSV")
    ncsv = sems["cCSV"].n
    ph2cs.close()

    ph2mm = ExitStack()
    psQK = [ph2mm.enter_context(nc.psum_tensor(f"psQK{j}", [128, 512], F32))
            for j in range(4)]
    W("tensor", "cCSV", ncsv)
    W("tensor", "cBCE", 1)
    g = 0
    for ec in range(EC):
        for tq in range(4):
            bank = g % 4
            if g >= 4:
                W("tensor", "cQKEV", g - 3)
            for kc in range(DC):
                I("tensor", lambda e, bank=bank, ec=ec, tq=tq, kc=kc:
                  e.matmul(psQK[bank][:],
                           wqkv_bf[:, kc * 3 * D + ec * 128:
                                   kc * 3 * D + (ec + 1) * 128],
                           x_fm[:, kc * SL + tq * 512:kc * SL + (tq + 1) * 512],
                           start=(kc == 0), stop=False))
            I("tensor", lambda e, bank=bank, ec=ec, tq=tq:
              e.matmul(psQK[bank][:], r1neg[0:1, ec * 128:(ec + 1) * 128],
                       m_bf[0:1, tq * 512:(tq + 1) * 512],
                       start=False, stop=False))
            I("tensor", lambda e, bank=bank, ec=ec, tq=tq:
              e.matmul(psQK[bank][:], r1b[0:1, ec * 128:(ec + 1) * 128],
                       invr_bf[0:1, tq * 512:(tq + 1) * 512],
                       start=False, stop=True), "cQKMM")
            W("vector", "cQKMM", g + 1)
            if ec < 16:
                dstap = lambda ec=ec, tq=tq: qk_fm[:, ec * SL + tq * 512:
                                                   ec * SL + (tq + 1) * 512]
            else:
                dstap = lambda ec=ec, tq=tq: v_fm[:, (ec - 16) * SL + tq * 512:
                                                  (ec - 16) * SL + (tq + 1) * 512]
            I("vector", lambda e, bank=bank, d=dstap, tq=tq:
              e.tensor_mul(d(), psQK[bank][:],
                           rbc1_t[:, tq * 512:(tq + 1) * 512]), "cQKEV")
            g += 1
    nqkev = sems["cQKEV"].n
    if debug:
        W("sync", "cQKEV", nqkev)
        dma("sync", dbg["d_xfm_m"], x_fm[:], "cDBG")
    ph2mm.close()
    rbc1.close()
    scB.close()

    scD = ExitStack()
    v_aug = scD.enter_context(nc.sbuf_tensor("v_aug", [128, 16 * 1040], BF16, side="right"))
    pt_c = scD.enter_context(nc.sbuf_tensor("pt_c", [128, 3 * 1024], BF16,
                                            side="right"))
    ph2tr = ExitStack()
    psTrV = [ph2tr.enter_context(nc.psum_tensor(f"psTrV{j}", [128, 128], BF16))
             for j in range(2)]
    W("vector", "cQKEV", 96)
    for tt in range(TT):
        I("vector", lambda e, tt=tt: e.memset(
            v_aug[:, tt * 1040:(tt + 1) * 1040]
            .rearrange("p (h c) -> p h c", h=16, c=65)[:, :, 64:65], 1.0))
    for c in range(8):
        ecv = 16 + c
        W("tensor", "cQKEV", (ecv + 1) * 4)
        for tt in range(TT):
            m = c * TT + tt
            if m >= 2:
                W("tensor", "cVTRE", m - 1)
            I("tensor", lambda e, m=m, c=c, tt=tt:
              e.transpose(psTrV[m % 2][:],
                          v_fm[:, c * SL + tt * 128:c * SL + (tt + 1) * 128],
                          ident[:]), "cVTR")
            W("vector", "cVTR", m + 1)
            I("vector", lambda e, m=m, c=c, tt=tt:
              e.tensor_copy(
                  v_aug[:, tt * 1040 + (2 * c) * 65:
                        tt * 1040 + (2 * c) * 65 + 130]
                  .rearrange("p (h c) -> p h c", h=2, c=65)[:, :, 0:64],
                  psTrV[m % 2][:].rearrange("p (h c) -> p h c", h=2, c=64)),
              "cVTRE")
    nvtre = sems["cVTRE"].n
    if debug:
        W("sync", "cVTRE", nvtre)
        dma("sync", dbg["d_xfm_v"], x_fm[:], "cDBG")
    ph2tr.close()
    scV.close()

    ph3 = ExitStack()
    psS = [ph3.enter_context(nc.psum_tensor(f"psS{j}", [128, 1024], F32))
           for j in range(2)]
    psO = [ph3.enter_context(nc.psum_tensor(f"psO{j}", [128, 1024], F32))
           for j in range(2)]
    scE = ExitStack()
    attn_fm = scE.enter_context(nc.sbuf_tensor("attn_fm", [128, 8 * SL], BF16))
    scE2 = ExitStack()
    sums_bf = scE2.enter_context(nc.sbuf_tensor("sums_bf", [16, SL], BF16))
    inv_bf = scE2.enter_context(nc.sbuf_tensor("inv_bf", [16, SL], BF16))
    invb = scE2.enter_context(nc.sbuf_tensor("invb", [128, SL], BF16))
    invrow = [scE2.enter_context(nc.sbuf_tensor("invrow0", [1, SL], BF16))]
    srow2 = [scE2.enter_context(nc.sbuf_tensor(f"srow{s}", [1, SL], BF16))
             for s in range(2)]

    W("tensor", "cVTRE", nvtre)
    W("scalar", "cVTRE", nvtre)
    CH = 8 * 2 * KT * 2

    def _cqk(s):
        c, r = divmod(s, 2 * KT)
        qh, kt = divmod(r, KT)
        return c, qh, kt

    def emit_qk(k):
        s, qq = k // 2, k % 2
        c, qh, kt = _cqk(s)
        if k >= 2:
            W("tensor", "cEXP", k - 1)
        qcol = c * SL + qh * 1024 + qq * 512
        kcol = (8 + c) * SL + kt * 128
        I("tensor", lambda e, k=k, qcol=qcol, kcol=kcol:
          e.matmul(psS[k % 2][:, 0:512],
                   qk_fm[0:64, kcol:kcol + 128],
                   qk_fm[0:64, qcol:qcol + 512],
                   start=True, stop=True, tile_position=(0, 0)))
        I("tensor", lambda e, k=k, qcol=qcol, kcol=kcol:
          e.matmul(psS[k % 2][:, 512:1024],
                   qk_fm[64:128, kcol:kcol + 128],
                   qk_fm[64:128, qcol:qcol + 512],
                   start=True, stop=True, tile_position=(64, 0)), "cQKA")

    def emit_exp(k):
        W("scalar", "cQKA", k + 1)
        if k >= 3:
            W("scalar", "cPV", 2 * (k - 2))
        I("scalar", lambda e, k=k:
          e.activation(pt_c[:, (k % 3) * 1024:(k % 3) * 1024 + 1024],
                       psS[k % 2][:], ACT.Exp, bias=consts[:, 0:1],
                       scale=0.125), "cEXP")

    def emit_pv(k):
        s, qq = k // 2, k % 2
        c, qh, kt = _cqk(s)
        m = 2 * c + qh
        W("tensor", "cEXP", k + 1)
        for sub in range(2):
            if kt == 0 and qq == 0 and m >= 1:
                W("tensor", "cOEV", 4 * (m - 1) + 2 * (sub + 1))
            I("tensor", lambda e, sub=sub, qq=qq, kt=kt, c=c, k=k:
              e.matmul(psO[sub][0:65, qq * 512:(qq + 1) * 512],
                       v_aug[:, kt * 1040 + (2 * c + sub) * 65:
                             kt * 1040 + (2 * c + sub) * 65 + 65],
                       pt_c[:, (k % 3) * 1024 + sub * 512:
                             (k % 3) * 1024 + (sub + 1) * 512],
                       start=(kt == 0), stop=(kt == KT - 1)), "cPV")

    def emit_evac(k):
        s = k // 2
        c, qh, kt = _cqk(s)
        m = 2 * c + qh
        W("vector", "cPV", 2 * (k + 1))
        if qh == 0 and c >= 1:
            W("vector", "cSUM", 32 * c)
        for sub in range(2):
            I("vector", lambda e, sub=sub, c=c, qh=qh:
              e.tensor_copy(attn_fm[sub * 64:sub * 64 + 64,
                                    c * SL + qh * 1024:c * SL + qh * 1024 + 1024],
                            psO[sub][0:64, :]), "cOEV")
            I("vector", lambda e, sub=sub, qh=qh:
              e.tensor_copy(srow2[sub][0:1, qh * 1024:qh * 1024 + 1024],
                            psO[sub][64:65, :]), "cOEV")
        if qh == 1:
            W("sync", "cOEV", 4 * (m + 1))
            for sub in range(2):
                dma("sync", hsum[2 * c + sub:2 * c + sub + 1, :],
                    srow2[sub][:], "cSUM")

    emit_qk(0)
    for k in range(CH):
        emit_exp(k)
        if k + 1 < CH:
            emit_qk(k + 1)
        if k >= 2:
            emit_pv(k - 2)
            if (k - 2) % 2 == 1 and _cqk((k - 2) // 2)[2] == KT - 1:
                emit_evac(k - 2)
    for k in (CH - 2, CH - 1):
        emit_pv(k)
        if k % 2 == 1 and _cqk(k // 2)[2] == KT - 1:
            emit_evac(k)
    W("sync", "cSUM", sems["cSUM"].n)
    W("sync", "cSUMB", sems["cSUMB"].n)
    dma("sync", sums_bf[:], hsum, "cSUM")
    W("vector", "cSUM", sems["cSUM"].n)
    def _recip_lp(e):
        with nc.allow_low_precision("softmax denominator reciprocal in bf16; "
                                    "0.4% relative is far inside native tol"):
            return e.reciprocal(inv_bf[:], sums_bf[:])
    I("vector", _recip_lp, "cINV")
    W("sync", "cINV", 1)
    dma("sync", hinv, inv_bf[:], "cHIV")
    for hp in range(HEADS // 2):
        for sub in range(2):
            h = 2 * hp + sub
            ib = h
            if ib >= 1:
                W("sync", "cBCH", ib)
            W("sync", "cHIV", 16)
            dma("sync", invrow[0][:], hinv[h:h + 1, :], "cIRW")
            W("tensor", "cIRW", sems["cIRW"].n)
            W("tensor", "cPV", sems["cPV"].n)
            if hp >= 1:
                W("tensor", "cIBC", hp)
            for qq in range(4):
                I("tensor", lambda e, sub=sub, qq=qq, ib=ib:
                  e.matmul(psS[qq // 2][sub * 64:sub * 64 + 64,
                                        (qq % 2) * 512:(qq % 2) * 512 + 512],
                           onesrow[0:1, 0:64],
                           invrow[0][0:1, qq * 512:(qq + 1) * 512],
                           start=True, stop=True), "cBCH" if qq == 3 else None)
        W("vector", "cBCH", 2 * hp + 2)
        I("vector", lambda e: e.tensor_copy(invb[:, 0:1024], psS[0][:]))
        I("vector", lambda e: e.tensor_copy(invb[:, 1024:2048], psS[1][:]), "cIBC")
        I("vector", lambda e, hp=hp:
          e.tensor_mul(attn_fm[:, hp * SL:(hp + 1) * SL],
                       attn_fm[:, hp * SL:(hp + 1) * SL],
                       invb[:]), "cNRM")
    nnrm = sems["cNRM"].n
    if debug:
        W("sync", "cNRM", nnrm)
        dma("sync", dbg["d_xfm_n"], x_fm[:], "cDBG")
    ph3.close()
    scD.close()
    scE2.close()

    scG = ExitStack()
    x2_fm = scG.enter_context(nc.sbuf_tensor("x2_fm", [128, DC * SL], BF16, side="right"))
    scF = ExitStack()
    wout_bf = scF.enter_context(nc.sbuf_tensor("wout_bf", [128, DC * D], BF16, side="right"))

    for kc in range(DC):
        stream_w(wout[kc * 128:(kc + 1) * 128, :], 1024,
                 lambda e, sl, kc=kc: e.activation(
                     wout_bf[:, kc * D:(kc + 1) * D], wring[:, sl:sl + 1024],
                     ACT.Identity, bias=0.0, scale=1.0), eng="scalar")
    nwo = sems["cWCAST"].n

    ph4 = ExitStack()
    psP = [ph4.enter_context(nc.psum_tensor(f"psP{j}", [128, 512], F32))
           for j in range(4)]
    W("tensor", "cWCAST", nwo)
    W("tensor", "cNRM", nnrm)
    g = 0
    for oc in range(DC):
        for tq in range(4):
            bank = g % 4
            if g >= 4:
                W("tensor", "cX2", g - 3)
            for kc in range(DC):
                I("tensor", lambda e, bank=bank, oc=oc, tq=tq, kc=kc:
                  e.matmul(psP[bank][:],
                           wout_bf[:, kc * D + oc * 128:kc * D + (oc + 1) * 128],
                           attn_fm[:, kc * SL + tq * 512:kc * SL + (tq + 1) * 512],
                           start=(kc == 0), stop=False))
            I("tensor", lambda e, bank=bank, oc=oc, tq=tq:
              e.matmul(psP[bank][:], borow_bf[0:1, oc * 128:(oc + 1) * 128],
                       onesrow[0:1, 0:512],
                       start=False, stop=True), "cPMM")
            W("vector", "cPMM", g + 1)
            I("vector", lambda e, bank=bank, oc=oc, tq=tq:
              e.tensor_tensor(
                  x2_fm[:, oc * SL + tq * 512:oc * SL + (tq + 1) * 512],
                  psP[bank][:],
                  x_fm[:, oc * SL + tq * 512:oc * SL + (tq + 1) * 512],
                  ALU.add), "cX2")
            g += 1
    nx2 = sems["cX2"].n
    ph4.close()
    scE.close()
    scC.close()
    scX.close()
    scF.close()

    if debug:
        W("sync", "cX2", nx2)
    scI = ExitStack()
    hq8 = scI.enter_context(nc.sbuf_tensor("hq8", [128, NC1 * 512], FP8))
    otm = scI.enter_context(nc.sbuf_tensor("otm", [128, 4 * D], F32))
    w1q8 = scI.enter_context(nc.sbuf_tensor("w1q8", [128, DC * 4096], FP8))
    identf = scI.enter_context(nc.sbuf_tensor("identf", [128, 128], F32))
    w2q8 = scI.enter_context(nc.sbuf_tensor("w2q8", [128, NC1 * 1024], FP8))
    hbf = scI.enter_context(nc.sbuf_tensor("hbf", [128, 2 * 512], BF16))
    y2bf = scI.enter_context(nc.sbuf_tensor("y2bf", [128, 3 * 512], BF16))
    t1s = scI.enter_context(nc.sbuf_tensor("t1s", [128, 512], F32))
    ofm = scI.enter_context(nc.sbuf_tensor("ofm", [128, 3 * 512], F32))
    mring = scI.enter_context(nc.sbuf_tensor("mring", [128, 3 * 512], BF16))



    W("sync", "cQKA", sems["cQKA"].n)
    dma("sync", identf[:], idf32, "cIDF")
    scH = ExitStack()
    sqs = scH.enter_context(nc.sbuf_tensor("sqs2", [128, 2 * SL], BF16))
    m2bc = scH.enter_context(nc.sbuf_tensor("m2bc", [128, SL], BF16))
    r2bc = scH.enter_context(nc.sbuf_tensor("r2bc", [128, SL], BF16))
    xh2q8 = scH.enter_context(nc.sbuf_tensor("xh2q8", [128, DC * SL], FP8))
    ph4b = ExitStack()
    psSt2 = [ph4b.enter_context(nc.psum_tensor(f"psSt2{j}", [1, SL], F32))
             for j in range(2)]
    W("tensor", "cX2", nx2)
    W("vector", "cX2", nx2)
    for qc in range(4):
        for c in range(DC):
            I("tensor", lambda e, qc=qc, c=c:
              e.matmul(psSt2[0][:, qc * 512:(qc + 1) * 512], onescol[:],
                       x2_fm[:, c * SL + qc * 512:c * SL + (qc + 1) * 512],
                       start=(c == 0), stop=(c == DC - 1)))
    for c in range(DC):
        if c >= 2:
            W("vector", "cSQMM2", c - 1)
        I("vector", lambda e, c=c, sl=(c % 2) * SL:
          e.tensor_mul(sqs[:, sl:sl + SL], x2_fm[:, c * SL:(c + 1) * SL],
                       x2_fm[:, c * SL:(c + 1) * SL]), "cSQV2")
        W("tensor", "cSQV2", c + 1)
        for qc in range(4):
            I("tensor", lambda e, qc=qc, c=c, sl=(c % 2) * SL:
              e.matmul(psSt2[1][:, qc * 512:(qc + 1) * 512], onescol[:],
                       sqs[:, sl + qc * 512:sl + (qc + 1) * 512],
                       start=(c == 0), stop=(c == DC - 1)),
              "cSQMM2" if qc == 3 else None)
    I("tensor", lambda e: e.nop(), "cSTATS2")
    W("vector", "cSTATS2", 1)
    I("vector", lambda e: e.tensor_scalar(m_bf[:], psSt2[0][:], 1.0 / D,
                                          None, ALU.mult))
    I("vector", lambda e: e.tensor_mul(sqs[0:1, 0:SL], m_bf[:], m_bf[:]))
    I("vector", lambda e: e.scalar_tensor_tensor(psSt2[0][:], psSt2[1][:],
                                                 1.0 / D, sqs[0:1, 0:SL], ALU.mult,
                                                 ALU.subtract), "cNAR2")
    W("scalar", "cCONST", nconst)
    W("scalar", "cNAR2", 1)
    I("scalar", lambda e: e.activation(invr_bf[:], psSt2[0][:], ACT.Sqrt,
                                       bias=consts[0:1, 1:2], scale=1.0), "cNAR2")
    W("vector", "cNAR2", 2)
    def _recst_cNAR2(e):
        with nc.allow_low_precision("LN rstd in bf16; 0.4% relative scale "
                                    "noise is far inside the native gate"):
            return e.reciprocal(rstd_bf[:], invr_bf[:])
    I("vector", _recst_cNAR2, "cNAR2")
    ph4b.close()

    W("scalar", "cQKA", sems["cQKA"].n)
    w1base = sems["cWCAST"].n
    for q4 in range(4):
        for kc in range(DC):
            stream_w(w1t[(q4 * 8 + kc) * 128:(q4 * 8 + kc + 1) * 128, :],
                     1024,
                     lambda e, sl, kc=kc, q4=q4: e.activation(
                         w1q8[:, kc * 4096 + q4 * 1024:
                              kc * 4096 + (q4 + 1) * 1024],
                         wring[:, sl:sl + 1024], ACT.Identity,
                         bias=0.0, scale=1.0), eng="scalar")
    nw1 = sems["cWCAST"].n
    w2base = ring_i
    w2win = []
    for k in range(NC1):
        i = ring_i
        ring_i += 1
        sl = (i % 3) * 1024
        sw = WSLOT[i % 3]
        j = i - 3
        if j >= 0:
            if j < w2base:
                W("sync", "cWCAST", j + 1)
            else:
                W("sync", "cW2V", j + 1 - w2base)
        dma("sync", wring[:, sl:sl + 1024],
            w2t[k * 128:(k + 1) * 128, 0:1024], sw)
        w2win.append((sw, sems[sw].n))

    ph4c = ExitStack()
    psB2 = ph4c.enter_context(nc.psum_tensor("psB2", [128, SL], F32))
    W("tensor", "cNAR2", 3)
    for qc in range(4):
        I("tensor", lambda e, qc=qc:
          e.matmul(psB2[:, qc * 512:(qc + 1) * 512], onesrow[0:1, 0:128],
                   m_bf[0:1, qc * 512:(qc + 1) * 512], start=True, stop=True),
          "cBC2" if qc == 3 else None)
    W("vector", "cBC2", 1)
    I("vector", lambda e: e.tensor_copy(m2bc[:], psB2[:]), "cBC2E")
    W("tensor", "cBC2E", 1)
    for qc in range(4):
        I("tensor", lambda e, qc=qc:
          e.matmul(psB2[:, qc * 512:(qc + 1) * 512], onesrow[0:1, 0:128],
                   rstd_bf[0:1, qc * 512:(qc + 1) * 512], start=True, stop=True),
          "cBC2" if qc == 3 else None)
    W("vector", "cBC2", 2)
    I("vector", lambda e: e.tensor_copy(r2bc[:], psB2[:]), "cBC2E")
    ph4c.close()

    for c in range(DC):
        sl = (c % 2) * SL
        I("vector", lambda e, c=c, sl=sl:
          e.tensor_sub(sqs[:, sl:sl + SL], x2_fm[:, c * SL:(c + 1) * SL],
                       m2bc[:]))
        I("vector", lambda e, c=c, sl=sl:
          e.tensor_mul(xh2q8[:, c * SL:(c + 1) * SL], sqs[:, sl:sl + SL],
                       r2bc[:]), "cXH2")
    nxh2 = sems["cXH2"].n

    ph5 = ExitStack()
    psF = [ph5.enter_context(nc.psum_tensor(f"psF{j}", [128, 512], F32))
           for j in range(3)]
    psG = [ph5.enter_context(nc.psum_tensor(f"psG{j}", [128, 512], F32))
           for j in range(3)]
    psTrO = [ph5.enter_context(nc.psum_tensor(f"psTrO{j}", [128, 128], F32))
             for j in range(2)]

    W("tensor", "cXH2", nxh2)
    gf = 0
    gg = 0
    tro = 0
    pend = []

    def emit_oasm(p):
        nonlocal tro
        ts_p, oc_p = divmod(p, DC)
        W("tensor", "cIDF", 16)
        W("tensor", "cOASM", p + 1)
        for j in range(4):
            tt = ts_p * 4 + j
            if tro >= 2:
                W("tensor", "cOCP", tro - 1)
            I("tensor", lambda e, tro=tro, j=j, p=p:
              e.transpose(psTrO[tro % 2][:],
                          ofm[:, (p % 3) * 512 + j * 128:
                              (p % 3) * 512 + (j + 1) * 128],
                          identf[:]), "cOTR")
            W("vector", "cOTR", tro + 1)
            if oc_p == 0 and tt >= 4:
                so = ("cOUT", "cOUT1", "cOUT2", "cOUT3")[tt % 4]
                W("vector", so, 16 * ts_p)
            I("vector", lambda e, tro=tro, tt=tt, oc_p=oc_p:
              e.tensor_copy(otm[:, (tt % 4) * D + oc_p * 128:
                            (tt % 4) * D + (oc_p + 1) * 128],
                            psTrO[tro % 2][:]), "cOCP")
            tro += 1
        if oc_p == DC - 1:
            for j in range(4):
                tt = ts_p * 4 + j
                W("sync", "cOCP", (ts_p * DC + 7) * 4 + j + 1)
                dma("sync", yout[tt * 128:(tt + 1) * 128, :],
                    otm[:, (tt % 4) * D:(tt % 4) * D + D],
                    ("cOUT", "cOUT1", "cOUT2", "cOUT3")[tt % 4])

    for ts in range(TSN):
        for ncc in range(NC1):
            bank = gf % 3
            if ts == 0:
                W("tensor", "cWCAST", w1base + 8 * (ncc // 8 + 1))
            if gf >= 3:
                W("tensor", "cFEV", gf - 2)
            for kc in range(DC):
                I("tensor", lambda e, bank=bank, ncc=ncc, ts=ts, kc=kc:
                  e.matmul(psF[bank][:],
                           w1q8[:, kc * 4096 + ncc * 128:kc * 4096 + (ncc + 1) * 128],
                           xh2q8[:, kc * SL + ts * 512:kc * SL + (ts + 1) * 512],
                           start=(kc == 0), stop=(kc == DC - 1)), "cFMM" if kc == DC - 1 else None)
            W("scalar", "cBQ", 1)
            W("scalar", "cFMM", gf + 1)
            if gf >= 2:
                W("scalar", "cHQ", gf - 1)
            hs = gf % 2
            I("scalar", lambda e, bank=bank, hs=hs, ncc=ncc:
              e.activation(hbf[:, hs * 512:(hs + 1) * 512], psF[bank][:],
                           ACT.Relu, bias=b1q[:, ncc:ncc + 1], scale=1.0),
              "cFEV")
            W("vector", "cFEV", gf + 1)
            if ts >= 1:
                W("vector", "cGMM", ts * 8)
            I("vector", lambda e, hs=hs, ncc=ncc:
              e.tensor_copy(hq8[:, ncc * 512:(ncc + 1) * 512],
                            hbf[:, hs * 512:(hs + 1) * 512]), "cHQ")
            if ts == 0:
                sl2 = ((w2base + ncc) % 3) * 1024
                W("vector", w2win[ncc][0], w2win[ncc][1])
                I("vector", lambda e, ncc=ncc, sl2=sl2:
                  e.tensor_copy(w2q8[:, ncc * 1024:(ncc + 1) * 1024],
                                wring[:, sl2:sl2 + 1024]), "cW2V")
            gf += 1
        for oc in range(DC):
            gb = gg % 3
            if gg >= 3:
                W("sync", "cMUSE", gg - 2)
            sm = ("cMIN", "cMIN1", "cMIN2")[gg % 3]
            dma("sync", mring[:, (gg % 3) * 512:(gg % 3 + 1) * 512],
                maskfm[oc * 128:(oc + 1) * 128, ts * 512:(ts + 1) * 512], sm)
            nmin = (sm, sems[sm].n)
            W("tensor", "cHQ", (ts + 1) * NC1)
            if gg == 0:
                W("tensor", "cW2V", NC1)
            if gg >= 3:
                W("tensor", "cGEV", gg - 2)
            for ncc in range(NC1):
                I("tensor", lambda e, gb=gb, ncc=ncc, oc=oc:
                  e.matmul(psG[gb][:],
                           w2q8[:, ncc * 1024 + oc * 128:
                                ncc * 1024 + (oc + 1) * 128],
                           hq8[:, ncc * 512:(ncc + 1) * 512],
                           start=(ncc == 0), stop=(ncc == NC1 - 1)),
                  "cGMM" if ncc == NC1 - 1 else None)
            W("scalar", "cGMM", gg + 1)
            if gg >= 3:
                W("scalar", "cMUSE", gg - 2)
            I("scalar", lambda e, gb=gb, gg=gg, oc=oc:
              e.activation(y2bf[:, (gg % 3) * 512:(gg % 3 + 1) * 512],
                           psG[gb][:], ACT.Identity,
                           bias=b2q[:, oc:oc + 1], scale=1.0), "cGEV")
            W("vector", "cGEV", gg + 1)
            W("vector", nmin[0], nmin[1])
            I("vector", lambda e, gg=gg:
              e.tensor_mul(t1s[:], y2bf[:, (gg % 3) * 512:(gg % 3 + 1) * 512],
                           mring[:, (gg % 3) * 512:(gg % 3 + 1) * 512]), "cMUSE")
            if gg >= 3:
                W("vector", "cOTR", 4 * (gg - 2))
            I("vector", lambda e, gg=gg, oc=oc, ts=ts:
              e.tensor_tensor(ofm[:, (gg % 3) * 512:(gg % 3 + 1) * 512],
                              t1s[:],
                              x2_fm[:, oc * SL + ts * 512:oc * SL + (ts + 1) * 512],
                              ALU.add), "cOASM")
            pend.append(gg)
            if len(pend) > 2:
                emit_oasm(pend.pop(0))
            gg += 1
    while pend:
        emit_oasm(pend.pop(0))
    ph5.close()
    scH.close()
    scI.close()
    scG.close()

    if debug:
        W("sync", "cOUT", sems["cOUT"].n)
        W("sync", "cDBG", sems["cDBG"].n)
        dma("sync", dbg["d_xfm"], x_fm[:], "cOUT")
        dma("sync", dbg["d_x2fm"], x2_fm[:], "cOUT")
        dma("sync", dbg["d_hq"], hq8[:].bitcast(mybir.dt.uint8), "cOUT")
        dma("sync", dbg["d_xh2"], xh2q8[:].bitcast(mybir.dt.uint8), "cOUT")
        dma("sync", dbg["d_w1q"], w1q8[:].bitcast(mybir.dt.uint8), "cOUT")
        dma("sync", dbg["d_mbf"], m_bf[:], "cOUT")
        dma("sync", dbg["d_b1q"], b1q[:], "cOUT")
        dma("sync", dbg["d_b1c"], b1csb[:], "cOUT")
        dma("sync", dbg["d_rstd"], rstd_bf[:], "cOUT")
    for so in ("cOUT", "cOUT1", "cOUT2", "cOUT3"):
        W("gpsimd", so, sems[so].n)

    with nc.Block() as blk:
        @blk.sync
        def _(e):
            for fn in prog["sync"]:
                fn(e)

        @blk.scalar
        def _(e):
            for fn in prog["scalar"]:
                fn(e)

        @blk.vector
        def _(e):
            for fn in prog["vector"]:
                fn(e)

        @blk.tensor
        def _(e):
            for fn in prog["tensor"]:
                fn(e)

        @blk.gpsimd
        def _(e):
            for fn in prog["gpsimd"]:
                fn(e)

    perm.close()
    return nc


_NC_CACHE = None


def kernel(L0_attn_x, L0_attn_W_qkv, L0_attn_b_qkv, L0_attn_W_out,
           L0_attn_b_out, L0_ffn_W1, L0_ffn_b1, L0_ffn_W2, L0_ffn_b2):
    global _NC_CACHE
    x = np.asarray(L0_attn_x, dtype=np.float32)
    B = x.shape[0]
    keep = _dropout_keep_mask((B, SL, D))
    maskf = np.where(keep, np.float32(1.25), np.float32(0.0))

    wqkv = np.ascontiguousarray(
        np.asarray(L0_attn_W_qkv, dtype=np.float32)
        .reshape(8, 128, 3, 1024).transpose(0, 2, 1, 3).reshape(24 * 128, 1024))
    wout = np.ascontiguousarray(np.asarray(L0_attn_W_out, dtype=np.float32))
    w1t = np.ascontiguousarray(
        np.asarray(L0_ffn_W1, dtype=np.float32).T
        .reshape(8, 128, 4, 1024).transpose(2, 0, 1, 3).reshape(32 * 128, 1024))
    w2t = np.ascontiguousarray(np.asarray(L0_ffn_W2, dtype=np.float32).T)
    bq = np.asarray(L0_attn_b_qkv, dtype=np.float32)
    bo = np.asarray(L0_attn_b_out, dtype=np.float32)
    b1 = np.asarray(L0_ffn_b1, dtype=np.float32)
    b2 = np.asarray(L0_ffn_b2, dtype=np.float32)

    brow = np.zeros((2, 3 * D), dtype=np.float32)
    brow[0, :] = bq
    brow[1, :D] = bo
    b1c = np.ascontiguousarray(b1.reshape(32, 128).T)
    b2c = np.ascontiguousarray(b2.reshape(8, 128).T)
    idbf = np.eye(128, dtype=ml_dtypes.bfloat16)
    idf32 = np.eye(128, dtype=np.float32)

    if _NC_CACHE is None:
        _NC_CACHE = build_nc()
    nc = _NC_CACHE

    in_maps = []
    for b in range(B):
        in_maps.append({
            "xin": np.ascontiguousarray(x[b]),
            "wqkv": wqkv, "wout": wout, "w1t": w1t, "w2t": w2t,
            "brow": brow, "b1c": b1c, "b2c": b2c,
            "idbf": idbf, "idf32": idf32,
            "maskfm": np.ascontiguousarray(
                maskf[b].T.astype(ml_dtypes.bfloat16)),
        })
    res = run_bass_kernel_spmd(nc, in_maps, core_ids=list(range(B)))
    out = np.stack([np.asarray(r["yout"]) for r in res.results], axis=0)
    return out.astype(np.float32)


# version 22
# speedup vs baseline: 1.0265x; 1.0012x over previous
import numpy as np
import ml_dtypes
import concourse.bass as bass
import concourse.mybir as mybir
from concourse.bass_utils import run_bass_kernel_spmd
from contextlib import ExitStack

F32 = mybir.dt.float32
BF16 = mybir.dt.bfloat16
FP8 = mybir.dt.float8e5
ALU = mybir.AluOpType
ACT = mybir.ActivationFunctionType

SL = 2048
D = 1024
HEADS = 16
TT = 16
DC = 8
EC = 24
KT = 16
NC1 = 32
TSN = 4
EXP_SHIFT = -12.0


def _threefry2x32(k0, k1, x0, x1):
    rot = [13, 15, 26, 6, 17, 29, 16, 24]
    ks0, ks1 = np.uint32(k0), np.uint32(k1)
    ks2 = np.uint32(np.uint32(ks0) ^ np.uint32(ks1) ^ np.uint32(0x1BD11BDA))
    x0 = (x0 + ks0).astype(np.uint32)
    x1 = (x1 + ks1).astype(np.uint32)

    def rotl(v, d):
        return ((v << np.uint32(d)) | (v >> np.uint32(32 - d))).astype(np.uint32)

    ks = [ks0, ks1, ks2]
    for i in range(5):
        r = rot[:4] if i % 2 == 0 else rot[4:]
        for j in range(4):
            x0 = (x0 + x1).astype(np.uint32)
            x1 = rotl(x1, r[j])
            x1 = (x1 ^ x0).astype(np.uint32)
        x0 = (x0 + ks[(i + 1) % 3]).astype(np.uint32)
        x1 = (x1 + ks[(i + 2) % 3] + np.uint32(i + 1)).astype(np.uint32)
    return x0, x1


def _dropout_keep_mask(shape, p_keep=0.8, key=(0, 42)):
    n = int(np.prod(shape))
    idx = np.arange(n, dtype=np.uint64)
    hi = (idx >> np.uint64(32)).astype(np.uint32)
    lo = (idx & np.uint64(0xFFFFFFFF)).astype(np.uint32)
    b0, b1 = _threefry2x32(key[0], key[1], hi, lo)
    bits = (b0 ^ b1).astype(np.uint32)
    fl = ((bits >> np.uint32(9)) | np.uint32(0x3F800000)).view(np.float32) \
        - np.float32(1.0)
    return (fl < np.float32(p_keep)).reshape(shape)


def build_nc(debug=False):
    nc = bass.Bass("TRN2", target_bir_lowering=False,
                   dynamic_dma_scratch_size=512)

    xin = nc.dram_tensor("xin", [SL, D], F32, kind="ExternalInput").ap()
    wqkv = nc.dram_tensor("wqkv", [24 * 128, 1024], F32, kind="ExternalInput").ap()
    wout = nc.dram_tensor("wout", [D, D], F32, kind="ExternalInput").ap()
    w1t = nc.dram_tensor("w1t", [32 * 128, 1024], F32, kind="ExternalInput").ap()
    w2t = nc.dram_tensor("w2t", [4 * D, D], F32, kind="ExternalInput").ap()
    brow = nc.dram_tensor("brow", [2, 3 * D], F32, kind="ExternalInput").ap()
    b1c = nc.dram_tensor("b1c", [128, 32], F32, kind="ExternalInput").ap()
    b2c = nc.dram_tensor("b2c", [128, 8], F32, kind="ExternalInput").ap()
    idbf = nc.dram_tensor("idbf", [128, 128], BF16, kind="ExternalInput").ap()
    idf32 = nc.dram_tensor("idf32", [128, 128], F32, kind="ExternalInput").ap()
    maskfm = nc.dram_tensor("maskfm", [D, SL], BF16, kind="ExternalInput").ap()
    yout = nc.dram_tensor("yout", [SL, D], F32, kind="ExternalOutput").ap()
    hsum = nc.dram_tensor("hsum", [16, SL], BF16, kind="Internal").ap()
    hinv = nc.dram_tensor("hinv", [16, SL], BF16, kind="Internal").ap()
    dbg = {}
    if debug:
        for nm, shp, dt in [("d_xfm", [128, DC * SL], BF16),
                            ("d_xfm_e", [128, DC * SL], BF16),
                            ("d_xfm_m", [128, DC * SL], BF16),
                            ("d_xfm_v", [128, DC * SL], BF16),
                            ("d_xfm_a", [128, DC * SL], BF16),
                            ("d_xfm_n", [128, DC * SL], BF16),
                            ("d_xfm_p", [128, DC * SL], BF16),
                            ("d_x2fm", [128, DC * SL], BF16),
                            ("d_hq", [128, NC1 * 512], mybir.dt.uint8),
                            ("d_xh2", [128, DC * SL], mybir.dt.uint8),
                            ("d_w1q", [128, DC * 4096], mybir.dt.uint8),
                            ("d_mbf", [1, SL], BF16),
                            ("d_b1q", [128, 32], F32),
                            ("d_b1c", [128, 32], F32),
                            ("d_rstd", [1, SL], BF16)]:
            dbg[nm] = nc.dram_tensor(nm, shp, dt, kind="ExternalOutput").ap()

    prog = {k: [] for k in ("sync", "scalar", "vector", "tensor", "gpsimd")}

    def em(eng, fn):
        prog[eng].append(fn)

    sems = {}

    class Sem:
        __slots__ = ("q", "n")

        def __init__(self, q):
            self.q = q
            self.n = 0

    perm = ExitStack()

    def mksem(name):
        sems[name] = Sem(perm.enter_context(nc.semaphore(name)))

    for name in ["cBIN", "cCONST", "cXIN", "cXCAST", "cXTR", "cXTRE",
                 "cSQV", "cSQMM", "cSTATS", "cNAR", "cBC", "cBCE",
                 "cWIN", "cWCAST", "cCST", "cCSV",
                 "cQKMM", "cQKEV", "cVTR", "cVTRE",
                 "cQKA", "cEXP", "cPV", "cOEV", "cINV", "cBCH", "cIBC",
                 "cNRM", "cPMM", "cX2",
                 "cSQV2", "cSQMM2", "cSTATS2", "cNAR2", "cBC2", "cBC2E",
                 "cXH2", "cFMM", "cFEV", "cHQ",
                 "cW2V", "cGMM", "cGEV", "cOTR", "cOASM",
                 "cMIN", "cMUSE", "cOUT", "cOCP", "cIRW", "cSUM", "cIDF", "cDBG", "cHIV", "cBQ",
                 "cSUMB", "cXINB", "cWINB", "cWINC", "cWIND", "cIRWB",
                 "cOUT1", "cOUT2", "cOUT3", "cMIN1", "cMIN2"]:
        mksem(name)

    def W(eng, sem, val):
        if val <= 0:
            return
        em(eng, lambda e, s=sems[sem].q, v=val: e.wait_ge(s, v))

    def I(eng, fn, sem=None, k=1):
        if sem is None:
            em(eng, fn)
            return None
        s = sems[sem]
        s.n += k
        em(eng, lambda e, f=fn, q=s.q, kk=k: f(e).then_inc(q, kk))
        return s.n

    def dma(eng, dst, src, sem):
        return I(eng, lambda e, d=dst, s=src: e.dma_start(out=d, in_=s), sem, 16)

    A = perm.enter_context

    ident = A(nc.sbuf_tensor("ident", [128, 128], BF16))
    wring = A(nc.sbuf_tensor("wring", [128, 3 * 1024], F32))
    borow_bf = A(nc.sbuf_tensor("borow_bf", [1, D], BF16))
    onesrow = A(nc.sbuf_tensor("onesrow", [1, 512], BF16))
    rstd_bf = A(nc.sbuf_tensor("rstd_bf", [1, SL], BF16))
    m_bf = A(nc.sbuf_tensor("m_bf", [1, SL], BF16))
    invr_bf = A(nc.sbuf_tensor("invr_bf", [1, SL], BF16))
    consts = A(nc.sbuf_tensor("consts", [128, 2], F32))
    onescol = A(nc.sbuf_tensor("onescol", [128, 1], BF16))
    b1csb = A(nc.sbuf_tensor("b1csb", [128, 32], F32))
    b2csb = A(nc.sbuf_tensor("b2csb", [128, 8], F32))
    b1q8 = A(nc.sbuf_tensor("b1q8", [128, 32], FP8))
    b2q8 = A(nc.sbuf_tensor("b2q8", [128, 8], FP8))
    b1q = A(nc.sbuf_tensor("b1q", [128, 32], F32))
    b2q = A(nc.sbuf_tensor("b2q", [128, 8], F32))
    scX = ExitStack()
    x_fm = scX.enter_context(nc.sbuf_tensor("x_fm", [128, DC * SL], BF16))

    dma("sync", ident[:], idbf, "cBIN")
    dma("sync", b1csb[:], b1c, "cBIN")
    dma("sync", b2csb[:], b2c, "cBIN")
    nbin = sems["cBIN"].n

    def g_memsets(e):
        e.memset(onescol[:], 1.0)
        e.memset(consts[:, 0:1], EXP_SHIFT)
        return e.memset(consts[:, 1:2], 1e-5)
    I("gpsimd", g_memsets, "cCONST")

    W("vector", "cBIN", nbin)
    I("vector", lambda e: e.tensor_copy(b1q8[:], b1csb[:]))
    I("vector", lambda e: e.tensor_copy(b2q8[:], b2csb[:]))
    I("vector", lambda e: e.memset(onesrow[:], 1.0), "cCONST")
    W("scalar", "cCONST", 2)
    I("scalar", lambda e: e.activation(b1q[:], b1q8[:], ACT.Identity,
                                       bias=0.0, scale=1.0))
    I("scalar", lambda e: e.activation(b2q[:], b2q8[:], ACT.Identity,
                                       bias=0.0, scale=1.0), "cBQ")
    nconst = sems["cCONST"].n

    scB = ExitStack()
    wqkv_bf = scB.enter_context(nc.sbuf_tensor("wqkv_bf", [128, DC * 3 * D], BF16, side="right"))
    r1neg = scB.enter_context(nc.sbuf_tensor("r1neg", [1, 3 * D], BF16, side="right"))
    r1b = scB.enter_context(nc.sbuf_tensor("r1b", [1, 3 * D], BF16, side="right"))

    W("sync", "cCONST", nconst)
    ring_i = 0

    WSLOT = ("cWIN", "cWINB", "cWINC")

    def stream_w(src_ap, width, cast_fn, rows_n=128, eng="vector"):
        nonlocal ring_i
        i = ring_i
        ring_i += 1
        sl = (i % 3) * 1024
        sw = WSLOT[i % 3]
        if i >= 3:
            W("sync", "cWCAST", i - 2)
        dma("sync", wring[0:rows_n, sl:sl + width], src_ap, sw)
        W(eng, sw, sems[sw].n)
        I(eng, lambda e, sl=sl: cast_fn(e, sl), "cWCAST")

    def emit_wqkv_piece(j):
        kc, p3 = divmod(j, 3)
        stream_w(wqkv[j * 128:(j + 1) * 128, :],
                 1024,
                 lambda e, sl, kc=kc, p3=p3: e.activation(
                     wqkv_bf[:, kc * 3 * D + p3 * 1024:
                             kc * 3 * D + (p3 + 1) * 1024],
                     wring[:, sl:sl + 1024], ACT.Identity,
                     bias=0.0, scale=1.0), eng="scalar")

    ph1 = ExitStack()
    xstage = ph1.enter_context(nc.sbuf_tensor("xstage", [128, 2 * D], F32, side="right"))
    xtmr = ph1.enter_context(nc.sbuf_tensor("xtmr", [128, 2 * D], BF16, side="right"))
    psTrX = [ph1.enter_context(nc.psum_tensor(f"psTrX{j}", [128, 128], BF16))
             for j in range(2)]

    W("vector", "cCONST", nconst)
    for tt in range(TT):
        sl = (tt % 2) * D
        if tt >= 2:
            W("sync", "cXCAST", tt - 1)
        sx = "cXIN" if tt % 2 == 0 else "cXINB"
        dma("sync", xstage[:, sl:sl + D], xin[tt * 128:(tt + 1) * 128, :], sx)
        W("vector", sx, sems[sx].n)
        if tt >= 2:
            W("vector", "cXTR", (tt - 1) * 8)
        I("vector", lambda e, sl=sl: e.tensor_copy(xtmr[:, sl:sl + D],
                                                   xstage[:, sl:sl + D]),
          "cXCAST")
        for c in range(DC):
            m = tt * 8 + c
            W("tensor", "cXCAST", tt + 1)
            if m >= 2:
                W("tensor", "cXTRE", m - 1)
            I("tensor", lambda e, m=m, sl=sl, c=c:
              e.transpose(psTrX[m % 2][:],
                          xtmr[:, sl + c * 128:sl + (c + 1) * 128], ident[:]),
              "cXTR")
            W("vector", "cXTR", m + 1)
            I("vector", lambda e, m=m, c=c, tt=tt:
              e.tensor_copy(
                  x_fm[:, c * SL + tt * 128:c * SL + (tt + 1) * 128],
                  psTrX[m % 2][:]), "cXTRE")
    ph1.close()
    nxtre = sems["cXTRE"].n
    if debug:
        W("sync", "cXTRE", nxtre)
        dma("sync", dbg["d_xfm_e"], x_fm[:], "cDBG")

    ph1b = ExitStack()
    sqs = ph1b.enter_context(nc.sbuf_tensor("sqs1", [128, 2 * SL], BF16))
    psSt = [ph1b.enter_context(nc.psum_tensor(f"psSt{j}", [1, SL], F32))
            for j in range(2)]
    W("vector", "cXTRE", nxtre)
    W("tensor", "cXTRE", nxtre)
    for qc in range(4):
        for c in range(DC):
            I("tensor", lambda e, qc=qc, c=c:
              e.matmul(psSt[0][:, qc * 512:(qc + 1) * 512], onescol[:],
                       x_fm[:, c * SL + qc * 512:c * SL + (qc + 1) * 512],
                       start=(c == 0), stop=(c == DC - 1)))
    for c in range(DC):
        if c >= 2:
            W("vector", "cSQMM", c - 1)
        I("vector", lambda e, c=c, sl=(c % 2) * SL:
          e.tensor_mul(sqs[:, sl:sl + SL], x_fm[:, c * SL:(c + 1) * SL],
                       x_fm[:, c * SL:(c + 1) * SL]), "cSQV")
        W("tensor", "cSQV", c + 1)
        for qc in range(4):
            I("tensor", lambda e, qc=qc, c=c, sl=(c % 2) * SL:
              e.matmul(psSt[1][:, qc * 512:(qc + 1) * 512], onescol[:],
                       sqs[:, sl + qc * 512:sl + (qc + 1) * 512],
                       start=(c == 0), stop=(c == DC - 1)),
              "cSQMM" if qc == 3 else None)
    I("tensor", lambda e: e.nop(), "cSTATS")
    W("vector", "cSTATS", 1)
    I("vector", lambda e: e.tensor_scalar(m_bf[:], psSt[0][:], 1.0 / D,
                                          None, ALU.mult))
    I("vector", lambda e: e.tensor_mul(sqs[0:1, 0:SL], m_bf[:], m_bf[:]))
    I("vector", lambda e: e.scalar_tensor_tensor(psSt[0][:], psSt[1][:],
                                                 1.0 / D, sqs[0:1, 0:SL], ALU.mult,
                                                 ALU.subtract), "cNAR")
    W("scalar", "cCONST", nconst)
    W("scalar", "cNAR", 1)
    I("scalar", lambda e: e.activation(invr_bf[:], psSt[0][:], ACT.Sqrt,
                                       bias=consts[0:1, 1:2], scale=1.0), "cNAR")
    W("vector", "cNAR", 2)
    def _recst_cNAR(e):
        with nc.allow_low_precision("LN rstd in bf16; 0.4% relative scale "
                                    "noise is far inside the native gate"):
            return e.reciprocal(rstd_bf[:], invr_bf[:])
    I("vector", _recst_cNAR, "cNAR")
    ph1b.close()

    ph1c = ExitStack()
    psB = ph1c.enter_context(nc.psum_tensor("psB", [128, SL], F32))
    rbc1 = ExitStack()
    rbc1_t = rbc1.enter_context(nc.sbuf_tensor("rbc1", [128, SL], BF16, side="right"))
    W("tensor", "cNAR", 3)
    for qc in range(4):
        I("tensor", lambda e, qc=qc:
          e.matmul(psB[:, qc * 512:(qc + 1) * 512], onesrow[0:1, 0:128],
                   rstd_bf[0:1, qc * 512:(qc + 1) * 512], start=True, stop=True),
          "cBC")
    W("vector", "cBC", 4)
    I("vector", lambda e: e.tensor_copy(rbc1_t[:], psB[:]), "cBCE")
    ph1c.close()

    scC = ExitStack()
    qk_fm = scC.enter_context(nc.sbuf_tensor("qk_fm", [128, 16 * SL], BF16))
    scV = ExitStack()
    v_fm = scV.enter_context(nc.sbuf_tensor("v_fm", [128, 8 * SL], BF16))
    for j in range(24):
        emit_wqkv_piece(j)
    nwq = sems["cWCAST"].n

    for p3 in range(3):
        stream_w(brow[0:1, p3 * 1024:(p3 + 1) * 1024], 1024,
                 lambda e, sl, p3=p3: e.tensor_copy(
                     r1b[0:1, p3 * 1024:(p3 + 1) * 1024],
                     wring[0:1, sl:sl + 1024]), rows_n=1)
    stream_w(brow[1:2, 0:1024], 1024,
             lambda e, sl: e.tensor_copy(borow_bf[0:1, 0:1024],
                                         wring[0:1, sl:sl + 1024]), rows_n=1)

    ph2cs = ExitStack()
    psCS = ph2cs.enter_context(nc.psum_tensor("psCS", [1, 512], F32))
    W("tensor", "cWCAST", nwq)
    for piece in range(6):
        if piece >= 1:
            W("tensor", "cCSV", piece)
        for kc in range(DC):
            I("tensor", lambda e, piece=piece, kc=kc:
              e.matmul(psCS[:], onescol[:],
                       wqkv_bf[:, kc * 3 * D + piece * 512:
                               kc * 3 * D + (piece + 1) * 512],
                       start=(kc == 0), stop=(kc == DC - 1)),
              "cCST" if kc == DC - 1 else None)
        W("vector", "cCST", piece + 1)
        I("vector", lambda e, piece=piece:
          e.tensor_scalar(r1neg[0:1, piece * 512:(piece + 1) * 512], psCS[:],
                          -1.0, None, ALU.mult), "cCSV")
    I("vector", lambda e: e.tensor_copy(r1neg[0:1, 0:1], r1neg[0:1, 0:1]), "cCSV")
    ncsv = sems["cCSV"].n
    ph2cs.close()

    ph2mm = ExitStack()
    psQK = [ph2mm.enter_context(nc.psum_tensor(f"psQK{j}", [128, 512], F32))
            for j in range(4)]
    W("tensor", "cCSV", ncsv)
    W("tensor", "cBCE", 1)
    g = 0
    for ec in range(EC):
        for tq in range(4):
            bank = g % 4
            if g >= 4:
                W("tensor", "cQKEV", g - 3)
            for kc in range(DC):
                I("tensor", lambda e, bank=bank, ec=ec, tq=tq, kc=kc:
                  e.matmul(psQK[bank][:],
                           wqkv_bf[:, kc * 3 * D + ec * 128:
                                   kc * 3 * D + (ec + 1) * 128],
                           x_fm[:, kc * SL + tq * 512:kc * SL + (tq + 1) * 512],
                           start=(kc == 0), stop=False))
            I("tensor", lambda e, bank=bank, ec=ec, tq=tq:
              e.matmul(psQK[bank][:], r1neg[0:1, ec * 128:(ec + 1) * 128],
                       m_bf[0:1, tq * 512:(tq + 1) * 512],
                       start=False, stop=False))
            I("tensor", lambda e, bank=bank, ec=ec, tq=tq:
              e.matmul(psQK[bank][:], r1b[0:1, ec * 128:(ec + 1) * 128],
                       invr_bf[0:1, tq * 512:(tq + 1) * 512],
                       start=False, stop=True), "cQKMM")
            W("vector", "cQKMM", g + 1)
            if ec < 16:
                dstap = lambda ec=ec, tq=tq: qk_fm[:, ec * SL + tq * 512:
                                                   ec * SL + (tq + 1) * 512]
            else:
                dstap = lambda ec=ec, tq=tq: v_fm[:, (ec - 16) * SL + tq * 512:
                                                  (ec - 16) * SL + (tq + 1) * 512]
            I("vector", lambda e, bank=bank, d=dstap, tq=tq:
              e.tensor_mul(d(), psQK[bank][:],
                           rbc1_t[:, tq * 512:(tq + 1) * 512]), "cQKEV")
            g += 1
    nqkev = sems["cQKEV"].n
    if debug:
        W("sync", "cQKEV", nqkev)
        dma("sync", dbg["d_xfm_m"], x_fm[:], "cDBG")
    ph2mm.close()
    rbc1.close()
    scB.close()

    scD = ExitStack()
    v_aug = scD.enter_context(nc.sbuf_tensor("v_aug", [128, 16 * 1040], BF16, side="right"))
    pt_c = scD.enter_context(nc.sbuf_tensor("pt_c", [128, 3 * 1024], BF16,
                                            side="right"))
    ph2tr = ExitStack()
    psTrV = [ph2tr.enter_context(nc.psum_tensor(f"psTrV{j}", [128, 128], BF16))
             for j in range(2)]
    W("vector", "cQKEV", 96)
    for tt in range(TT):
        I("vector", lambda e, tt=tt: e.memset(
            v_aug[:, tt * 1040:(tt + 1) * 1040]
            .rearrange("p (h c) -> p h c", h=16, c=65)[:, :, 64:65], 1.0))
    for c in range(8):
        ecv = 16 + c
        W("tensor", "cQKEV", (ecv + 1) * 4)
        for tt in range(TT):
            m = c * TT + tt
            if m >= 2:
                W("tensor", "cVTRE", m - 1)
            I("tensor", lambda e, m=m, c=c, tt=tt:
              e.transpose(psTrV[m % 2][:],
                          v_fm[:, c * SL + tt * 128:c * SL + (tt + 1) * 128],
                          ident[:]), "cVTR")
            W("vector", "cVTR", m + 1)
            I("vector", lambda e, m=m, c=c, tt=tt:
              e.tensor_copy(
                  v_aug[:, tt * 1040 + (2 * c) * 65:
                        tt * 1040 + (2 * c) * 65 + 130]
                  .rearrange("p (h c) -> p h c", h=2, c=65)[:, :, 0:64],
                  psTrV[m % 2][:].rearrange("p (h c) -> p h c", h=2, c=64)),
              "cVTRE")
    nvtre = sems["cVTRE"].n
    if debug:
        W("sync", "cVTRE", nvtre)
        dma("sync", dbg["d_xfm_v"], x_fm[:], "cDBG")
    ph2tr.close()
    scV.close()

    ph3 = ExitStack()
    psS = [ph3.enter_context(nc.psum_tensor(f"psS{j}", [128, 1024], F32))
           for j in range(2)]
    psO = [ph3.enter_context(nc.psum_tensor(f"psO{j}", [128, 1024], F32))
           for j in range(2)]
    scE = ExitStack()
    attn_fm = scE.enter_context(nc.sbuf_tensor("attn_fm", [128, 8 * SL], BF16))
    scE2 = ExitStack()
    sums_bf = scE2.enter_context(nc.sbuf_tensor("sums_bf", [16, SL], BF16))
    inv_bf = scE2.enter_context(nc.sbuf_tensor("inv_bf", [16, SL], BF16))
    invb = scE2.enter_context(nc.sbuf_tensor("invb", [128, SL], BF16))
    invrow = [scE2.enter_context(nc.sbuf_tensor("invrow0", [1, SL], BF16))]
    scE3 = ExitStack()
    srow2 = [scE3.enter_context(nc.sbuf_tensor(f"srow{s}", [1, SL], BF16))
             for s in range(2)]

    W("tensor", "cVTRE", nvtre)
    W("scalar", "cVTRE", nvtre)
    CH = 8 * 2 * KT * 2

    def _cqk(s):
        c, r = divmod(s, 2 * KT)
        qh, kt = divmod(r, KT)
        return c, qh, kt

    def emit_qk(k):
        s, qq = k // 2, k % 2
        c, qh, kt = _cqk(s)
        if k >= 2:
            W("tensor", "cEXP", k - 1)
        qcol = c * SL + qh * 1024 + qq * 512
        kcol = (8 + c) * SL + kt * 128
        I("tensor", lambda e, k=k, qcol=qcol, kcol=kcol:
          e.matmul(psS[k % 2][:, 0:512],
                   qk_fm[0:64, kcol:kcol + 128],
                   qk_fm[0:64, qcol:qcol + 512],
                   start=True, stop=True, tile_position=(0, 0)))
        I("tensor", lambda e, k=k, qcol=qcol, kcol=kcol:
          e.matmul(psS[k % 2][:, 512:1024],
                   qk_fm[64:128, kcol:kcol + 128],
                   qk_fm[64:128, qcol:qcol + 512],
                   start=True, stop=True, tile_position=(64, 0)), "cQKA")

    def emit_exp(k):
        W("scalar", "cQKA", k + 1)
        if k >= 3:
            W("scalar", "cPV", 2 * (k - 2))
        I("scalar", lambda e, k=k:
          e.activation(pt_c[:, (k % 3) * 1024:(k % 3) * 1024 + 1024],
                       psS[k % 2][:], ACT.Exp, bias=consts[:, 0:1],
                       scale=0.125), "cEXP")

    def emit_pv(k):
        s, qq = k // 2, k % 2
        c, qh, kt = _cqk(s)
        m = 2 * c + qh
        W("tensor", "cEXP", k + 1)
        for sub in range(2):
            if kt == 0 and qq == 0 and m >= 1:
                W("tensor", "cOEV", 4 * (m - 1) + 2 * (sub + 1))
            I("tensor", lambda e, sub=sub, qq=qq, kt=kt, c=c, k=k:
              e.matmul(psO[sub][0:65, qq * 512:(qq + 1) * 512],
                       v_aug[:, kt * 1040 + (2 * c + sub) * 65:
                             kt * 1040 + (2 * c + sub) * 65 + 65],
                       pt_c[:, (k % 3) * 1024 + sub * 512:
                             (k % 3) * 1024 + (sub + 1) * 512],
                       start=(kt == 0), stop=(kt == KT - 1)), "cPV")

    def emit_evac(k):
        s = k // 2
        c, qh, kt = _cqk(s)
        m = 2 * c + qh
        W("vector", "cPV", 2 * (k + 1))
        if qh == 0 and c >= 1:
            W("vector", "cSUM", 32 * c)
        for sub in range(2):
            I("vector", lambda e, sub=sub, c=c, qh=qh:
              e.tensor_copy(attn_fm[sub * 64:sub * 64 + 64,
                                    c * SL + qh * 1024:c * SL + qh * 1024 + 1024],
                            psO[sub][0:64, :]), "cOEV")
            I("vector", lambda e, sub=sub, qh=qh:
              e.tensor_copy(srow2[sub][0:1, qh * 1024:qh * 1024 + 1024],
                            psO[sub][64:65, :]), "cOEV")
        if qh == 1:
            W("sync", "cOEV", 4 * (m + 1))
            for sub in range(2):
                dma("sync", hsum[2 * c + sub:2 * c + sub + 1, :],
                    srow2[sub][:], "cSUM")

    emit_qk(0)
    for k in range(CH):
        emit_exp(k)
        if k + 1 < CH:
            emit_qk(k + 1)
        if k >= 2:
            emit_pv(k - 2)
            if (k - 2) % 2 == 1 and _cqk((k - 2) // 2)[2] == KT - 1:
                emit_evac(k - 2)
    for k in (CH - 2, CH - 1):
        emit_pv(k)
        if k % 2 == 1 and _cqk(k // 2)[2] == KT - 1:
            emit_evac(k)
    scE3.close()
    scD.close()
    scG = ExitStack()
    x2_fm = scG.enter_context(nc.sbuf_tensor("x2_fm", [128, DC * SL], BF16, side="right"))
    scF = ExitStack()
    wout_bf = scF.enter_context(nc.sbuf_tensor("wout_bf", [128, DC * D], BF16, side="right"))
    for kc in range(DC):
        stream_w(wout[kc * 128:(kc + 1) * 128, :], 1024,
                 lambda e, sl, kc=kc: e.activation(
                     wout_bf[:, kc * D:(kc + 1) * D], wring[:, sl:sl + 1024],
                     ACT.Identity, bias=0.0, scale=1.0), eng="scalar")
    nwo = sems["cWCAST"].n

    W("sync", "cSUM", sems["cSUM"].n)
    W("sync", "cSUMB", sems["cSUMB"].n)
    dma("sync", sums_bf[:], hsum, "cSUM")
    W("vector", "cSUM", sems["cSUM"].n)
    def _recip_lp(e):
        with nc.allow_low_precision("softmax denominator reciprocal in bf16; "
                                    "0.4% relative is far inside native tol"):
            return e.reciprocal(inv_bf[:], sums_bf[:])
    I("vector", _recip_lp, "cINV")
    W("sync", "cINV", 1)
    dma("sync", hinv, inv_bf[:], "cHIV")
    for hp in range(HEADS // 2):
        for sub in range(2):
            h = 2 * hp + sub
            ib = h
            if ib >= 1:
                W("sync", "cBCH", ib)
            W("sync", "cHIV", 16)
            dma("sync", invrow[0][:], hinv[h:h + 1, :], "cIRW")
            W("tensor", "cIRW", sems["cIRW"].n)
            W("tensor", "cPV", sems["cPV"].n)
            if hp >= 1:
                W("tensor", "cIBC", hp)
            for qq in range(4):
                I("tensor", lambda e, sub=sub, qq=qq, ib=ib:
                  e.matmul(psS[qq // 2][sub * 64:sub * 64 + 64,
                                        (qq % 2) * 512:(qq % 2) * 512 + 512],
                           onesrow[0:1, 0:64],
                           invrow[0][0:1, qq * 512:(qq + 1) * 512],
                           start=True, stop=True), "cBCH" if qq == 3 else None)
        W("vector", "cBCH", 2 * hp + 2)
        I("vector", lambda e: e.tensor_copy(invb[:, 0:1024], psS[0][:]))
        I("vector", lambda e: e.tensor_copy(invb[:, 1024:2048], psS[1][:]), "cIBC")
        I("vector", lambda e, hp=hp:
          e.tensor_mul(attn_fm[:, hp * SL:(hp + 1) * SL],
                       attn_fm[:, hp * SL:(hp + 1) * SL],
                       invb[:]), "cNRM")
    nnrm = sems["cNRM"].n
    if debug:
        W("sync", "cNRM", nnrm)
        dma("sync", dbg["d_xfm_n"], x_fm[:], "cDBG")
    ph3.close()
    scE2.close()

    ph4 = ExitStack()
    psP = [ph4.enter_context(nc.psum_tensor(f"psP{j}", [128, 512], F32))
           for j in range(4)]
    W("tensor", "cWCAST", nwo)
    W("tensor", "cNRM", nnrm)
    g = 0
    for oc in range(DC):
        for tq in range(4):
            bank = g % 4
            if g >= 4:
                W("tensor", "cX2", g - 3)
            for kc in range(DC):
                I("tensor", lambda e, bank=bank, oc=oc, tq=tq, kc=kc:
                  e.matmul(psP[bank][:],
                           wout_bf[:, kc * D + oc * 128:kc * D + (oc + 1) * 128],
                           attn_fm[:, kc * SL + tq * 512:kc * SL + (tq + 1) * 512],
                           start=(kc == 0), stop=False))
            I("tensor", lambda e, bank=bank, oc=oc, tq=tq:
              e.matmul(psP[bank][:], borow_bf[0:1, oc * 128:(oc + 1) * 128],
                       onesrow[0:1, 0:512],
                       start=False, stop=True), "cPMM")
            W("vector", "cPMM", g + 1)
            I("vector", lambda e, bank=bank, oc=oc, tq=tq:
              e.tensor_tensor(
                  x2_fm[:, oc * SL + tq * 512:oc * SL + (tq + 1) * 512],
                  psP[bank][:],
                  x_fm[:, oc * SL + tq * 512:oc * SL + (tq + 1) * 512],
                  ALU.add), "cX2")
            g += 1
    nx2 = sems["cX2"].n
    ph4.close()
    scE.close()
    scC.close()
    scX.close()
    scF.close()

    if debug:
        W("sync", "cX2", nx2)
    scI = ExitStack()
    hq8 = scI.enter_context(nc.sbuf_tensor("hq8", [128, NC1 * 512], FP8))
    otm = scI.enter_context(nc.sbuf_tensor("otm", [128, 4 * D], F32))
    w1q8 = scI.enter_context(nc.sbuf_tensor("w1q8", [128, DC * 4096], FP8))
    identf = scI.enter_context(nc.sbuf_tensor("identf", [128, 128], F32))
    w2q8 = scI.enter_context(nc.sbuf_tensor("w2q8", [128, NC1 * 1024], FP8))
    hbf = scI.enter_context(nc.sbuf_tensor("hbf", [128, 2 * 512], BF16))
    y2bf = scI.enter_context(nc.sbuf_tensor("y2bf", [128, 3 * 512], BF16))
    t1s = scI.enter_context(nc.sbuf_tensor("t1s", [128, 512], F32))
    ofm = scI.enter_context(nc.sbuf_tensor("ofm", [128, 3 * 512], F32))
    mring = scI.enter_context(nc.sbuf_tensor("mring", [128, 3 * 512], BF16))



    W("sync", "cQKA", sems["cQKA"].n)
    dma("sync", identf[:], idf32, "cIDF")

    W("scalar", "cQKA", sems["cQKA"].n)
    w1base = sems["cWCAST"].n
    for q4 in range(4):
        for kc in range(DC):
            stream_w(w1t[(q4 * 8 + kc) * 128:(q4 * 8 + kc + 1) * 128, :],
                     1024,
                     lambda e, sl, kc=kc, q4=q4: e.activation(
                         w1q8[:, kc * 4096 + q4 * 1024:
                              kc * 4096 + (q4 + 1) * 1024],
                         wring[:, sl:sl + 1024], ACT.Identity,
                         bias=0.0, scale=1.0), eng="scalar")
    nw1 = sems["cWCAST"].n
    w2base = ring_i
    w2win = []
    for k in range(NC1):
        i = ring_i
        ring_i += 1
        sl = (i % 3) * 1024
        sw = WSLOT[i % 3]
        j = i - 3
        if j >= 0:
            if j < w2base:
                W("sync", "cWCAST", j + 1)
            else:
                W("sync", "cW2V", j + 1 - w2base)
        dma("sync", wring[:, sl:sl + 1024],
            w2t[k * 128:(k + 1) * 128, 0:1024], sw)
        w2win.append((sw, sems[sw].n))

    scH = ExitStack()
    sqs = scH.enter_context(nc.sbuf_tensor("sqs2", [128, 2 * SL], BF16))
    m2bc = scH.enter_context(nc.sbuf_tensor("m2bc", [128, SL], BF16))
    r2bc = scH.enter_context(nc.sbuf_tensor("r2bc", [128, SL], BF16))
    xh2q8 = scH.enter_context(nc.sbuf_tensor("xh2q8", [128, DC * SL], FP8))
    ph4b = ExitStack()
    psSt2 = [ph4b.enter_context(nc.psum_tensor(f"psSt2{j}", [1, SL], F32))
             for j in range(2)]
    W("tensor", "cX2", nx2)
    W("vector", "cX2", nx2)
    for qc in range(4):
        for c in range(DC):
            I("tensor", lambda e, qc=qc, c=c:
              e.matmul(psSt2[0][:, qc * 512:(qc + 1) * 512], onescol[:],
                       x2_fm[:, c * SL + qc * 512:c * SL + (qc + 1) * 512],
                       start=(c == 0), stop=(c == DC - 1)))
    for c in range(DC):
        if c >= 2:
            W("vector", "cSQMM2", c - 1)
        I("vector", lambda e, c=c, sl=(c % 2) * SL:
          e.tensor_mul(sqs[:, sl:sl + SL], x2_fm[:, c * SL:(c + 1) * SL],
                       x2_fm[:, c * SL:(c + 1) * SL]), "cSQV2")
        W("tensor", "cSQV2", c + 1)
        for qc in range(4):
            I("tensor", lambda e, qc=qc, c=c, sl=(c % 2) * SL:
              e.matmul(psSt2[1][:, qc * 512:(qc + 1) * 512], onescol[:],
                       sqs[:, sl + qc * 512:sl + (qc + 1) * 512],
                       start=(c == 0), stop=(c == DC - 1)),
              "cSQMM2" if qc == 3 else None)
    I("tensor", lambda e: e.nop(), "cSTATS2")
    W("vector", "cSTATS2", 1)
    I("vector", lambda e: e.tensor_scalar(m_bf[:], psSt2[0][:], 1.0 / D,
                                          None, ALU.mult))
    I("vector", lambda e: e.tensor_mul(sqs[0:1, 0:SL], m_bf[:], m_bf[:]))
    I("vector", lambda e: e.scalar_tensor_tensor(psSt2[0][:], psSt2[1][:],
                                                 1.0 / D, sqs[0:1, 0:SL], ALU.mult,
                                                 ALU.subtract), "cNAR2")
    W("scalar", "cCONST", nconst)
    W("scalar", "cNAR2", 1)
    I("scalar", lambda e: e.activation(invr_bf[:], psSt2[0][:], ACT.Sqrt,
                                       bias=consts[0:1, 1:2], scale=1.0), "cNAR2")
    W("vector", "cNAR2", 2)
    def _recst_cNAR2(e):
        with nc.allow_low_precision("LN rstd in bf16; 0.4% relative scale "
                                    "noise is far inside the native gate"):
            return e.reciprocal(rstd_bf[:], invr_bf[:])
    I("vector", _recst_cNAR2, "cNAR2")
    ph4b.close()

    ph4c = ExitStack()
    psB2 = ph4c.enter_context(nc.psum_tensor("psB2", [128, SL], F32))
    W("tensor", "cNAR2", 3)
    for qc in range(4):
        I("tensor", lambda e, qc=qc:
          e.matmul(psB2[:, qc * 512:(qc + 1) * 512], onesrow[0:1, 0:128],
                   m_bf[0:1, qc * 512:(qc + 1) * 512], start=True, stop=True),
          "cBC2" if qc == 3 else None)
    W("vector", "cBC2", 1)
    I("vector", lambda e: e.tensor_copy(m2bc[:], psB2[:]), "cBC2E")
    W("tensor", "cBC2E", 1)
    for qc in range(4):
        I("tensor", lambda e, qc=qc:
          e.matmul(psB2[:, qc * 512:(qc + 1) * 512], onesrow[0:1, 0:128],
                   rstd_bf[0:1, qc * 512:(qc + 1) * 512], start=True, stop=True),
          "cBC2" if qc == 3 else None)
    W("vector", "cBC2", 2)
    I("vector", lambda e: e.tensor_copy(r2bc[:], psB2[:]), "cBC2E")
    ph4c.close()

    for c in range(DC):
        sl = (c % 2) * SL
        I("vector", lambda e, c=c, sl=sl:
          e.tensor_sub(sqs[:, sl:sl + SL], x2_fm[:, c * SL:(c + 1) * SL],
                       m2bc[:]))
        I("vector", lambda e, c=c, sl=sl:
          e.tensor_mul(xh2q8[:, c * SL:(c + 1) * SL], sqs[:, sl:sl + SL],
                       r2bc[:]), "cXH2")
    nxh2 = sems["cXH2"].n

    ph5 = ExitStack()
    psF = [ph5.enter_context(nc.psum_tensor(f"psF{j}", [128, 512], F32))
           for j in range(3)]
    psG = [ph5.enter_context(nc.psum_tensor(f"psG{j}", [128, 512], F32))
           for j in range(3)]
    psTrO = [ph5.enter_context(nc.psum_tensor(f"psTrO{j}", [128, 128], F32))
             for j in range(2)]

    W("tensor", "cXH2", nxh2)
    gf = 0
    gg = 0
    tro = 0
    pend = []

    def emit_oasm(p):
        nonlocal tro
        ts_p, oc_p = divmod(p, DC)
        W("tensor", "cIDF", 16)
        W("tensor", "cOASM", p + 1)
        for j in range(4):
            tt = ts_p * 4 + j
            if tro >= 2:
                W("tensor", "cOCP", tro - 1)
            I("tensor", lambda e, tro=tro, j=j, p=p:
              e.transpose(psTrO[tro % 2][:],
                          ofm[:, (p % 3) * 512 + j * 128:
                              (p % 3) * 512 + (j + 1) * 128],
                          identf[:]), "cOTR")
            W("vector", "cOTR", tro + 1)
            if oc_p == 0 and tt >= 4:
                so = ("cOUT", "cOUT1", "cOUT2", "cOUT3")[tt % 4]
                W("vector", so, 16 * ts_p)
            I("vector", lambda e, tro=tro, tt=tt, oc_p=oc_p:
              e.tensor_copy(otm[:, (tt % 4) * D + oc_p * 128:
                            (tt % 4) * D + (oc_p + 1) * 128],
                            psTrO[tro % 2][:]), "cOCP")
            tro += 1
        if oc_p == DC - 1:
            for j in range(4):
                tt = ts_p * 4 + j
                W("sync", "cOCP", (ts_p * DC + 7) * 4 + j + 1)
                dma("sync", yout[tt * 128:(tt + 1) * 128, :],
                    otm[:, (tt % 4) * D:(tt % 4) * D + D],
                    ("cOUT", "cOUT1", "cOUT2", "cOUT3")[tt % 4])

    for ts in range(TSN):
        for ncc in range(NC1):
            bank = gf % 3
            if ts == 0:
                W("tensor", "cWCAST", w1base + 8 * (ncc // 8 + 1))
            if gf >= 3:
                W("tensor", "cFEV", gf - 2)
            for kc in range(DC):
                I("tensor", lambda e, bank=bank, ncc=ncc, ts=ts, kc=kc:
                  e.matmul(psF[bank][:],
                           w1q8[:, kc * 4096 + ncc * 128:kc * 4096 + (ncc + 1) * 128],
                           xh2q8[:, kc * SL + ts * 512:kc * SL + (ts + 1) * 512],
                           start=(kc == 0), stop=(kc == DC - 1)), "cFMM" if kc == DC - 1 else None)
            W("scalar", "cBQ", 1)
            W("scalar", "cFMM", gf + 1)
            if gf >= 2:
                W("scalar", "cHQ", gf - 1)
            hs = gf % 2
            I("scalar", lambda e, bank=bank, hs=hs, ncc=ncc:
              e.activation(hbf[:, hs * 512:(hs + 1) * 512], psF[bank][:],
                           ACT.Relu, bias=b1q[:, ncc:ncc + 1], scale=1.0),
              "cFEV")
            W("vector", "cFEV", gf + 1)
            if ts >= 1:
                W("vector", "cGMM", ts * 8)
            I("vector", lambda e, hs=hs, ncc=ncc:
              e.tensor_copy(hq8[:, ncc * 512:(ncc + 1) * 512],
                            hbf[:, hs * 512:(hs + 1) * 512]), "cHQ")
            if ts == 0:
                sl2 = ((w2base + ncc) % 3) * 1024
                W("vector", w2win[ncc][0], w2win[ncc][1])
                I("vector", lambda e, ncc=ncc, sl2=sl2:
                  e.tensor_copy(w2q8[:, ncc * 1024:(ncc + 1) * 1024],
                                wring[:, sl2:sl2 + 1024]), "cW2V")
            gf += 1
        for oc in range(DC):
            gb = gg % 3
            if gg >= 3:
                W("sync", "cMUSE", gg - 2)
            sm = ("cMIN", "cMIN1", "cMIN2")[gg % 3]
            dma("sync", mring[:, (gg % 3) * 512:(gg % 3 + 1) * 512],
                maskfm[oc * 128:(oc + 1) * 128, ts * 512:(ts + 1) * 512], sm)
            nmin = (sm, sems[sm].n)
            W("tensor", "cHQ", (ts + 1) * NC1)
            if gg == 0:
                W("tensor", "cW2V", NC1)
            if gg >= 3:
                W("tensor", "cGEV", gg - 2)
            for ncc in range(NC1):
                I("tensor", lambda e, gb=gb, ncc=ncc, oc=oc:
                  e.matmul(psG[gb][:],
                           w2q8[:, ncc * 1024 + oc * 128:
                                ncc * 1024 + (oc + 1) * 128],
                           hq8[:, ncc * 512:(ncc + 1) * 512],
                           start=(ncc == 0), stop=(ncc == NC1 - 1)),
                  "cGMM" if ncc == NC1 - 1 else None)
            W("scalar", "cGMM", gg + 1)
            if gg >= 3:
                W("scalar", "cMUSE", gg - 2)
            I("scalar", lambda e, gb=gb, gg=gg, oc=oc:
              e.activation(y2bf[:, (gg % 3) * 512:(gg % 3 + 1) * 512],
                           psG[gb][:], ACT.Identity,
                           bias=b2q[:, oc:oc + 1], scale=1.0), "cGEV")
            W("vector", "cGEV", gg + 1)
            W("vector", nmin[0], nmin[1])
            I("vector", lambda e, gg=gg:
              e.tensor_mul(t1s[:], y2bf[:, (gg % 3) * 512:(gg % 3 + 1) * 512],
                           mring[:, (gg % 3) * 512:(gg % 3 + 1) * 512]), "cMUSE")
            if gg >= 3:
                W("vector", "cOTR", 4 * (gg - 2))
            I("vector", lambda e, gg=gg, oc=oc, ts=ts:
              e.tensor_tensor(ofm[:, (gg % 3) * 512:(gg % 3 + 1) * 512],
                              t1s[:],
                              x2_fm[:, oc * SL + ts * 512:oc * SL + (ts + 1) * 512],
                              ALU.add), "cOASM")
            pend.append(gg)
            if len(pend) > 2:
                emit_oasm(pend.pop(0))
            gg += 1
    while pend:
        emit_oasm(pend.pop(0))
    ph5.close()
    scH.close()
    scI.close()
    scG.close()

    if debug:
        W("sync", "cOUT", sems["cOUT"].n)
        W("sync", "cDBG", sems["cDBG"].n)
        dma("sync", dbg["d_xfm"], x_fm[:], "cOUT")
        dma("sync", dbg["d_x2fm"], x2_fm[:], "cOUT")
        dma("sync", dbg["d_hq"], hq8[:].bitcast(mybir.dt.uint8), "cOUT")
        dma("sync", dbg["d_xh2"], xh2q8[:].bitcast(mybir.dt.uint8), "cOUT")
        dma("sync", dbg["d_w1q"], w1q8[:].bitcast(mybir.dt.uint8), "cOUT")
        dma("sync", dbg["d_mbf"], m_bf[:], "cOUT")
        dma("sync", dbg["d_b1q"], b1q[:], "cOUT")
        dma("sync", dbg["d_b1c"], b1csb[:], "cOUT")
        dma("sync", dbg["d_rstd"], rstd_bf[:], "cOUT")
    for so in ("cOUT", "cOUT1", "cOUT2", "cOUT3"):
        W("gpsimd", so, sems[so].n)

    with nc.Block() as blk:
        @blk.sync
        def _(e):
            for fn in prog["sync"]:
                fn(e)

        @blk.scalar
        def _(e):
            for fn in prog["scalar"]:
                fn(e)

        @blk.vector
        def _(e):
            for fn in prog["vector"]:
                fn(e)

        @blk.tensor
        def _(e):
            for fn in prog["tensor"]:
                fn(e)

        @blk.gpsimd
        def _(e):
            for fn in prog["gpsimd"]:
                fn(e)

    perm.close()
    return nc


_NC_CACHE = None


def kernel(L0_attn_x, L0_attn_W_qkv, L0_attn_b_qkv, L0_attn_W_out,
           L0_attn_b_out, L0_ffn_W1, L0_ffn_b1, L0_ffn_W2, L0_ffn_b2):
    global _NC_CACHE
    x = np.asarray(L0_attn_x, dtype=np.float32)
    B = x.shape[0]
    keep = _dropout_keep_mask((B, SL, D))
    maskf = np.where(keep, np.float32(1.25), np.float32(0.0))

    wqkv = np.ascontiguousarray(
        np.asarray(L0_attn_W_qkv, dtype=np.float32)
        .reshape(8, 128, 3, 1024).transpose(0, 2, 1, 3).reshape(24 * 128, 1024))
    wout = np.ascontiguousarray(np.asarray(L0_attn_W_out, dtype=np.float32))
    w1t = np.ascontiguousarray(
        np.asarray(L0_ffn_W1, dtype=np.float32).T
        .reshape(8, 128, 4, 1024).transpose(2, 0, 1, 3).reshape(32 * 128, 1024))
    w2t = np.ascontiguousarray(np.asarray(L0_ffn_W2, dtype=np.float32).T)
    bq = np.asarray(L0_attn_b_qkv, dtype=np.float32)
    bo = np.asarray(L0_attn_b_out, dtype=np.float32)
    b1 = np.asarray(L0_ffn_b1, dtype=np.float32)
    b2 = np.asarray(L0_ffn_b2, dtype=np.float32)

    brow = np.zeros((2, 3 * D), dtype=np.float32)
    brow[0, :] = bq
    brow[1, :D] = bo
    b1c = np.ascontiguousarray(b1.reshape(32, 128).T)
    b2c = np.ascontiguousarray(b2.reshape(8, 128).T)
    idbf = np.eye(128, dtype=ml_dtypes.bfloat16)
    idf32 = np.eye(128, dtype=np.float32)

    if _NC_CACHE is None:
        _NC_CACHE = build_nc()
    nc = _NC_CACHE

    in_maps = []
    for b in range(B):
        in_maps.append({
            "xin": np.ascontiguousarray(x[b]),
            "wqkv": wqkv, "wout": wout, "w1t": w1t, "w2t": w2t,
            "brow": brow, "b1c": b1c, "b2c": b2c,
            "idbf": idbf, "idf32": idf32,
            "maskfm": np.ascontiguousarray(
                maskf[b].T.astype(ml_dtypes.bfloat16)),
        })
    res = run_bass_kernel_spmd(nc, in_maps, core_ids=list(range(B)))
    out = np.stack([np.asarray(r["yout"]) for r in res.results], axis=0)
    return out.astype(np.float32)


# version 23
# speedup vs baseline: 1.0444x; 1.0175x over previous
import numpy as np
import ml_dtypes
import concourse.bass as bass
import concourse.mybir as mybir
from concourse.bass_utils import run_bass_kernel_spmd
from contextlib import ExitStack

F32 = mybir.dt.float32
BF16 = mybir.dt.bfloat16
FP8 = mybir.dt.float8e5
ALU = mybir.AluOpType
ACT = mybir.ActivationFunctionType

SL = 2048
D = 1024
HEADS = 16
TT = 16
DC = 8
EC = 24
KT = 16
NC1 = 32
TSN = 4
EXP_SHIFT = -12.0


def _threefry2x32(k0, k1, x0, x1):
    rot = [13, 15, 26, 6, 17, 29, 16, 24]
    ks0, ks1 = np.uint32(k0), np.uint32(k1)
    ks2 = np.uint32(np.uint32(ks0) ^ np.uint32(ks1) ^ np.uint32(0x1BD11BDA))
    x0 = (x0 + ks0).astype(np.uint32)
    x1 = (x1 + ks1).astype(np.uint32)

    def rotl(v, d):
        return ((v << np.uint32(d)) | (v >> np.uint32(32 - d))).astype(np.uint32)

    ks = [ks0, ks1, ks2]
    for i in range(5):
        r = rot[:4] if i % 2 == 0 else rot[4:]
        for j in range(4):
            x0 = (x0 + x1).astype(np.uint32)
            x1 = rotl(x1, r[j])
            x1 = (x1 ^ x0).astype(np.uint32)
        x0 = (x0 + ks[(i + 1) % 3]).astype(np.uint32)
        x1 = (x1 + ks[(i + 2) % 3] + np.uint32(i + 1)).astype(np.uint32)
    return x0, x1


def _dropout_keep_mask(shape, p_keep=0.8, key=(0, 42)):
    n = int(np.prod(shape))
    idx = np.arange(n, dtype=np.uint64)
    hi = (idx >> np.uint64(32)).astype(np.uint32)
    lo = (idx & np.uint64(0xFFFFFFFF)).astype(np.uint32)
    b0, b1 = _threefry2x32(key[0], key[1], hi, lo)
    bits = (b0 ^ b1).astype(np.uint32)
    fl = ((bits >> np.uint32(9)) | np.uint32(0x3F800000)).view(np.float32) \
        - np.float32(1.0)
    return (fl < np.float32(p_keep)).reshape(shape)


def build_nc(debug=False):
    nc = bass.Bass("TRN2", target_bir_lowering=False,
                   dynamic_dma_scratch_size=512)

    xin = nc.dram_tensor("xin", [SL, D], F32, kind="ExternalInput").ap()
    wqkv = nc.dram_tensor("wqkv", [24 * 128, 1024], F32, kind="ExternalInput").ap()
    wout = nc.dram_tensor("wout", [D, D], F32, kind="ExternalInput").ap()
    w1t = nc.dram_tensor("w1t", [32 * 128, 1024], F32, kind="ExternalInput").ap()
    w2t = nc.dram_tensor("w2t", [4 * D, D], F32, kind="ExternalInput").ap()
    brow = nc.dram_tensor("brow", [2, 3 * D], F32, kind="ExternalInput").ap()
    b1c = nc.dram_tensor("b1c", [128, 32], F32, kind="ExternalInput").ap()
    b2c = nc.dram_tensor("b2c", [128, 8], F32, kind="ExternalInput").ap()
    idbf = nc.dram_tensor("idbf", [128, 128], BF16, kind="ExternalInput").ap()
    idf32 = nc.dram_tensor("idf32", [128, 128], F32, kind="ExternalInput").ap()
    maskfm = nc.dram_tensor("maskfm", [D, SL], BF16, kind="ExternalInput").ap()
    yout = nc.dram_tensor("yout", [SL, D], F32, kind="ExternalOutput").ap()
    hsum = nc.dram_tensor("hsum", [16, SL], BF16, kind="Internal").ap()
    hinv = nc.dram_tensor("hinv", [16, SL], BF16, kind="Internal").ap()
    dbg = {}
    if debug:
        for nm, shp, dt in [("d_xfm", [128, DC * SL], BF16),
                            ("d_xfm_e", [128, DC * SL], BF16),
                            ("d_xfm_m", [128, DC * SL], BF16),
                            ("d_xfm_v", [128, DC * SL], BF16),
                            ("d_xfm_a", [128, DC * SL], BF16),
                            ("d_xfm_n", [128, DC * SL], BF16),
                            ("d_xfm_p", [128, DC * SL], BF16),
                            ("d_x2fm", [128, DC * SL], BF16),
                            ("d_hq", [128, NC1 * 512], mybir.dt.uint8),
                            ("d_xh2", [128, DC * SL], mybir.dt.uint8),
                            ("d_w1q", [128, DC * 4096], mybir.dt.uint8),
                            ("d_mbf", [1, SL], BF16),
                            ("d_b1q", [128, 32], F32),
                            ("d_b1c", [128, 32], F32),
                            ("d_rstd", [1, SL], BF16)]:
            dbg[nm] = nc.dram_tensor(nm, shp, dt, kind="ExternalOutput").ap()

    prog = {k: [] for k in ("sync", "scalar", "vector", "tensor", "gpsimd")}

    def em(eng, fn):
        prog[eng].append(fn)

    sems = {}

    class Sem:
        __slots__ = ("q", "n")

        def __init__(self, q):
            self.q = q
            self.n = 0

    perm = ExitStack()

    def mksem(name):
        sems[name] = Sem(perm.enter_context(nc.semaphore(name)))

    for name in ["cBIN", "cCONST", "cXIN", "cXCAST", "cXTR", "cXTRE",
                 "cSQV", "cSQMM", "cSTATS", "cNAR", "cBC", "cBCE",
                 "cWIN", "cWCAST", "cCST", "cCSV",
                 "cQKMM", "cQKEV", "cVTR", "cVTRE",
                 "cQKA", "cEXP", "cPV", "cOEV", "cINV", "cBCH", "cIBC",
                 "cNRM", "cPMM", "cX2",
                 "cSQV2", "cSQMM2", "cSTATS2", "cNAR2", "cBC2", "cBC2E",
                 "cXH2", "cFMM", "cFEV", "cHQ",
                 "cW2V", "cGMM", "cGEV", "cOTR", "cOASM",
                 "cMIN", "cMUSE", "cOUT", "cOCP", "cIRW", "cSUM", "cIDF", "cDBG", "cHIV", "cBQ",
                 "cSUMB", "cXINB", "cWINB", "cWINC", "cWIND", "cIRWB",
                 "cOUT1", "cOUT2", "cOUT3", "cMIN1", "cMIN2"]:
        mksem(name)

    def W(eng, sem, val):
        if val <= 0:
            return
        em(eng, lambda e, s=sems[sem].q, v=val: e.wait_ge(s, v))

    def I(eng, fn, sem=None, k=1):
        if sem is None:
            em(eng, fn)
            return None
        s = sems[sem]
        s.n += k
        em(eng, lambda e, f=fn, q=s.q, kk=k: f(e).then_inc(q, kk))
        return s.n

    def dma(eng, dst, src, sem):
        return I(eng, lambda e, d=dst, s=src: e.dma_start(out=d, in_=s), sem, 16)

    A = perm.enter_context

    ident = A(nc.sbuf_tensor("ident", [128, 128], BF16))
    wring = A(nc.sbuf_tensor("wring", [128, 3 * 1024], F32))
    borow_bf = A(nc.sbuf_tensor("borow_bf", [1, D], BF16))
    onesrow = A(nc.sbuf_tensor("onesrow", [1, 512], BF16))
    rstd_bf = A(nc.sbuf_tensor("rstd_bf", [1, SL], BF16))
    m_bf = A(nc.sbuf_tensor("m_bf", [1, SL], BF16))
    invr_bf = A(nc.sbuf_tensor("invr_bf", [1, SL], BF16))
    consts = A(nc.sbuf_tensor("consts", [128, 2], F32))
    onescol = A(nc.sbuf_tensor("onescol", [128, 1], BF16))
    b1csb = A(nc.sbuf_tensor("b1csb", [128, 32], F32))
    b2csb = A(nc.sbuf_tensor("b2csb", [128, 8], F32))
    b1q8 = A(nc.sbuf_tensor("b1q8", [128, 32], FP8))
    b2q8 = A(nc.sbuf_tensor("b2q8", [128, 8], FP8))
    b1q = A(nc.sbuf_tensor("b1q", [128, 32], F32))
    b2q = A(nc.sbuf_tensor("b2q", [128, 8], F32))
    scX = ExitStack()
    x_fm = scX.enter_context(nc.sbuf_tensor("x_fm", [128, DC * SL], BF16))

    dma("sync", ident[:], idbf, "cBIN")
    dma("sync", b1csb[:], b1c, "cBIN")
    dma("sync", b2csb[:], b2c, "cBIN")
    nbin = sems["cBIN"].n

    def g_memsets(e):
        e.memset(onescol[:], 1.0)
        e.memset(consts[:, 0:1], EXP_SHIFT)
        return e.memset(consts[:, 1:2], 1e-5)
    I("gpsimd", g_memsets, "cCONST")

    W("vector", "cBIN", nbin)
    I("vector", lambda e: e.tensor_copy(b1q8[:], b1csb[:]))
    I("vector", lambda e: e.tensor_copy(b2q8[:], b2csb[:]))
    I("vector", lambda e: e.memset(onesrow[:], 1.0), "cCONST")
    W("scalar", "cCONST", 2)
    I("scalar", lambda e: e.activation(b1q[:], b1q8[:], ACT.Identity,
                                       bias=0.0, scale=1.0))
    I("scalar", lambda e: e.activation(b2q[:], b2q8[:], ACT.Identity,
                                       bias=0.0, scale=1.0), "cBQ")
    nconst = sems["cCONST"].n

    scB = ExitStack()
    wqkv_bf = scB.enter_context(nc.sbuf_tensor("wqkv_bf", [128, DC * 3 * D], BF16, side="right"))
    r1neg = scB.enter_context(nc.sbuf_tensor("r1neg", [1, 3 * D], BF16, side="right"))
    r1b = scB.enter_context(nc.sbuf_tensor("r1b", [1, 3 * D], BF16, side="right"))

    W("sync", "cCONST", nconst)
    ring_i = 0

    WSLOT = ("cWIN", "cWINB", "cWINC")

    def stream_w(src_ap, width, cast_fn, rows_n=128, eng="vector"):
        nonlocal ring_i
        i = ring_i
        ring_i += 1
        sl = (i % 3) * 1024
        sw = WSLOT[i % 3]
        if i >= 3:
            W("sync", "cWCAST", i - 2)
        dma("sync", wring[0:rows_n, sl:sl + width], src_ap, sw)
        W(eng, sw, sems[sw].n)
        I(eng, lambda e, sl=sl: cast_fn(e, sl), "cWCAST")

    def emit_wqkv_piece(j):
        kc, p3 = divmod(j, 3)
        stream_w(wqkv[j * 128:(j + 1) * 128, :],
                 1024,
                 lambda e, sl, kc=kc, p3=p3: e.activation(
                     wqkv_bf[:, kc * 3 * D + p3 * 1024:
                             kc * 3 * D + (p3 + 1) * 1024],
                     wring[:, sl:sl + 1024], ACT.Identity,
                     bias=0.0, scale=1.0), eng="scalar")

    ph1 = ExitStack()
    xstage = ph1.enter_context(nc.sbuf_tensor("xstage", [128, 2 * D], F32, side="right"))
    xtmr = ph1.enter_context(nc.sbuf_tensor("xtmr", [128, 2 * D], BF16, side="right"))
    psTrX = [ph1.enter_context(nc.psum_tensor(f"psTrX{j}", [128, 128], BF16))
             for j in range(2)]

    W("vector", "cCONST", nconst)
    for tt in range(TT):
        sl = (tt % 2) * D
        if tt >= 2:
            W("sync", "cXCAST", tt - 1)
        sx = "cXIN" if tt % 2 == 0 else "cXINB"
        dma("sync", xstage[:, sl:sl + D], xin[tt * 128:(tt + 1) * 128, :], sx)
        W("vector", sx, sems[sx].n)
        if tt >= 2:
            W("vector", "cXTR", (tt - 1) * 8)
        I("vector", lambda e, sl=sl: e.tensor_copy(xtmr[:, sl:sl + D],
                                                   xstage[:, sl:sl + D]),
          "cXCAST")
        for c in range(DC):
            m = tt * 8 + c
            W("tensor", "cXCAST", tt + 1)
            if m >= 2:
                W("tensor", "cXTRE", m - 1)
            I("tensor", lambda e, m=m, sl=sl, c=c:
              e.transpose(psTrX[m % 2][:],
                          xtmr[:, sl + c * 128:sl + (c + 1) * 128], ident[:]),
              "cXTR")
            W("vector", "cXTR", m + 1)
            I("vector", lambda e, m=m, c=c, tt=tt:
              e.tensor_copy(
                  x_fm[:, c * SL + tt * 128:c * SL + (tt + 1) * 128],
                  psTrX[m % 2][:]), "cXTRE")
    ph1.close()
    nxtre = sems["cXTRE"].n
    if debug:
        W("sync", "cXTRE", nxtre)
        dma("sync", dbg["d_xfm_e"], x_fm[:], "cDBG")

    ph1b = ExitStack()
    sqs = ph1b.enter_context(nc.sbuf_tensor("sqs1", [128, 2 * SL], BF16))
    psSt = [ph1b.enter_context(nc.psum_tensor(f"psSt{j}", [1, SL], F32))
            for j in range(2)]
    W("vector", "cXTRE", nxtre)
    W("tensor", "cXTRE", nxtre)
    for qc in range(4):
        for c in range(DC):
            I("tensor", lambda e, qc=qc, c=c:
              e.matmul(psSt[0][:, qc * 512:(qc + 1) * 512], onescol[:],
                       x_fm[:, c * SL + qc * 512:c * SL + (qc + 1) * 512],
                       start=(c == 0), stop=(c == DC - 1)))
    for c in range(DC):
        if c >= 2:
            W("vector", "cSQMM", c - 1)
        I("vector", lambda e, c=c, sl=(c % 2) * SL:
          e.tensor_mul(sqs[:, sl:sl + SL], x_fm[:, c * SL:(c + 1) * SL],
                       x_fm[:, c * SL:(c + 1) * SL]), "cSQV")
        W("tensor", "cSQV", c + 1)
        for qc in range(4):
            I("tensor", lambda e, qc=qc, c=c, sl=(c % 2) * SL:
              e.matmul(psSt[1][:, qc * 512:(qc + 1) * 512], onescol[:],
                       sqs[:, sl + qc * 512:sl + (qc + 1) * 512],
                       start=(c == 0), stop=(c == DC - 1)),
              "cSQMM" if qc == 3 else None)
    I("tensor", lambda e: e.nop(), "cSTATS")
    W("vector", "cSTATS", 1)
    I("vector", lambda e: e.tensor_scalar(m_bf[:], psSt[0][:], 1.0 / D,
                                          None, ALU.mult))
    I("vector", lambda e: e.tensor_mul(sqs[0:1, 0:SL], m_bf[:], m_bf[:]))
    I("vector", lambda e: e.scalar_tensor_tensor(psSt[0][:], psSt[1][:],
                                                 1.0 / D, sqs[0:1, 0:SL], ALU.mult,
                                                 ALU.subtract), "cNAR")
    W("scalar", "cCONST", nconst)
    W("scalar", "cNAR", 1)
    I("scalar", lambda e: e.activation(invr_bf[:], psSt[0][:], ACT.Sqrt,
                                       bias=consts[0:1, 1:2], scale=1.0), "cNAR")
    W("vector", "cNAR", 2)
    def _recst_cNAR(e):
        with nc.allow_low_precision("LN rstd in bf16; 0.4% relative scale "
                                    "noise is far inside the native gate"):
            return e.reciprocal(rstd_bf[:], invr_bf[:])
    I("vector", _recst_cNAR, "cNAR")
    ph1b.close()

    ph1c = ExitStack()
    psB = ph1c.enter_context(nc.psum_tensor("psB", [128, SL], F32))
    rbc1 = ExitStack()
    rbc1_t = rbc1.enter_context(nc.sbuf_tensor("rbc1", [128, SL], BF16, side="right"))
    W("tensor", "cNAR", 3)
    for qc in range(4):
        I("tensor", lambda e, qc=qc:
          e.matmul(psB[:, qc * 512:(qc + 1) * 512], onesrow[0:1, 0:128],
                   rstd_bf[0:1, qc * 512:(qc + 1) * 512], start=True, stop=True),
          "cBC")
    W("vector", "cBC", 4)
    I("vector", lambda e: e.tensor_copy(rbc1_t[:], psB[:]), "cBCE")
    ph1c.close()

    scC = ExitStack()
    qk_fm = scC.enter_context(nc.sbuf_tensor("qk_fm", [128, 16 * SL], BF16))
    scV = ExitStack()
    v_fm = scV.enter_context(nc.sbuf_tensor("v_fm", [128, 8 * SL], BF16))
    for j in range(24):
        emit_wqkv_piece(j)
    nwq = sems["cWCAST"].n

    for p3 in range(3):
        stream_w(brow[0:1, p3 * 1024:(p3 + 1) * 1024], 1024,
                 lambda e, sl, p3=p3: e.tensor_copy(
                     r1b[0:1, p3 * 1024:(p3 + 1) * 1024],
                     wring[0:1, sl:sl + 1024]), rows_n=1)
    stream_w(brow[1:2, 0:1024], 1024,
             lambda e, sl: e.tensor_copy(borow_bf[0:1, 0:1024],
                                         wring[0:1, sl:sl + 1024]), rows_n=1)

    ph2cs = ExitStack()
    psCS = ph2cs.enter_context(nc.psum_tensor("psCS", [1, 512], F32))
    W("tensor", "cWCAST", nwq)
    for piece in range(6):
        if piece >= 1:
            W("tensor", "cCSV", piece)
        for kc in range(DC):
            I("tensor", lambda e, piece=piece, kc=kc:
              e.matmul(psCS[:], onescol[:],
                       wqkv_bf[:, kc * 3 * D + piece * 512:
                               kc * 3 * D + (piece + 1) * 512],
                       start=(kc == 0), stop=(kc == DC - 1)),
              "cCST" if kc == DC - 1 else None)
        W("vector", "cCST", piece + 1)
        I("vector", lambda e, piece=piece:
          e.tensor_scalar(r1neg[0:1, piece * 512:(piece + 1) * 512], psCS[:],
                          -1.0, None, ALU.mult), "cCSV")
    I("vector", lambda e: e.tensor_copy(r1neg[0:1, 0:1], r1neg[0:1, 0:1]), "cCSV")
    ncsv = sems["cCSV"].n
    ph2cs.close()

    ph2mm = ExitStack()
    psQK = [ph2mm.enter_context(nc.psum_tensor(f"psQK{j}", [128, 512], F32))
            for j in range(4)]
    W("tensor", "cCSV", ncsv)
    W("tensor", "cBCE", 1)
    g = 0
    for ec in range(EC):
        for tq in range(4):
            bank = g % 4
            if g >= 4:
                W("tensor", "cQKEV", g - 3)
            for kc in range(DC):
                I("tensor", lambda e, bank=bank, ec=ec, tq=tq, kc=kc:
                  e.matmul(psQK[bank][:],
                           wqkv_bf[:, kc * 3 * D + ec * 128:
                                   kc * 3 * D + (ec + 1) * 128],
                           x_fm[:, kc * SL + tq * 512:kc * SL + (tq + 1) * 512],
                           start=(kc == 0), stop=False))
            I("tensor", lambda e, bank=bank, ec=ec, tq=tq:
              e.matmul(psQK[bank][:], r1neg[0:1, ec * 128:(ec + 1) * 128],
                       m_bf[0:1, tq * 512:(tq + 1) * 512],
                       start=False, stop=False))
            I("tensor", lambda e, bank=bank, ec=ec, tq=tq:
              e.matmul(psQK[bank][:], r1b[0:1, ec * 128:(ec + 1) * 128],
                       invr_bf[0:1, tq * 512:(tq + 1) * 512],
                       start=False, stop=True), "cQKMM")
            W("vector", "cQKMM", g + 1)
            if ec < 16:
                dstap = lambda ec=ec, tq=tq: qk_fm[:, ec * SL + tq * 512:
                                                   ec * SL + (tq + 1) * 512]
            else:
                dstap = lambda ec=ec, tq=tq: v_fm[:, (ec - 16) * SL + tq * 512:
                                                  (ec - 16) * SL + (tq + 1) * 512]
            I("vector", lambda e, bank=bank, d=dstap, tq=tq:
              e.tensor_mul(d(), psQK[bank][:],
                           rbc1_t[:, tq * 512:(tq + 1) * 512]), "cQKEV")
            g += 1
    nqkev = sems["cQKEV"].n
    if debug:
        W("sync", "cQKEV", nqkev)
        dma("sync", dbg["d_xfm_m"], x_fm[:], "cDBG")
    ph2mm.close()
    rbc1.close()
    scB.close()

    scD = ExitStack()
    v_aug = scD.enter_context(nc.sbuf_tensor("v_aug", [128, 16 * 1040], BF16, side="right"))
    pt_c = scD.enter_context(nc.sbuf_tensor("pt_c", [128, 3 * 1024], BF16,
                                            side="right"))
    ph2tr = ExitStack()
    psTrV = [ph2tr.enter_context(nc.psum_tensor(f"psTrV{j}", [128, 128], BF16))
             for j in range(2)]
    W("vector", "cQKEV", 96)
    for tt in range(TT):
        I("vector", lambda e, tt=tt: e.memset(
            v_aug[:, tt * 1040:(tt + 1) * 1040]
            .rearrange("p (h c) -> p h c", h=16, c=65)[:, :, 64:65], 1.0))
    for c in range(8):
        ecv = 16 + c
        W("tensor", "cQKEV", (ecv + 1) * 4)
        for tt in range(TT):
            m = c * TT + tt
            if m >= 2:
                W("tensor", "cVTRE", m - 1)
            I("tensor", lambda e, m=m, c=c, tt=tt:
              e.transpose(psTrV[m % 2][:],
                          v_fm[:, c * SL + tt * 128:c * SL + (tt + 1) * 128],
                          ident[:]), "cVTR")
            W("vector", "cVTR", m + 1)
            I("vector", lambda e, m=m, c=c, tt=tt:
              e.tensor_copy(
                  v_aug[:, tt * 1040 + (2 * c) * 65:
                        tt * 1040 + (2 * c) * 65 + 130]
                  .rearrange("p (h c) -> p h c", h=2, c=65)[:, :, 0:64],
                  psTrV[m % 2][:].rearrange("p (h c) -> p h c", h=2, c=64)),
              "cVTRE")
    nvtre = sems["cVTRE"].n
    if debug:
        W("sync", "cVTRE", nvtre)
        dma("sync", dbg["d_xfm_v"], x_fm[:], "cDBG")
    ph2tr.close()
    scV.close()

    ph3 = ExitStack()
    psS = [ph3.enter_context(nc.psum_tensor(f"psS{j}", [128, 1024], F32))
           for j in range(2)]
    psO = [ph3.enter_context(nc.psum_tensor(f"psO{j}", [128, 1024], F32))
           for j in range(2)]
    scE = ExitStack()
    attn_fm = scE.enter_context(nc.sbuf_tensor("attn_fm", [128, 8 * SL], BF16))
    scE2 = ExitStack()
    sums_bf = scE2.enter_context(nc.sbuf_tensor("sums_bf", [16, SL], BF16))
    invb = scE2.enter_context(nc.sbuf_tensor("invb", [128, SL], BF16))
    invrow = [scE2.enter_context(nc.sbuf_tensor(f"invrow{j}", [1, SL], BF16))
              for j in range(2)]
    scE3 = ExitStack()
    srow2 = [scE3.enter_context(nc.sbuf_tensor(f"srow{s}", [1, SL], BF16))
             for s in range(2)]

    W("tensor", "cVTRE", nvtre)
    W("scalar", "cVTRE", nvtre)
    CH = 8 * 2 * KT * 2

    def _cqk(s):
        c, r = divmod(s, 2 * KT)
        qh, kt = divmod(r, KT)
        return c, qh, kt

    def emit_qk(k):
        s, qq = k // 2, k % 2
        c, qh, kt = _cqk(s)
        if k >= 2:
            W("tensor", "cEXP", k - 1)
        qcol = c * SL + qh * 1024 + qq * 512
        kcol = (8 + c) * SL + kt * 128
        I("tensor", lambda e, k=k, qcol=qcol, kcol=kcol:
          e.matmul(psS[k % 2][:, 0:512],
                   qk_fm[0:64, kcol:kcol + 128],
                   qk_fm[0:64, qcol:qcol + 512],
                   start=True, stop=True, tile_position=(0, 0)))
        I("tensor", lambda e, k=k, qcol=qcol, kcol=kcol:
          e.matmul(psS[k % 2][:, 512:1024],
                   qk_fm[64:128, kcol:kcol + 128],
                   qk_fm[64:128, qcol:qcol + 512],
                   start=True, stop=True, tile_position=(64, 0)), "cQKA")

    def emit_exp(k):
        W("scalar", "cQKA", k + 1)
        if k >= 3:
            W("scalar", "cPV", 2 * (k - 2))
        I("scalar", lambda e, k=k:
          e.activation(pt_c[:, (k % 3) * 1024:(k % 3) * 1024 + 1024],
                       psS[k % 2][:], ACT.Exp, bias=consts[:, 0:1],
                       scale=0.125), "cEXP")

    def emit_pv(k):
        s, qq = k // 2, k % 2
        c, qh, kt = _cqk(s)
        m = 2 * c + qh
        W("tensor", "cEXP", k + 1)
        for sub in range(2):
            if kt == 0 and qq == 0 and m >= 1:
                W("tensor", "cOEV", 4 * (m - 1) + 2 * (sub + 1))
            I("tensor", lambda e, sub=sub, qq=qq, kt=kt, c=c, k=k:
              e.matmul(psO[sub][0:65, qq * 512:(qq + 1) * 512],
                       v_aug[:, kt * 1040 + (2 * c + sub) * 65:
                             kt * 1040 + (2 * c + sub) * 65 + 65],
                       pt_c[:, (k % 3) * 1024 + sub * 512:
                             (k % 3) * 1024 + (sub + 1) * 512],
                       start=(kt == 0), stop=(kt == KT - 1)), "cPV")

    def emit_evac(k):
        s = k // 2
        c, qh, kt = _cqk(s)
        m = 2 * c + qh
        W("vector", "cPV", 2 * (k + 1))
        if qh == 0 and c >= 1:
            W("vector", "cSUM", 32 * c)
        for sub in range(2):
            I("vector", lambda e, sub=sub, c=c, qh=qh:
              e.tensor_copy(attn_fm[sub * 64:sub * 64 + 64,
                                    c * SL + qh * 1024:c * SL + qh * 1024 + 1024],
                            psO[sub][0:64, :]), "cOEV")
            I("vector", lambda e, sub=sub, qh=qh:
              e.tensor_copy(srow2[sub][0:1, qh * 1024:qh * 1024 + 1024],
                            psO[sub][64:65, :]), "cOEV")
        if qh == 1:
            W("sync", "cOEV", 4 * (m + 1))
            for sub in range(2):
                dma("sync", hsum[2 * c + sub:2 * c + sub + 1, :],
                    srow2[sub][:], "cSUM")

    emit_qk(0)
    for k in range(CH):
        emit_exp(k)
        if k + 1 < CH:
            emit_qk(k + 1)
        if k >= 2:
            emit_pv(k - 2)
            if (k - 2) % 2 == 1 and _cqk((k - 2) // 2)[2] == KT - 1:
                emit_evac(k - 2)
    for k in (CH - 2, CH - 1):
        emit_pv(k)
        if k % 2 == 1 and _cqk(k // 2)[2] == KT - 1:
            emit_evac(k)
    scE3.close()
    scD.close()
    scG = ExitStack()
    x2_fm = scG.enter_context(nc.sbuf_tensor("x2_fm", [128, DC * SL], BF16, side="right"))
    scF = ExitStack()
    wout_bf = scF.enter_context(nc.sbuf_tensor("wout_bf", [128, DC * D], BF16, side="right"))
    for kc in range(DC):
        stream_w(wout[kc * 128:(kc + 1) * 128, :], 1024,
                 lambda e, sl, kc=kc: e.activation(
                     wout_bf[:, kc * D:(kc + 1) * D], wring[:, sl:sl + 1024],
                     ACT.Identity, bias=0.0, scale=1.0), eng="scalar")
    nwo = sems["cWCAST"].n

    W("sync", "cSUM", sems["cSUM"].n)
    W("sync", "cSUMB", sems["cSUMB"].n)
    dma("sync", sums_bf[:], hsum, "cSUM")
    W("vector", "cSUM", sems["cSUM"].n)
    def _recip_lp(e):
        with nc.allow_low_precision("softmax denominator reciprocal in bf16; "
                                    "0.4% relative is far inside native tol"):
            return e.reciprocal(sums_bf[:], sums_bf[:])
    I("vector", _recip_lp, "cINV")
    W("sync", "cINV", 1)
    dma("sync", hinv, sums_bf[:], "cHIV")
    for hp in range(HEADS // 2):
        for sub in range(2):
            h = 2 * hp + sub
            ib = h
            if ib >= 2:
                W("sync", "cBCH", ib - 1)
            W("sync", "cHIV", 16)
            si = "cIRW" if ib % 2 == 0 else "cIRWB"
            dma("sync", invrow[ib % 2][:], hinv[h:h + 1, :], si)
            W("tensor", si, sems[si].n)
            W("tensor", "cPV", sems["cPV"].n)
            if hp >= 1:
                W("tensor", "cIBC", hp)
            for qq in range(4):
                I("tensor", lambda e, sub=sub, qq=qq, ib=ib:
                  e.matmul(psS[qq // 2][sub * 64:sub * 64 + 64,
                                        (qq % 2) * 512:(qq % 2) * 512 + 512],
                           onesrow[0:1, 0:64],
                           invrow[ib % 2][0:1, qq * 512:(qq + 1) * 512],
                           start=True, stop=True), "cBCH" if qq == 3 else None)
        W("vector", "cBCH", 2 * hp + 2)
        I("vector", lambda e: e.tensor_copy(invb[:, 0:1024], psS[0][:]))
        I("vector", lambda e: e.tensor_copy(invb[:, 1024:2048], psS[1][:]), "cIBC")
        I("vector", lambda e, hp=hp:
          e.tensor_mul(attn_fm[:, hp * SL:(hp + 1) * SL],
                       attn_fm[:, hp * SL:(hp + 1) * SL],
                       invb[:]), "cNRM")
    nnrm = sems["cNRM"].n
    if debug:
        W("sync", "cNRM", nnrm)
        dma("sync", dbg["d_xfm_n"], x_fm[:], "cDBG")
    ph3.close()
    scE2.close()

    ph4 = ExitStack()
    psP = [ph4.enter_context(nc.psum_tensor(f"psP{j}", [128, 512], F32))
           for j in range(4)]
    W("tensor", "cWCAST", nwo)
    W("tensor", "cNRM", nnrm)
    g = 0
    for oc in range(DC):
        for tq in range(4):
            bank = g % 4
            if g >= 4:
                W("tensor", "cX2", g - 3)
            for kc in range(DC):
                I("tensor", lambda e, bank=bank, oc=oc, tq=tq, kc=kc:
                  e.matmul(psP[bank][:],
                           wout_bf[:, kc * D + oc * 128:kc * D + (oc + 1) * 128],
                           attn_fm[:, kc * SL + tq * 512:kc * SL + (tq + 1) * 512],
                           start=(kc == 0), stop=False))
            I("tensor", lambda e, bank=bank, oc=oc, tq=tq:
              e.matmul(psP[bank][:], borow_bf[0:1, oc * 128:(oc + 1) * 128],
                       onesrow[0:1, 0:512],
                       start=False, stop=True), "cPMM")
            W("vector", "cPMM", g + 1)
            I("vector", lambda e, bank=bank, oc=oc, tq=tq:
              e.tensor_tensor(
                  x2_fm[:, oc * SL + tq * 512:oc * SL + (tq + 1) * 512],
                  psP[bank][:],
                  x_fm[:, oc * SL + tq * 512:oc * SL + (tq + 1) * 512],
                  ALU.add), "cX2")
            g += 1
    nx2 = sems["cX2"].n
    ph4.close()
    scE.close()
    scC.close()
    scX.close()
    scF.close()

    if debug:
        W("sync", "cX2", nx2)
    scI = ExitStack()
    hq8 = scI.enter_context(nc.sbuf_tensor("hq8", [128, NC1 * 512], FP8))
    otm = scI.enter_context(nc.sbuf_tensor("otm", [128, 4 * D], F32))
    w1q8 = scI.enter_context(nc.sbuf_tensor("w1q8", [128, DC * 4096], FP8))
    identf = scI.enter_context(nc.sbuf_tensor("identf", [128, 128], F32))
    w2q8 = scI.enter_context(nc.sbuf_tensor("w2q8", [128, NC1 * 1024], FP8))
    hbf = scI.enter_context(nc.sbuf_tensor("hbf", [128, 2 * 512], BF16))
    y2bf = scI.enter_context(nc.sbuf_tensor("y2bf", [128, 3 * 512], BF16))
    t1s = scI.enter_context(nc.sbuf_tensor("t1s", [128, 512], F32))
    ofm = scI.enter_context(nc.sbuf_tensor("ofm", [128, 3 * 512], F32))
    mring = scI.enter_context(nc.sbuf_tensor("mring", [128, 3 * 512], BF16))



    W("sync", "cQKA", sems["cQKA"].n)
    dma("sync", identf[:], idf32, "cIDF")

    W("scalar", "cQKA", sems["cQKA"].n)
    w1base = sems["cWCAST"].n
    for q4 in range(4):
        for kc in range(DC):
            stream_w(w1t[(q4 * 8 + kc) * 128:(q4 * 8 + kc + 1) * 128, :],
                     1024,
                     lambda e, sl, kc=kc, q4=q4: e.activation(
                         w1q8[:, kc * 4096 + q4 * 1024:
                              kc * 4096 + (q4 + 1) * 1024],
                         wring[:, sl:sl + 1024], ACT.Identity,
                         bias=0.0, scale=1.0), eng="scalar")
    nw1 = sems["cWCAST"].n
    w2base = ring_i
    w2win = []
    for k in range(NC1):
        i = ring_i
        ring_i += 1
        sl = (i % 3) * 1024
        sw = WSLOT[i % 3]
        j = i - 3
        if j >= 0:
            if j < w2base:
                W("sync", "cWCAST", j + 1)
            else:
                W("sync", "cW2V", j + 1 - w2base)
        dma("sync", wring[:, sl:sl + 1024],
            w2t[k * 128:(k + 1) * 128, 0:1024], sw)
        w2win.append((sw, sems[sw].n))

    scH = ExitStack()
    sqs = scH.enter_context(nc.sbuf_tensor("sqs2", [128, 2 * SL], BF16))
    m2bc = scH.enter_context(nc.sbuf_tensor("m2bc", [128, SL], BF16))
    r2bc = scH.enter_context(nc.sbuf_tensor("r2bc", [128, SL], BF16))
    xh2q8 = scH.enter_context(nc.sbuf_tensor("xh2q8", [128, DC * SL], FP8))
    ph4b = ExitStack()
    psSt2 = [ph4b.enter_context(nc.psum_tensor(f"psSt2{j}", [1, SL], F32))
             for j in range(2)]
    W("tensor", "cX2", nx2)
    W("vector", "cX2", nx2)
    for qc in range(4):
        for c in range(DC):
            I("tensor", lambda e, qc=qc, c=c:
              e.matmul(psSt2[0][:, qc * 512:(qc + 1) * 512], onescol[:],
                       x2_fm[:, c * SL + qc * 512:c * SL + (qc + 1) * 512],
                       start=(c == 0), stop=(c == DC - 1)))
    for c in range(DC):
        if c >= 2:
            W("vector", "cSQMM2", c - 1)
        I("vector", lambda e, c=c, sl=(c % 2) * SL:
          e.tensor_mul(sqs[:, sl:sl + SL], x2_fm[:, c * SL:(c + 1) * SL],
                       x2_fm[:, c * SL:(c + 1) * SL]), "cSQV2")
        W("tensor", "cSQV2", c + 1)
        for qc in range(4):
            I("tensor", lambda e, qc=qc, c=c, sl=(c % 2) * SL:
              e.matmul(psSt2[1][:, qc * 512:(qc + 1) * 512], onescol[:],
                       sqs[:, sl + qc * 512:sl + (qc + 1) * 512],
                       start=(c == 0), stop=(c == DC - 1)),
              "cSQMM2" if qc == 3 else None)
    I("tensor", lambda e: e.nop(), "cSTATS2")
    W("vector", "cSTATS2", 1)
    I("vector", lambda e: e.tensor_scalar(m_bf[:], psSt2[0][:], 1.0 / D,
                                          None, ALU.mult))
    I("vector", lambda e: e.tensor_mul(sqs[0:1, 0:SL], m_bf[:], m_bf[:]))
    I("vector", lambda e: e.scalar_tensor_tensor(psSt2[0][:], psSt2[1][:],
                                                 1.0 / D, sqs[0:1, 0:SL], ALU.mult,
                                                 ALU.subtract), "cNAR2")
    W("scalar", "cCONST", nconst)
    W("scalar", "cNAR2", 1)
    I("scalar", lambda e: e.activation(invr_bf[:], psSt2[0][:], ACT.Sqrt,
                                       bias=consts[0:1, 1:2], scale=1.0), "cNAR2")
    W("vector", "cNAR2", 2)
    def _recst_cNAR2(e):
        with nc.allow_low_precision("LN rstd in bf16; 0.4% relative scale "
                                    "noise is far inside the native gate"):
            return e.reciprocal(rstd_bf[:], invr_bf[:])
    I("vector", _recst_cNAR2, "cNAR2")
    ph4b.close()

    ph4c = ExitStack()
    psB2 = ph4c.enter_context(nc.psum_tensor("psB2", [128, SL], F32))
    W("tensor", "cNAR2", 3)
    for qc in range(4):
        I("tensor", lambda e, qc=qc:
          e.matmul(psB2[:, qc * 512:(qc + 1) * 512], onesrow[0:1, 0:128],
                   m_bf[0:1, qc * 512:(qc + 1) * 512], start=True, stop=True),
          "cBC2" if qc == 3 else None)
    W("vector", "cBC2", 1)
    I("vector", lambda e: e.tensor_copy(m2bc[:], psB2[:]), "cBC2E")
    W("tensor", "cBC2E", 1)
    for qc in range(4):
        I("tensor", lambda e, qc=qc:
          e.matmul(psB2[:, qc * 512:(qc + 1) * 512], onesrow[0:1, 0:128],
                   rstd_bf[0:1, qc * 512:(qc + 1) * 512], start=True, stop=True),
          "cBC2" if qc == 3 else None)
    W("vector", "cBC2", 2)
    I("vector", lambda e: e.tensor_copy(r2bc[:], psB2[:]), "cBC2E")
    ph4c.close()

    for c in range(DC):
        sl = (c % 2) * SL
        I("vector", lambda e, c=c, sl=sl:
          e.tensor_sub(sqs[:, sl:sl + SL], x2_fm[:, c * SL:(c + 1) * SL],
                       m2bc[:]))
        I("vector", lambda e, c=c, sl=sl:
          e.tensor_mul(xh2q8[:, c * SL:(c + 1) * SL], sqs[:, sl:sl + SL],
                       r2bc[:]), "cXH2")
    nxh2 = sems["cXH2"].n

    ph5 = ExitStack()
    psF = [ph5.enter_context(nc.psum_tensor(f"psF{j}", [128, 512], F32))
           for j in range(3)]
    psG = [ph5.enter_context(nc.psum_tensor(f"psG{j}", [128, 512], F32))
           for j in range(3)]
    psTrO = [ph5.enter_context(nc.psum_tensor(f"psTrO{j}", [128, 128], F32))
             for j in range(2)]

    W("tensor", "cXH2", nxh2)
    gf = 0
    gg = 0
    tro = 0
    pend = []

    def emit_oasm(p):
        nonlocal tro
        ts_p, oc_p = divmod(p, DC)
        W("tensor", "cIDF", 16)
        W("tensor", "cOASM", p + 1)
        for j in range(4):
            tt = ts_p * 4 + j
            if tro >= 2:
                W("tensor", "cOCP", tro - 1)
            I("tensor", lambda e, tro=tro, j=j, p=p:
              e.transpose(psTrO[tro % 2][:],
                          ofm[:, (p % 3) * 512 + j * 128:
                              (p % 3) * 512 + (j + 1) * 128],
                          identf[:]), "cOTR")
            W("vector", "cOTR", tro + 1)
            if oc_p == 0 and tt >= 4:
                so = ("cOUT", "cOUT1", "cOUT2", "cOUT3")[tt % 4]
                W("vector", so, 16 * ts_p)
            I("vector", lambda e, tro=tro, tt=tt, oc_p=oc_p:
              e.tensor_copy(otm[:, (tt % 4) * D + oc_p * 128:
                            (tt % 4) * D + (oc_p + 1) * 128],
                            psTrO[tro % 2][:]), "cOCP")
            tro += 1
        if oc_p == DC - 1:
            for j in range(4):
                tt = ts_p * 4 + j
                W("sync", "cOCP", (ts_p * DC + 7) * 4 + j + 1)
                dma("sync", yout[tt * 128:(tt + 1) * 128, :],
                    otm[:, (tt % 4) * D:(tt % 4) * D + D],
                    ("cOUT", "cOUT1", "cOUT2", "cOUT3")[tt % 4])

    for ts in range(TSN):
        for ncc in range(NC1):
            bank = gf % 3
            if ts == 0:
                W("tensor", "cWCAST", w1base + 8 * (ncc // 8 + 1))
            if gf >= 3:
                W("tensor", "cFEV", gf - 2)
            for kc in range(DC):
                I("tensor", lambda e, bank=bank, ncc=ncc, ts=ts, kc=kc:
                  e.matmul(psF[bank][:],
                           w1q8[:, kc * 4096 + ncc * 128:kc * 4096 + (ncc + 1) * 128],
                           xh2q8[:, kc * SL + ts * 512:kc * SL + (ts + 1) * 512],
                           start=(kc == 0), stop=(kc == DC - 1)), "cFMM" if kc == DC - 1 else None)
            W("scalar", "cBQ", 1)
            W("scalar", "cFMM", gf + 1)
            if gf >= 2:
                W("scalar", "cHQ", gf - 1)
            hs = gf % 2
            I("scalar", lambda e, bank=bank, hs=hs, ncc=ncc:
              e.activation(hbf[:, hs * 512:(hs + 1) * 512], psF[bank][:],
                           ACT.Relu, bias=b1q[:, ncc:ncc + 1], scale=1.0),
              "cFEV")
            W("vector", "cFEV", gf + 1)
            if ts >= 1:
                W("vector", "cGMM", ts * 8)
            I("vector", lambda e, hs=hs, ncc=ncc:
              e.tensor_copy(hq8[:, ncc * 512:(ncc + 1) * 512],
                            hbf[:, hs * 512:(hs + 1) * 512]), "cHQ")
            if ts == 0:
                sl2 = ((w2base + ncc) % 3) * 1024
                W("vector", w2win[ncc][0], w2win[ncc][1])
                I("vector", lambda e, ncc=ncc, sl2=sl2:
                  e.tensor_copy(w2q8[:, ncc * 1024:(ncc + 1) * 1024],
                                wring[:, sl2:sl2 + 1024]), "cW2V")
            gf += 1
        for oc in range(DC):
            gb = gg % 3
            if gg >= 3:
                W("sync", "cMUSE", gg - 2)
            sm = ("cMIN", "cMIN1", "cMIN2")[gg % 3]
            dma("sync", mring[:, (gg % 3) * 512:(gg % 3 + 1) * 512],
                maskfm[oc * 128:(oc + 1) * 128, ts * 512:(ts + 1) * 512], sm)
            nmin = (sm, sems[sm].n)
            W("tensor", "cHQ", (ts + 1) * NC1)
            if gg == 0:
                W("tensor", "cW2V", NC1)
            if gg >= 3:
                W("tensor", "cGEV", gg - 2)
            for ncc in range(NC1):
                I("tensor", lambda e, gb=gb, ncc=ncc, oc=oc:
                  e.matmul(psG[gb][:],
                           w2q8[:, ncc * 1024 + oc * 128:
                                ncc * 1024 + (oc + 1) * 128],
                           hq8[:, ncc * 512:(ncc + 1) * 512],
                           start=(ncc == 0), stop=(ncc == NC1 - 1)),
                  "cGMM" if ncc == NC1 - 1 else None)
            W("scalar", "cGMM", gg + 1)
            if gg >= 3:
                W("scalar", "cMUSE", gg - 2)
            I("scalar", lambda e, gb=gb, gg=gg, oc=oc:
              e.activation(y2bf[:, (gg % 3) * 512:(gg % 3 + 1) * 512],
                           psG[gb][:], ACT.Identity,
                           bias=b2q[:, oc:oc + 1], scale=1.0), "cGEV")
            W("vector", "cGEV", gg + 1)
            W("vector", nmin[0], nmin[1])
            I("vector", lambda e, gg=gg:
              e.tensor_mul(t1s[:], y2bf[:, (gg % 3) * 512:(gg % 3 + 1) * 512],
                           mring[:, (gg % 3) * 512:(gg % 3 + 1) * 512]), "cMUSE")
            if gg >= 3:
                W("vector", "cOTR", 4 * (gg - 2))
            I("vector", lambda e, gg=gg, oc=oc, ts=ts:
              e.tensor_tensor(ofm[:, (gg % 3) * 512:(gg % 3 + 1) * 512],
                              t1s[:],
                              x2_fm[:, oc * SL + ts * 512:oc * SL + (ts + 1) * 512],
                              ALU.add), "cOASM")
            pend.append(gg)
            if len(pend) > 2:
                emit_oasm(pend.pop(0))
            gg += 1
    while pend:
        emit_oasm(pend.pop(0))
    ph5.close()
    scH.close()
    scI.close()
    scG.close()

    if debug:
        W("sync", "cOUT", sems["cOUT"].n)
        W("sync", "cDBG", sems["cDBG"].n)
        dma("sync", dbg["d_xfm"], x_fm[:], "cOUT")
        dma("sync", dbg["d_x2fm"], x2_fm[:], "cOUT")
        dma("sync", dbg["d_hq"], hq8[:].bitcast(mybir.dt.uint8), "cOUT")
        dma("sync", dbg["d_xh2"], xh2q8[:].bitcast(mybir.dt.uint8), "cOUT")
        dma("sync", dbg["d_w1q"], w1q8[:].bitcast(mybir.dt.uint8), "cOUT")
        dma("sync", dbg["d_mbf"], m_bf[:], "cOUT")
        dma("sync", dbg["d_b1q"], b1q[:], "cOUT")
        dma("sync", dbg["d_b1c"], b1csb[:], "cOUT")
        dma("sync", dbg["d_rstd"], rstd_bf[:], "cOUT")
    for so in ("cOUT", "cOUT1", "cOUT2", "cOUT3"):
        W("gpsimd", so, sems[so].n)

    with nc.Block() as blk:
        @blk.sync
        def _(e):
            for fn in prog["sync"]:
                fn(e)

        @blk.scalar
        def _(e):
            for fn in prog["scalar"]:
                fn(e)

        @blk.vector
        def _(e):
            for fn in prog["vector"]:
                fn(e)

        @blk.tensor
        def _(e):
            for fn in prog["tensor"]:
                fn(e)

        @blk.gpsimd
        def _(e):
            for fn in prog["gpsimd"]:
                fn(e)

    perm.close()
    return nc


_NC_CACHE = None


def kernel(L0_attn_x, L0_attn_W_qkv, L0_attn_b_qkv, L0_attn_W_out,
           L0_attn_b_out, L0_ffn_W1, L0_ffn_b1, L0_ffn_W2, L0_ffn_b2):
    global _NC_CACHE
    x = np.asarray(L0_attn_x, dtype=np.float32)
    B = x.shape[0]
    keep = _dropout_keep_mask((B, SL, D))
    maskf = np.where(keep, np.float32(1.25), np.float32(0.0))

    wqkv = np.ascontiguousarray(
        np.asarray(L0_attn_W_qkv, dtype=np.float32)
        .reshape(8, 128, 3, 1024).transpose(0, 2, 1, 3).reshape(24 * 128, 1024))
    wout = np.ascontiguousarray(np.asarray(L0_attn_W_out, dtype=np.float32))
    w1t = np.ascontiguousarray(
        np.asarray(L0_ffn_W1, dtype=np.float32).T
        .reshape(8, 128, 4, 1024).transpose(2, 0, 1, 3).reshape(32 * 128, 1024))
    w2t = np.ascontiguousarray(np.asarray(L0_ffn_W2, dtype=np.float32).T)
    bq = np.asarray(L0_attn_b_qkv, dtype=np.float32)
    bo = np.asarray(L0_attn_b_out, dtype=np.float32)
    b1 = np.asarray(L0_ffn_b1, dtype=np.float32)
    b2 = np.asarray(L0_ffn_b2, dtype=np.float32)

    brow = np.zeros((2, 3 * D), dtype=np.float32)
    brow[0, :] = bq
    brow[1, :D] = bo
    b1c = np.ascontiguousarray(b1.reshape(32, 128).T)
    b2c = np.ascontiguousarray(b2.reshape(8, 128).T)
    idbf = np.eye(128, dtype=ml_dtypes.bfloat16)
    idf32 = np.eye(128, dtype=np.float32)

    if _NC_CACHE is None:
        _NC_CACHE = build_nc()
    nc = _NC_CACHE

    in_maps = []
    for b in range(B):
        in_maps.append({
            "xin": np.ascontiguousarray(x[b]),
            "wqkv": wqkv, "wout": wout, "w1t": w1t, "w2t": w2t,
            "brow": brow, "b1c": b1c, "b2c": b2c,
            "idbf": idbf, "idf32": idf32,
            "maskfm": np.ascontiguousarray(
                maskf[b].T.astype(ml_dtypes.bfloat16)),
        })
    res = run_bass_kernel_spmd(nc, in_maps, core_ids=list(range(B)))
    out = np.stack([np.asarray(r["yout"]) for r in res.results], axis=0)
    return out.astype(np.float32)


# version 24 (final)
# speedup vs baseline: 1.0515x; 1.0068x over previous
import numpy as np
import ml_dtypes
import concourse.bass as bass
import concourse.mybir as mybir
from concourse.bass_utils import run_bass_kernel_spmd
from contextlib import ExitStack

F32 = mybir.dt.float32
BF16 = mybir.dt.bfloat16
FP8 = mybir.dt.float8e5
ALU = mybir.AluOpType
ACT = mybir.ActivationFunctionType

SL = 2048
D = 1024
HEADS = 16
TT = 16
DC = 8
EC = 24
KT = 16
NC1 = 32
TSN = 4
EXP_SHIFT = -12.0


def _threefry2x32(k0, k1, x0, x1):
    rot = [13, 15, 26, 6, 17, 29, 16, 24]
    ks0, ks1 = np.uint32(k0), np.uint32(k1)
    ks2 = np.uint32(np.uint32(ks0) ^ np.uint32(ks1) ^ np.uint32(0x1BD11BDA))
    x0 = (x0 + ks0).astype(np.uint32)
    x1 = (x1 + ks1).astype(np.uint32)

    def rotl(v, d):
        return ((v << np.uint32(d)) | (v >> np.uint32(32 - d))).astype(np.uint32)

    ks = [ks0, ks1, ks2]
    for i in range(5):
        r = rot[:4] if i % 2 == 0 else rot[4:]
        for j in range(4):
            x0 = (x0 + x1).astype(np.uint32)
            x1 = rotl(x1, r[j])
            x1 = (x1 ^ x0).astype(np.uint32)
        x0 = (x0 + ks[(i + 1) % 3]).astype(np.uint32)
        x1 = (x1 + ks[(i + 2) % 3] + np.uint32(i + 1)).astype(np.uint32)
    return x0, x1


def _dropout_keep_mask(shape, p_keep=0.8, key=(0, 42)):
    n = int(np.prod(shape))
    idx = np.arange(n, dtype=np.uint64)
    hi = (idx >> np.uint64(32)).astype(np.uint32)
    lo = (idx & np.uint64(0xFFFFFFFF)).astype(np.uint32)
    b0, b1 = _threefry2x32(key[0], key[1], hi, lo)
    bits = (b0 ^ b1).astype(np.uint32)
    fl = ((bits >> np.uint32(9)) | np.uint32(0x3F800000)).view(np.float32) \
        - np.float32(1.0)
    return (fl < np.float32(p_keep)).reshape(shape)


def build_nc(debug=False):
    nc = bass.Bass("TRN2", target_bir_lowering=False,
                   dynamic_dma_scratch_size=512)

    xin = nc.dram_tensor("xin", [SL, D], F32, kind="ExternalInput").ap()
    wqkv = nc.dram_tensor("wqkv", [24 * 128, 1024], F32, kind="ExternalInput").ap()
    wout = nc.dram_tensor("wout", [D, D], F32, kind="ExternalInput").ap()
    w1t = nc.dram_tensor("w1t", [32 * 128, 1024], F32, kind="ExternalInput").ap()
    w2t = nc.dram_tensor("w2t", [4 * D, D], F32, kind="ExternalInput").ap()
    brow = nc.dram_tensor("brow", [2, 3 * D], F32, kind="ExternalInput").ap()
    b1c = nc.dram_tensor("b1c", [128, 32], F32, kind="ExternalInput").ap()
    b2c = nc.dram_tensor("b2c", [128, 8], F32, kind="ExternalInput").ap()
    idbf = nc.dram_tensor("idbf", [128, 128], BF16, kind="ExternalInput").ap()
    idf32 = nc.dram_tensor("idf32", [128, 128], F32, kind="ExternalInput").ap()
    maskfm = nc.dram_tensor("maskfm", [D, SL], BF16, kind="ExternalInput").ap()
    yout = nc.dram_tensor("yout", [SL, D], F32, kind="ExternalOutput").ap()
    hsum = nc.dram_tensor("hsum", [16, SL], BF16, kind="Internal").ap()
    hinv = nc.dram_tensor("hinv", [16, SL], BF16, kind="Internal").ap()
    dbg = {}
    if debug:
        for nm, shp, dt in [("d_xfm", [128, DC * SL], BF16),
                            ("d_xfm_e", [128, DC * SL], BF16),
                            ("d_xfm_m", [128, DC * SL], BF16),
                            ("d_xfm_v", [128, DC * SL], BF16),
                            ("d_xfm_a", [128, DC * SL], BF16),
                            ("d_xfm_n", [128, DC * SL], BF16),
                            ("d_xfm_p", [128, DC * SL], BF16),
                            ("d_x2fm", [128, DC * SL], BF16),
                            ("d_hq", [128, NC1 * 512], mybir.dt.uint8),
                            ("d_xh2", [128, DC * SL], mybir.dt.uint8),
                            ("d_w1q", [128, DC * 4096], mybir.dt.uint8),
                            ("d_mbf", [1, SL], BF16),
                            ("d_b1q", [128, 32], F32),
                            ("d_b1c", [128, 32], F32),
                            ("d_rstd", [1, SL], BF16)]:
            dbg[nm] = nc.dram_tensor(nm, shp, dt, kind="ExternalOutput").ap()

    prog = {k: [] for k in ("sync", "scalar", "vector", "tensor", "gpsimd")}

    def em(eng, fn):
        prog[eng].append(fn)

    sems = {}

    class Sem:
        __slots__ = ("q", "n")

        def __init__(self, q):
            self.q = q
            self.n = 0

    perm = ExitStack()

    def mksem(name):
        sems[name] = Sem(perm.enter_context(nc.semaphore(name)))

    for name in ["cBIN", "cCONST", "cXIN", "cXCAST", "cXTR", "cXTRE",
                 "cSQV", "cSQMM", "cSTATS", "cNAR", "cBC", "cBCE",
                 "cWIN", "cWCAST", "cCST", "cCSV",
                 "cQKMM", "cQKEV", "cVTR", "cVTRE",
                 "cQKA", "cEXP", "cPV", "cOEV", "cINV", "cBCH", "cIBC",
                 "cNRM", "cPMM", "cX2",
                 "cSQV2", "cSQMM2", "cSTATS2", "cNAR2", "cBC2", "cBC2E",
                 "cXH2", "cFMM", "cFEV", "cHQ",
                 "cW2V", "cGMM", "cGEV", "cOTR", "cOASM",
                 "cMIN", "cMUSE", "cOUT", "cOCP", "cIRW", "cSUM", "cIDF", "cDBG", "cHIV", "cBQ",
                 "cSUMB", "cXINB", "cWINB", "cWINC", "cWIND", "cIRWB",
                 "cOUT1", "cOUT2", "cOUT3", "cMIN1", "cMIN2"]:
        mksem(name)

    seenw = {}

    def W(eng, sem, val):
        if val <= 0 or seenw.get((eng, sem), 0) >= val:
            return
        seenw[(eng, sem)] = val
        em(eng, lambda e, s=sems[sem].q, v=val: e.wait_ge(s, v))

    def I(eng, fn, sem=None, k=1):
        if sem is None:
            em(eng, fn)
            return None
        s = sems[sem]
        s.n += k
        em(eng, lambda e, f=fn, q=s.q, kk=k: f(e).then_inc(q, kk))
        return s.n

    def dma(eng, dst, src, sem):
        return I(eng, lambda e, d=dst, s=src: e.dma_start(out=d, in_=s), sem, 16)

    A = perm.enter_context

    ident = A(nc.sbuf_tensor("ident", [128, 128], BF16))
    wring = A(nc.sbuf_tensor("wring", [128, 3 * 1024], F32))
    borow_bf = A(nc.sbuf_tensor("borow_bf", [1, D], BF16))
    onesrow = A(nc.sbuf_tensor("onesrow", [1, 512], BF16))
    rstd_bf = A(nc.sbuf_tensor("rstd_bf", [1, SL], BF16))
    m_bf = A(nc.sbuf_tensor("m_bf", [1, SL], BF16))
    invr_bf = A(nc.sbuf_tensor("invr_bf", [1, SL], BF16))
    consts = A(nc.sbuf_tensor("consts", [128, 2], F32))
    onescol = A(nc.sbuf_tensor("onescol", [128, 1], BF16))
    b1csb = A(nc.sbuf_tensor("b1csb", [128, 32], F32))
    b2csb = A(nc.sbuf_tensor("b2csb", [128, 8], F32))
    b1q8 = A(nc.sbuf_tensor("b1q8", [128, 32], FP8))
    b2q8 = A(nc.sbuf_tensor("b2q8", [128, 8], FP8))
    b1q = A(nc.sbuf_tensor("b1q", [128, 32], F32))
    b2q = A(nc.sbuf_tensor("b2q", [128, 8], F32))
    scX = ExitStack()
    x_fm = scX.enter_context(nc.sbuf_tensor("x_fm", [128, DC * SL], BF16))

    dma("sync", ident[:], idbf, "cBIN")
    dma("sync", b1csb[:], b1c, "cBIN")
    dma("sync", b2csb[:], b2c, "cBIN")
    nbin = sems["cBIN"].n

    def g_memsets(e):
        e.memset(onescol[:], 1.0)
        e.memset(consts[:, 0:1], EXP_SHIFT)
        return e.memset(consts[:, 1:2], 1e-5)
    I("gpsimd", g_memsets, "cCONST")

    W("vector", "cBIN", nbin)
    I("vector", lambda e: e.tensor_copy(b1q8[:], b1csb[:]))
    I("vector", lambda e: e.tensor_copy(b2q8[:], b2csb[:]))
    I("vector", lambda e: e.memset(onesrow[:], 1.0), "cCONST")
    W("scalar", "cCONST", 2)
    I("scalar", lambda e: e.activation(b1q[:], b1q8[:], ACT.Identity,
                                       bias=0.0, scale=1.0))
    I("scalar", lambda e: e.activation(b2q[:], b2q8[:], ACT.Identity,
                                       bias=0.0, scale=1.0), "cBQ")
    nconst = sems["cCONST"].n

    scB = ExitStack()
    wqkv_bf = scB.enter_context(nc.sbuf_tensor("wqkv_bf", [128, DC * 3 * D], BF16, side="right"))
    r1neg = scB.enter_context(nc.sbuf_tensor("r1neg", [1, 3 * D], BF16, side="right"))
    r1b = scB.enter_context(nc.sbuf_tensor("r1b", [1, 3 * D], BF16, side="right"))

    W("sync", "cCONST", nconst)
    ring_i = 0

    WSLOT = ("cWIN", "cWINB", "cWINC")

    def stream_w(src_ap, width, cast_fn, rows_n=128, eng="vector"):
        nonlocal ring_i
        i = ring_i
        ring_i += 1
        sl = (i % 3) * 1024
        sw = WSLOT[i % 3]
        if i >= 3:
            W("sync", "cWCAST", i - 2)
        dma("sync", wring[0:rows_n, sl:sl + width], src_ap, sw)
        W(eng, sw, sems[sw].n)
        I(eng, lambda e, sl=sl: cast_fn(e, sl), "cWCAST")

    def emit_wqkv_piece(j):
        kc, p3 = divmod(j, 3)
        stream_w(wqkv[j * 128:(j + 1) * 128, :],
                 1024,
                 lambda e, sl, kc=kc, p3=p3: e.activation(
                     wqkv_bf[:, kc * 3 * D + p3 * 1024:
                             kc * 3 * D + (p3 + 1) * 1024],
                     wring[:, sl:sl + 1024], ACT.Identity,
                     bias=0.0, scale=1.0), eng="scalar")

    ph1 = ExitStack()
    xstage = ph1.enter_context(nc.sbuf_tensor("xstage", [128, 2 * D], F32, side="right"))
    xtmr = ph1.enter_context(nc.sbuf_tensor("xtmr", [128, 2 * D], BF16, side="right"))
    psTrX = [ph1.enter_context(nc.psum_tensor(f"psTrX{j}", [128, 128], BF16))
             for j in range(2)]

    W("vector", "cCONST", nconst)
    for tt in range(TT):
        sl = (tt % 2) * D
        if tt >= 2:
            W("sync", "cXCAST", tt - 1)
        sx = "cXIN" if tt % 2 == 0 else "cXINB"
        dma("sync", xstage[:, sl:sl + D], xin[tt * 128:(tt + 1) * 128, :], sx)
        W("vector", sx, sems[sx].n)
        if tt >= 2:
            W("vector", "cXTR", (tt - 1) * 8)
        I("vector", lambda e, sl=sl: e.tensor_copy(xtmr[:, sl:sl + D],
                                                   xstage[:, sl:sl + D]),
          "cXCAST")
        for c in range(DC):
            m = tt * 8 + c
            W("tensor", "cXCAST", tt + 1)
            if m >= 2:
                W("tensor", "cXTRE", m - 1)
            I("tensor", lambda e, m=m, sl=sl, c=c:
              e.transpose(psTrX[m % 2][:],
                          xtmr[:, sl + c * 128:sl + (c + 1) * 128], ident[:]),
              "cXTR")
            W("vector", "cXTR", m + 1)
            I("vector", lambda e, m=m, c=c, tt=tt:
              e.tensor_copy(
                  x_fm[:, c * SL + tt * 128:c * SL + (tt + 1) * 128],
                  psTrX[m % 2][:]), "cXTRE")
    ph1.close()
    nxtre = sems["cXTRE"].n
    if debug:
        W("sync", "cXTRE", nxtre)
        dma("sync", dbg["d_xfm_e"], x_fm[:], "cDBG")

    ph1b = ExitStack()
    sqs = ph1b.enter_context(nc.sbuf_tensor("sqs1", [128, 2 * SL], BF16))
    psSt = [ph1b.enter_context(nc.psum_tensor(f"psSt{j}", [1, SL], F32))
            for j in range(2)]
    W("vector", "cXTRE", nxtre)
    W("tensor", "cXTRE", nxtre)
    for qc in range(4):
        for c in range(DC):
            I("tensor", lambda e, qc=qc, c=c:
              e.matmul(psSt[0][:, qc * 512:(qc + 1) * 512], onescol[:],
                       x_fm[:, c * SL + qc * 512:c * SL + (qc + 1) * 512],
                       start=(c == 0), stop=(c == DC - 1)))
    for c in range(DC):
        if c >= 2:
            W("vector", "cSQMM", c - 1)
        I("vector", lambda e, c=c, sl=(c % 2) * SL:
          e.tensor_mul(sqs[:, sl:sl + SL], x_fm[:, c * SL:(c + 1) * SL],
                       x_fm[:, c * SL:(c + 1) * SL]), "cSQV")
        W("tensor", "cSQV", c + 1)
        for qc in range(4):
            I("tensor", lambda e, qc=qc, c=c, sl=(c % 2) * SL:
              e.matmul(psSt[1][:, qc * 512:(qc + 1) * 512], onescol[:],
                       sqs[:, sl + qc * 512:sl + (qc + 1) * 512],
                       start=(c == 0), stop=(c == DC - 1)),
              "cSQMM" if qc == 3 else None)
    I("tensor", lambda e: e.nop(), "cSTATS")
    W("vector", "cSTATS", 1)
    I("vector", lambda e: e.tensor_scalar(m_bf[:], psSt[0][:], 1.0 / D,
                                          None, ALU.mult))
    I("vector", lambda e: e.tensor_mul(sqs[0:1, 0:SL], m_bf[:], m_bf[:]))
    I("vector", lambda e: e.scalar_tensor_tensor(psSt[0][:], psSt[1][:],
                                                 1.0 / D, sqs[0:1, 0:SL], ALU.mult,
                                                 ALU.subtract), "cNAR")
    W("scalar", "cCONST", nconst)
    W("scalar", "cNAR", 1)
    I("scalar", lambda e: e.activation(invr_bf[:], psSt[0][:], ACT.Sqrt,
                                       bias=consts[0:1, 1:2], scale=1.0), "cNAR")
    W("vector", "cNAR", 2)
    def _recst_cNAR(e):
        with nc.allow_low_precision("LN rstd in bf16; 0.4% relative scale "
                                    "noise is far inside the native gate"):
            return e.reciprocal(rstd_bf[:], invr_bf[:])
    I("vector", _recst_cNAR, "cNAR")
    ph1b.close()

    ph1c = ExitStack()
    psB = ph1c.enter_context(nc.psum_tensor("psB", [128, SL], F32))
    rbc1 = ExitStack()
    rbc1_t = rbc1.enter_context(nc.sbuf_tensor("rbc1", [128, SL], BF16, side="right"))
    W("tensor", "cNAR", 3)
    for qc in range(4):
        I("tensor", lambda e, qc=qc:
          e.matmul(psB[:, qc * 512:(qc + 1) * 512], onesrow[0:1, 0:128],
                   rstd_bf[0:1, qc * 512:(qc + 1) * 512], start=True, stop=True),
          "cBC")
    W("vector", "cBC", 4)
    I("vector", lambda e: e.tensor_copy(rbc1_t[:], psB[:]), "cBCE")
    ph1c.close()

    scC = ExitStack()
    qk_fm = scC.enter_context(nc.sbuf_tensor("qk_fm", [128, 16 * SL], BF16))
    scV = ExitStack()
    v_fm = scV.enter_context(nc.sbuf_tensor("v_fm", [128, 8 * SL], BF16))
    for j in range(24):
        emit_wqkv_piece(j)
    nwq = sems["cWCAST"].n

    for p3 in range(3):
        stream_w(brow[0:1, p3 * 1024:(p3 + 1) * 1024], 1024,
                 lambda e, sl, p3=p3: e.tensor_copy(
                     r1b[0:1, p3 * 1024:(p3 + 1) * 1024],
                     wring[0:1, sl:sl + 1024]), rows_n=1)
    stream_w(brow[1:2, 0:1024], 1024,
             lambda e, sl: e.tensor_copy(borow_bf[0:1, 0:1024],
                                         wring[0:1, sl:sl + 1024]), rows_n=1)

    ph2cs = ExitStack()
    psCS = ph2cs.enter_context(nc.psum_tensor("psCS", [1, 512], F32))
    W("tensor", "cWCAST", nwq)
    for piece in range(6):
        if piece >= 1:
            W("tensor", "cCSV", piece)
        for kc in range(DC):
            I("tensor", lambda e, piece=piece, kc=kc:
              e.matmul(psCS[:], onescol[:],
                       wqkv_bf[:, kc * 3 * D + piece * 512:
                               kc * 3 * D + (piece + 1) * 512],
                       start=(kc == 0), stop=(kc == DC - 1)),
              "cCST" if kc == DC - 1 else None)
        W("vector", "cCST", piece + 1)
        I("vector", lambda e, piece=piece:
          e.tensor_scalar(r1neg[0:1, piece * 512:(piece + 1) * 512], psCS[:],
                          -1.0, None, ALU.mult), "cCSV")
    I("vector", lambda e: e.tensor_copy(r1neg[0:1, 0:1], r1neg[0:1, 0:1]), "cCSV")
    ncsv = sems["cCSV"].n
    ph2cs.close()

    ph2mm = ExitStack()
    psQK = [ph2mm.enter_context(nc.psum_tensor(f"psQK{j}", [128, 512], F32))
            for j in range(4)]
    W("tensor", "cCSV", ncsv)
    W("tensor", "cBCE", 1)
    g = 0
    for ec in range(EC):
        for tq in range(4):
            bank = g % 4
            if g >= 4:
                W("tensor", "cQKEV", g - 3)
            for kc in range(DC):
                I("tensor", lambda e, bank=bank, ec=ec, tq=tq, kc=kc:
                  e.matmul(psQK[bank][:],
                           wqkv_bf[:, kc * 3 * D + ec * 128:
                                   kc * 3 * D + (ec + 1) * 128],
                           x_fm[:, kc * SL + tq * 512:kc * SL + (tq + 1) * 512],
                           start=(kc == 0), stop=False))
            I("tensor", lambda e, bank=bank, ec=ec, tq=tq:
              e.matmul(psQK[bank][:], r1neg[0:1, ec * 128:(ec + 1) * 128],
                       m_bf[0:1, tq * 512:(tq + 1) * 512],
                       start=False, stop=False))
            I("tensor", lambda e, bank=bank, ec=ec, tq=tq:
              e.matmul(psQK[bank][:], r1b[0:1, ec * 128:(ec + 1) * 128],
                       invr_bf[0:1, tq * 512:(tq + 1) * 512],
                       start=False, stop=True), "cQKMM")
            W("vector", "cQKMM", g + 1)
            if ec < 16:
                dstap = lambda ec=ec, tq=tq: qk_fm[:, ec * SL + tq * 512:
                                                   ec * SL + (tq + 1) * 512]
            else:
                dstap = lambda ec=ec, tq=tq: v_fm[:, (ec - 16) * SL + tq * 512:
                                                  (ec - 16) * SL + (tq + 1) * 512]
            I("vector", lambda e, bank=bank, d=dstap, tq=tq:
              e.tensor_mul(d(), psQK[bank][:],
                           rbc1_t[:, tq * 512:(tq + 1) * 512]), "cQKEV")
            g += 1
    nqkev = sems["cQKEV"].n
    if debug:
        W("sync", "cQKEV", nqkev)
        dma("sync", dbg["d_xfm_m"], x_fm[:], "cDBG")
    ph2mm.close()
    rbc1.close()
    scB.close()

    scD = ExitStack()
    v_aug = scD.enter_context(nc.sbuf_tensor("v_aug", [128, 16 * 1040], BF16, side="right"))
    pt_c = scD.enter_context(nc.sbuf_tensor("pt_c", [128, 3 * 1024], BF16,
                                            side="right"))
    ph2tr = ExitStack()
    psTrV = [ph2tr.enter_context(nc.psum_tensor(f"psTrV{j}", [128, 128], BF16))
             for j in range(2)]
    W("vector", "cQKEV", 96)
    for tt in range(TT):
        I("vector", lambda e, tt=tt: e.memset(
            v_aug[:, tt * 1040:(tt + 1) * 1040]
            .rearrange("p (h c) -> p h c", h=16, c=65)[:, :, 64:65], 1.0))
    for c in range(8):
        ecv = 16 + c
        W("tensor", "cQKEV", (ecv + 1) * 4)
        for tt in range(TT):
            m = c * TT + tt
            if m >= 2:
                W("tensor", "cVTRE", m - 1)
            I("tensor", lambda e, m=m, c=c, tt=tt:
              e.transpose(psTrV[m % 2][:],
                          v_fm[:, c * SL + tt * 128:c * SL + (tt + 1) * 128],
                          ident[:]), "cVTR")
            W("vector", "cVTR", m + 1)
            I("vector", lambda e, m=m, c=c, tt=tt:
              e.tensor_copy(
                  v_aug[:, tt * 1040 + (2 * c) * 65:
                        tt * 1040 + (2 * c) * 65 + 130]
                  .rearrange("p (h c) -> p h c", h=2, c=65)[:, :, 0:64],
                  psTrV[m % 2][:].rearrange("p (h c) -> p h c", h=2, c=64)),
              "cVTRE")
    nvtre = sems["cVTRE"].n
    if debug:
        W("sync", "cVTRE", nvtre)
        dma("sync", dbg["d_xfm_v"], x_fm[:], "cDBG")
    ph2tr.close()
    scV.close()

    ph3 = ExitStack()
    psS = [ph3.enter_context(nc.psum_tensor(f"psS{j}", [128, 1024], F32))
           for j in range(2)]
    psO = [ph3.enter_context(nc.psum_tensor(f"psO{j}", [128, 1024], F32))
           for j in range(2)]
    scE = ExitStack()
    attn_fm = scE.enter_context(nc.sbuf_tensor("attn_fm", [128, 8 * SL], BF16))
    scE2 = ExitStack()
    sums_bf = scE2.enter_context(nc.sbuf_tensor("sums_bf", [16, SL], BF16))
    invb = scE2.enter_context(nc.sbuf_tensor("invb", [128, SL], BF16))
    invrow = [scE2.enter_context(nc.sbuf_tensor(f"invrow{j}", [1, SL], BF16))
              for j in range(2)]
    scE3 = ExitStack()
    srow2 = [scE3.enter_context(nc.sbuf_tensor(f"srow{s}", [1, SL], BF16))
             for s in range(2)]

    W("tensor", "cVTRE", nvtre)
    W("scalar", "cVTRE", nvtre)
    CH = 8 * 2 * KT * 2

    def _cqk(s):
        c, r = divmod(s, 2 * KT)
        qh, kt = divmod(r, KT)
        return c, qh, kt

    def emit_qk(k):
        s, qq = k // 2, k % 2
        c, qh, kt = _cqk(s)
        if k >= 2:
            W("tensor", "cEXP", k - 1)
        qcol = c * SL + qh * 1024 + qq * 512
        kcol = (8 + c) * SL + kt * 128
        I("tensor", lambda e, k=k, qcol=qcol, kcol=kcol:
          e.matmul(psS[k % 2][:, 0:512],
                   qk_fm[0:64, kcol:kcol + 128],
                   qk_fm[0:64, qcol:qcol + 512],
                   start=True, stop=True, tile_position=(0, 0)))
        I("tensor", lambda e, k=k, qcol=qcol, kcol=kcol:
          e.matmul(psS[k % 2][:, 512:1024],
                   qk_fm[64:128, kcol:kcol + 128],
                   qk_fm[64:128, qcol:qcol + 512],
                   start=True, stop=True, tile_position=(64, 0)), "cQKA")

    def emit_exp(k):
        W("scalar", "cQKA", k + 1)
        if k >= 3:
            W("scalar", "cPV", 2 * (k - 2))
        I("scalar", lambda e, k=k:
          e.activation(pt_c[:, (k % 3) * 1024:(k % 3) * 1024 + 1024],
                       psS[k % 2][:], ACT.Exp, bias=consts[:, 0:1],
                       scale=0.125), "cEXP")

    def emit_pv(k):
        s, qq = k // 2, k % 2
        c, qh, kt = _cqk(s)
        m = 2 * c + qh
        W("tensor", "cEXP", k + 1)
        for sub in range(2):
            if kt == 0 and qq == 0 and m >= 1:
                W("tensor", "cOEV", 4 * (m - 1) + 2 * (sub + 1))
            I("tensor", lambda e, sub=sub, qq=qq, kt=kt, c=c, k=k:
              e.matmul(psO[sub][0:65, qq * 512:(qq + 1) * 512],
                       v_aug[:, kt * 1040 + (2 * c + sub) * 65:
                             kt * 1040 + (2 * c + sub) * 65 + 65],
                       pt_c[:, (k % 3) * 1024 + sub * 512:
                             (k % 3) * 1024 + (sub + 1) * 512],
                       start=(kt == 0), stop=(kt == KT - 1)), "cPV")

    def emit_evac(k):
        s = k // 2
        c, qh, kt = _cqk(s)
        m = 2 * c + qh
        W("vector", "cPV", 2 * (k + 1))
        if qh == 0 and c >= 1:
            W("vector", "cSUM", 32 * c)
        for sub in range(2):
            I("vector", lambda e, sub=sub, c=c, qh=qh:
              e.tensor_copy(attn_fm[sub * 64:sub * 64 + 64,
                                    c * SL + qh * 1024:c * SL + qh * 1024 + 1024],
                            psO[sub][0:64, :]), "cOEV")
            I("vector", lambda e, sub=sub, qh=qh:
              e.tensor_copy(srow2[sub][0:1, qh * 1024:qh * 1024 + 1024],
                            psO[sub][64:65, :]), "cOEV")
        if qh == 1:
            W("sync", "cOEV", 4 * (m + 1))
            for sub in range(2):
                dma("sync", hsum[2 * c + sub:2 * c + sub + 1, :],
                    srow2[sub][:], "cSUM")

    emit_qk(0)
    for k in range(CH):
        emit_exp(k)
        if k + 1 < CH:
            emit_qk(k + 1)
        if k >= 2:
            emit_pv(k - 2)
            if (k - 2) % 2 == 1 and _cqk((k - 2) // 2)[2] == KT - 1:
                emit_evac(k - 2)
    for k in (CH - 2, CH - 1):
        emit_pv(k)
        if k % 2 == 1 and _cqk(k // 2)[2] == KT - 1:
            emit_evac(k)
    scE3.close()
    scD.close()
    scG = ExitStack()
    x2_fm = scG.enter_context(nc.sbuf_tensor("x2_fm", [128, DC * SL], BF16, side="right"))
    scF = ExitStack()
    wout_bf = scF.enter_context(nc.sbuf_tensor("wout_bf", [128, DC * D], BF16, side="right"))
    for kc in range(DC):
        stream_w(wout[kc * 128:(kc + 1) * 128, :], 1024,
                 lambda e, sl, kc=kc: e.activation(
                     wout_bf[:, kc * D:(kc + 1) * D], wring[:, sl:sl + 1024],
                     ACT.Identity, bias=0.0, scale=1.0), eng="scalar")
    nwo = sems["cWCAST"].n

    W("sync", "cSUM", sems["cSUM"].n)
    W("sync", "cSUMB", sems["cSUMB"].n)
    dma("sync", sums_bf[:], hsum, "cSUM")
    W("vector", "cSUM", sems["cSUM"].n)
    def _recip_lp(e):
        with nc.allow_low_precision("softmax denominator reciprocal in bf16; "
                                    "0.4% relative is far inside native tol"):
            return e.reciprocal(sums_bf[:], sums_bf[:])
    I("vector", _recip_lp, "cINV")
    W("sync", "cINV", 1)
    dma("sync", hinv, sums_bf[:], "cHIV")
    for hp in range(HEADS // 2):
        for sub in range(2):
            h = 2 * hp + sub
            ib = h
            if ib >= 2:
                W("sync", "cBCH", ib - 1)
            W("sync", "cHIV", 16)
            si = "cIRW" if ib % 2 == 0 else "cIRWB"
            dma("sync", invrow[ib % 2][:], hinv[h:h + 1, :], si)
            W("tensor", si, sems[si].n)
            W("tensor", "cPV", sems["cPV"].n)
            if hp >= 1:
                W("tensor", "cIBC", hp)
            for qq in range(4):
                I("tensor", lambda e, sub=sub, qq=qq, ib=ib:
                  e.matmul(psS[qq // 2][sub * 64:sub * 64 + 64,
                                        (qq % 2) * 512:(qq % 2) * 512 + 512],
                           onesrow[0:1, 0:64],
                           invrow[ib % 2][0:1, qq * 512:(qq + 1) * 512],
                           start=True, stop=True), "cBCH" if qq == 3 else None)
        W("vector", "cBCH", 2 * hp + 2)
        I("vector", lambda e: e.tensor_copy(invb[:, 0:1024], psS[0][:]))
        I("vector", lambda e: e.tensor_copy(invb[:, 1024:2048], psS[1][:]), "cIBC")
        I("vector", lambda e, hp=hp:
          e.tensor_mul(attn_fm[:, hp * SL:(hp + 1) * SL],
                       attn_fm[:, hp * SL:(hp + 1) * SL],
                       invb[:]), "cNRM")
    nnrm = sems["cNRM"].n
    if debug:
        W("sync", "cNRM", nnrm)
        dma("sync", dbg["d_xfm_n"], x_fm[:], "cDBG")
    ph3.close()
    scE2.close()

    ph4 = ExitStack()
    psP = [ph4.enter_context(nc.psum_tensor(f"psP{j}", [128, 512], F32))
           for j in range(4)]
    W("tensor", "cWCAST", nwo)
    W("tensor", "cNRM", nnrm)
    g = 0
    for oc in range(DC):
        for tq in range(4):
            bank = g % 4
            if g >= 4:
                W("tensor", "cX2", g - 3)
            for kc in range(DC):
                I("tensor", lambda e, bank=bank, oc=oc, tq=tq, kc=kc:
                  e.matmul(psP[bank][:],
                           wout_bf[:, kc * D + oc * 128:kc * D + (oc + 1) * 128],
                           attn_fm[:, kc * SL + tq * 512:kc * SL + (tq + 1) * 512],
                           start=(kc == 0), stop=False))
            I("tensor", lambda e, bank=bank, oc=oc, tq=tq:
              e.matmul(psP[bank][:], borow_bf[0:1, oc * 128:(oc + 1) * 128],
                       onesrow[0:1, 0:512],
                       start=False, stop=True), "cPMM")
            W("vector", "cPMM", g + 1)
            I("vector", lambda e, bank=bank, oc=oc, tq=tq:
              e.tensor_tensor(
                  x2_fm[:, oc * SL + tq * 512:oc * SL + (tq + 1) * 512],
                  psP[bank][:],
                  x_fm[:, oc * SL + tq * 512:oc * SL + (tq + 1) * 512],
                  ALU.add), "cX2")
            g += 1
    nx2 = sems["cX2"].n
    ph4.close()
    scE.close()
    scC.close()
    scX.close()
    scF.close()

    if debug:
        W("sync", "cX2", nx2)
    scI = ExitStack()
    hq8 = scI.enter_context(nc.sbuf_tensor("hq8", [128, NC1 * 512], FP8))
    otm = scI.enter_context(nc.sbuf_tensor("otm", [128, 4 * D], F32))
    w1q8 = scI.enter_context(nc.sbuf_tensor("w1q8", [128, DC * 4096], FP8))
    identf = scI.enter_context(nc.sbuf_tensor("identf", [128, 128], F32))
    w2q8 = scI.enter_context(nc.sbuf_tensor("w2q8", [128, NC1 * 1024], FP8))
    hbf = scI.enter_context(nc.sbuf_tensor("hbf", [128, 2 * 512], BF16))
    y2bf = scI.enter_context(nc.sbuf_tensor("y2bf", [128, 3 * 512], BF16))
    t1s = scI.enter_context(nc.sbuf_tensor("t1s", [128, 512], F32))
    ofm = scI.enter_context(nc.sbuf_tensor("ofm", [128, 3 * 512], F32))
    mring = scI.enter_context(nc.sbuf_tensor("mring", [128, 3 * 512], BF16))



    W("sync", "cQKA", sems["cQKA"].n)
    dma("sync", identf[:], idf32, "cIDF")

    W("scalar", "cQKA", sems["cQKA"].n)
    w1base = sems["cWCAST"].n
    for q4 in range(4):
        for kc in range(DC):
            stream_w(w1t[(q4 * 8 + kc) * 128:(q4 * 8 + kc + 1) * 128, :],
                     1024,
                     lambda e, sl, kc=kc, q4=q4: e.activation(
                         w1q8[:, kc * 4096 + q4 * 1024:
                              kc * 4096 + (q4 + 1) * 1024],
                         wring[:, sl:sl + 1024], ACT.Identity,
                         bias=0.0, scale=1.0), eng="scalar")
    nw1 = sems["cWCAST"].n
    w2base = ring_i
    w2win = []
    for k in range(NC1):
        i = ring_i
        ring_i += 1
        sl = (i % 3) * 1024
        sw = WSLOT[i % 3]
        j = i - 3
        if j >= 0:
            if j < w2base:
                W("sync", "cWCAST", j + 1)
            else:
                W("sync", "cW2V", j + 1 - w2base)
        dma("sync", wring[:, sl:sl + 1024],
            w2t[k * 128:(k + 1) * 128, 0:1024], sw)
        w2win.append((sw, sems[sw].n))

    scH = ExitStack()
    sqs = scH.enter_context(nc.sbuf_tensor("sqs2", [128, 2 * SL], BF16))
    m2bc = scH.enter_context(nc.sbuf_tensor("m2bc", [128, SL], BF16))
    r2bc = scH.enter_context(nc.sbuf_tensor("r2bc", [128, SL], BF16))
    xh2q8 = scH.enter_context(nc.sbuf_tensor("xh2q8", [128, DC * SL], FP8))
    ph4b = ExitStack()
    psSt2 = [ph4b.enter_context(nc.psum_tensor(f"psSt2{j}", [1, SL], F32))
             for j in range(2)]
    W("tensor", "cX2", nx2)
    W("vector", "cX2", nx2)
    for qc in range(4):
        for c in range(DC):
            I("tensor", lambda e, qc=qc, c=c:
              e.matmul(psSt2[0][:, qc * 512:(qc + 1) * 512], onescol[:],
                       x2_fm[:, c * SL + qc * 512:c * SL + (qc + 1) * 512],
                       start=(c == 0), stop=(c == DC - 1)))
    for c in range(DC):
        if c >= 2:
            W("vector", "cSQMM2", c - 1)
        I("vector", lambda e, c=c, sl=(c % 2) * SL:
          e.tensor_mul(sqs[:, sl:sl + SL], x2_fm[:, c * SL:(c + 1) * SL],
                       x2_fm[:, c * SL:(c + 1) * SL]), "cSQV2")
        W("tensor", "cSQV2", c + 1)
        for qc in range(4):
            I("tensor", lambda e, qc=qc, c=c, sl=(c % 2) * SL:
              e.matmul(psSt2[1][:, qc * 512:(qc + 1) * 512], onescol[:],
                       sqs[:, sl + qc * 512:sl + (qc + 1) * 512],
                       start=(c == 0), stop=(c == DC - 1)),
              "cSQMM2" if qc == 3 else None)
    I("tensor", lambda e: e.nop(), "cSTATS2")
    W("vector", "cSTATS2", 1)
    I("vector", lambda e: e.tensor_scalar(m_bf[:], psSt2[0][:], 1.0 / D,
                                          None, ALU.mult))
    I("vector", lambda e: e.tensor_mul(sqs[0:1, 0:SL], m_bf[:], m_bf[:]))
    I("vector", lambda e: e.scalar_tensor_tensor(psSt2[0][:], psSt2[1][:],
                                                 1.0 / D, sqs[0:1, 0:SL], ALU.mult,
                                                 ALU.subtract), "cNAR2")
    W("scalar", "cCONST", nconst)
    W("scalar", "cNAR2", 1)
    I("scalar", lambda e: e.activation(invr_bf[:], psSt2[0][:], ACT.Sqrt,
                                       bias=consts[0:1, 1:2], scale=1.0), "cNAR2")
    W("vector", "cNAR2", 2)
    def _recst_cNAR2(e):
        with nc.allow_low_precision("LN rstd in bf16; 0.4% relative scale "
                                    "noise is far inside the native gate"):
            return e.reciprocal(rstd_bf[:], invr_bf[:])
    I("vector", _recst_cNAR2, "cNAR2")
    ph4b.close()

    ph4c = ExitStack()
    psB2 = ph4c.enter_context(nc.psum_tensor("psB2", [128, SL], F32))
    W("tensor", "cNAR2", 3)
    for qc in range(4):
        I("tensor", lambda e, qc=qc:
          e.matmul(psB2[:, qc * 512:(qc + 1) * 512], onesrow[0:1, 0:128],
                   m_bf[0:1, qc * 512:(qc + 1) * 512], start=True, stop=True),
          "cBC2" if qc == 3 else None)
    W("vector", "cBC2", 1)
    I("vector", lambda e: e.tensor_copy(m2bc[:], psB2[:]), "cBC2E")
    W("tensor", "cBC2E", 1)
    for qc in range(4):
        I("tensor", lambda e, qc=qc:
          e.matmul(psB2[:, qc * 512:(qc + 1) * 512], onesrow[0:1, 0:128],
                   rstd_bf[0:1, qc * 512:(qc + 1) * 512], start=True, stop=True),
          "cBC2" if qc == 3 else None)
    W("vector", "cBC2", 2)
    I("vector", lambda e: e.tensor_copy(r2bc[:], psB2[:]), "cBC2E")
    ph4c.close()

    for c in range(DC):
        sl = (c % 2) * SL
        I("vector", lambda e, c=c, sl=sl:
          e.tensor_sub(sqs[:, sl:sl + SL], x2_fm[:, c * SL:(c + 1) * SL],
                       m2bc[:]))
        I("vector", lambda e, c=c, sl=sl:
          e.tensor_mul(xh2q8[:, c * SL:(c + 1) * SL], sqs[:, sl:sl + SL],
                       r2bc[:]), "cXH2")
    nxh2 = sems["cXH2"].n

    ph5 = ExitStack()
    psF = [ph5.enter_context(nc.psum_tensor(f"psF{j}", [128, 512], F32))
           for j in range(3)]
    psG = [ph5.enter_context(nc.psum_tensor(f"psG{j}", [128, 512], F32))
           for j in range(3)]
    psTrO = [ph5.enter_context(nc.psum_tensor(f"psTrO{j}", [128, 128], F32))
             for j in range(2)]

    W("tensor", "cXH2", nxh2)
    gf = 0
    gg = 0
    tro = 0
    pend = []

    def emit_oasm(p):
        nonlocal tro
        ts_p, oc_p = divmod(p, DC)
        W("tensor", "cIDF", 16)
        W("tensor", "cOASM", p + 1)
        for j in range(4):
            tt = ts_p * 4 + j
            if tro >= 2:
                W("tensor", "cOCP", tro - 1)
            I("tensor", lambda e, tro=tro, j=j, p=p:
              e.transpose(psTrO[tro % 2][:],
                          ofm[:, (p % 3) * 512 + j * 128:
                              (p % 3) * 512 + (j + 1) * 128],
                          identf[:]), "cOTR")
            W("vector", "cOTR", tro + 1)
            if oc_p == 0 and tt >= 4:
                so = ("cOUT", "cOUT1", "cOUT2", "cOUT3")[tt % 4]
                W("vector", so, 16 * ts_p)
            I("vector", lambda e, tro=tro, tt=tt, oc_p=oc_p:
              e.tensor_copy(otm[:, (tt % 4) * D + oc_p * 128:
                            (tt % 4) * D + (oc_p + 1) * 128],
                            psTrO[tro % 2][:]), "cOCP")
            tro += 1
        if oc_p == DC - 1:
            for j in range(4):
                tt = ts_p * 4 + j
                W("sync", "cOCP", (ts_p * DC + 7) * 4 + j + 1)
                dma("sync", yout[tt * 128:(tt + 1) * 128, :],
                    otm[:, (tt % 4) * D:(tt % 4) * D + D],
                    ("cOUT", "cOUT1", "cOUT2", "cOUT3")[tt % 4])

    for ts in range(TSN):
        for ncc in range(NC1):
            bank = gf % 3
            if ts == 0:
                W("tensor", "cWCAST", w1base + 8 * (ncc // 8 + 1))
            if gf >= 3:
                W("tensor", "cFEV", gf - 2)
            for kc in range(DC):
                I("tensor", lambda e, bank=bank, ncc=ncc, ts=ts, kc=kc:
                  e.matmul(psF[bank][:],
                           w1q8[:, kc * 4096 + ncc * 128:kc * 4096 + (ncc + 1) * 128],
                           xh2q8[:, kc * SL + ts * 512:kc * SL + (ts + 1) * 512],
                           start=(kc == 0), stop=(kc == DC - 1)), "cFMM" if kc == DC - 1 else None)
            W("scalar", "cBQ", 1)
            W("scalar", "cFMM", gf + 1)
            if gf >= 2:
                W("scalar", "cHQ", gf - 1)
            hs = gf % 2
            I("scalar", lambda e, bank=bank, hs=hs, ncc=ncc:
              e.activation(hbf[:, hs * 512:(hs + 1) * 512], psF[bank][:],
                           ACT.Relu, bias=b1q[:, ncc:ncc + 1], scale=1.0),
              "cFEV")
            W("vector", "cFEV", gf + 1)
            if ts >= 1:
                W("vector", "cGMM", ts * 8)
            I("vector", lambda e, hs=hs, ncc=ncc:
              e.tensor_copy(hq8[:, ncc * 512:(ncc + 1) * 512],
                            hbf[:, hs * 512:(hs + 1) * 512]), "cHQ")
            if ts == 0:
                sl2 = ((w2base + ncc) % 3) * 1024
                W("vector", w2win[ncc][0], w2win[ncc][1])
                I("vector", lambda e, ncc=ncc, sl2=sl2:
                  e.tensor_copy(w2q8[:, ncc * 1024:(ncc + 1) * 1024],
                                wring[:, sl2:sl2 + 1024]), "cW2V")
            gf += 1
        for oc in range(DC):
            gb = gg % 3
            if gg >= 3:
                W("sync", "cMUSE", gg - 2)
            sm = ("cMIN", "cMIN1", "cMIN2")[gg % 3]
            dma("sync", mring[:, (gg % 3) * 512:(gg % 3 + 1) * 512],
                maskfm[oc * 128:(oc + 1) * 128, ts * 512:(ts + 1) * 512], sm)
            nmin = (sm, sems[sm].n)
            W("tensor", "cHQ", (ts + 1) * NC1)
            if gg == 0:
                W("tensor", "cW2V", NC1)
            if gg >= 3:
                W("tensor", "cGEV", gg - 2)
            for ncc in range(NC1):
                I("tensor", lambda e, gb=gb, ncc=ncc, oc=oc:
                  e.matmul(psG[gb][:],
                           w2q8[:, ncc * 1024 + oc * 128:
                                ncc * 1024 + (oc + 1) * 128],
                           hq8[:, ncc * 512:(ncc + 1) * 512],
                           start=(ncc == 0), stop=(ncc == NC1 - 1)),
                  "cGMM" if ncc == NC1 - 1 else None)
            W("scalar", "cGMM", gg + 1)
            if gg >= 3:
                W("scalar", "cMUSE", gg - 2)
            I("scalar", lambda e, gb=gb, gg=gg, oc=oc:
              e.activation(y2bf[:, (gg % 3) * 512:(gg % 3 + 1) * 512],
                           psG[gb][:], ACT.Identity,
                           bias=b2q[:, oc:oc + 1], scale=1.0), "cGEV")
            W("vector", "cGEV", gg + 1)
            W("vector", nmin[0], nmin[1])
            I("vector", lambda e, gg=gg:
              e.tensor_mul(t1s[:], y2bf[:, (gg % 3) * 512:(gg % 3 + 1) * 512],
                           mring[:, (gg % 3) * 512:(gg % 3 + 1) * 512]), "cMUSE")
            if gg >= 3:
                W("vector", "cOTR", 4 * (gg - 2))
            I("vector", lambda e, gg=gg, oc=oc, ts=ts:
              e.tensor_tensor(ofm[:, (gg % 3) * 512:(gg % 3 + 1) * 512],
                              t1s[:],
                              x2_fm[:, oc * SL + ts * 512:oc * SL + (ts + 1) * 512],
                              ALU.add), "cOASM")
            pend.append(gg)
            if len(pend) > 2:
                emit_oasm(pend.pop(0))
            gg += 1
    while pend:
        emit_oasm(pend.pop(0))
    ph5.close()
    scH.close()
    scI.close()
    scG.close()

    if debug:
        W("sync", "cOUT", sems["cOUT"].n)
        W("sync", "cDBG", sems["cDBG"].n)
        dma("sync", dbg["d_xfm"], x_fm[:], "cOUT")
        dma("sync", dbg["d_x2fm"], x2_fm[:], "cOUT")
        dma("sync", dbg["d_hq"], hq8[:].bitcast(mybir.dt.uint8), "cOUT")
        dma("sync", dbg["d_xh2"], xh2q8[:].bitcast(mybir.dt.uint8), "cOUT")
        dma("sync", dbg["d_w1q"], w1q8[:].bitcast(mybir.dt.uint8), "cOUT")
        dma("sync", dbg["d_mbf"], m_bf[:], "cOUT")
        dma("sync", dbg["d_b1q"], b1q[:], "cOUT")
        dma("sync", dbg["d_b1c"], b1csb[:], "cOUT")
        dma("sync", dbg["d_rstd"], rstd_bf[:], "cOUT")
    for so in ("cOUT", "cOUT1", "cOUT2", "cOUT3"):
        W("gpsimd", so, sems[so].n)

    with nc.Block() as blk:
        @blk.sync
        def _(e):
            for fn in prog["sync"]:
                fn(e)

        @blk.scalar
        def _(e):
            for fn in prog["scalar"]:
                fn(e)

        @blk.vector
        def _(e):
            for fn in prog["vector"]:
                fn(e)

        @blk.tensor
        def _(e):
            for fn in prog["tensor"]:
                fn(e)

        @blk.gpsimd
        def _(e):
            for fn in prog["gpsimd"]:
                fn(e)

    perm.close()
    return nc


_NC_CACHE = None


def kernel(L0_attn_x, L0_attn_W_qkv, L0_attn_b_qkv, L0_attn_W_out,
           L0_attn_b_out, L0_ffn_W1, L0_ffn_b1, L0_ffn_W2, L0_ffn_b2):
    global _NC_CACHE
    x = np.asarray(L0_attn_x, dtype=np.float32)
    B = x.shape[0]
    keep = _dropout_keep_mask((B, SL, D))
    maskf = np.where(keep, np.float32(1.25), np.float32(0.0))

    wqkv = np.ascontiguousarray(
        np.asarray(L0_attn_W_qkv, dtype=np.float32)
        .reshape(8, 128, 3, 1024).transpose(0, 2, 1, 3).reshape(24 * 128, 1024))
    wout = np.ascontiguousarray(np.asarray(L0_attn_W_out, dtype=np.float32))
    w1t = np.ascontiguousarray(
        np.asarray(L0_ffn_W1, dtype=np.float32).T
        .reshape(8, 128, 4, 1024).transpose(2, 0, 1, 3).reshape(32 * 128, 1024))
    w2t = np.ascontiguousarray(np.asarray(L0_ffn_W2, dtype=np.float32).T)
    bq = np.asarray(L0_attn_b_qkv, dtype=np.float32)
    bo = np.asarray(L0_attn_b_out, dtype=np.float32)
    b1 = np.asarray(L0_ffn_b1, dtype=np.float32)
    b2 = np.asarray(L0_ffn_b2, dtype=np.float32)

    brow = np.zeros((2, 3 * D), dtype=np.float32)
    brow[0, :] = bq
    brow[1, :D] = bo
    b1c = np.ascontiguousarray(b1.reshape(32, 128).T)
    b2c = np.ascontiguousarray(b2.reshape(8, 128).T)
    idbf = np.eye(128, dtype=ml_dtypes.bfloat16)
    idf32 = np.eye(128, dtype=np.float32)

    if _NC_CACHE is None:
        _NC_CACHE = build_nc()
    nc = _NC_CACHE

    in_maps = []
    for b in range(B):
        in_maps.append({
            "xin": np.ascontiguousarray(x[b]),
            "wqkv": wqkv, "wout": wout, "w1t": w1t, "w2t": w2t,
            "brow": brow, "b1c": b1c, "b2c": b2c,
            "idbf": idbf, "idf32": idf32,
            "maskfm": np.ascontiguousarray(
                maskf[b].T.astype(ml_dtypes.bfloat16)),
        })
    res = run_bass_kernel_spmd(nc, in_maps, core_ids=list(range(B)))
    out = np.stack([np.asarray(r["yout"]) for r in res.results], axis=0)
    return out.astype(np.float32)
